# version 06
# baseline (speedup 1.0000x reference)
import numpy as np
import ml_dtypes
from contextlib import ExitStack
import concourse.bass as bass
import concourse.mybir as mybir
from concourse.bass import IndirectOffsetOnAxis
from concourse.bass_utils import run_bass_kernel_spmd

F32 = mybir.dt.float32
F32R = mybir.dt.float32r
BF16 = mybir.dt.bfloat16
I32 = mybir.dt.int32
AF = mybir.ActivationFunctionType
ALU = mybir.AluOpType
AX = mybir.AxisListType

NC = 8
B, S, D = 2, 4096, 4096
T = B * S
MY = T // NC
HD = 128
HPC = 4
INTER = 11008
SCALE = HD ** -0.5
NDMT = D // 128
N_IT = INTER // 128
bf16 = ml_dtypes.bfloat16
NO_Z = False
NO_GST = False
P5_LIMIT = None
NO_AUX = False
NO_TR = False


class Sem:
    def __init__(self, nc, name):
        self.h = nc.alloc_semaphore(name)
        self.n = 0

    def inc(self, k=1):
        self.n += k
        return self.n


def build_program(dbg=False):
    nc = bass.Bass("TRN2", target_bir_lowering=False)

    io = {}
    def inp(name, shape, dt):
        io[name] = nc.dram_tensor(name, shape, dt, kind="ExternalInput").ap()
    inp("x_my", [MY, D], F32)
    inp("wq_my", [D, HPC * HD], F32)
    inp("wk_my", [D, HD], F32)
    inp("wv_my", [D, HD], F32)
    inp("wo", [D, D], F32)
    inp("g_qw", [D // 8, INTER], I32)
    inp("g_sc", [D // 128, INTER], F32)
    inp("g_qz", [D // 128, INTER // 8], I32)
    inp("u_qw", [D // 8, INTER], I32)
    inp("u_sc", [D // 128, INTER], F32)
    inp("u_qz", [D // 128, INTER // 8], I32)
    inp("d_qw", [INTER // 8, D], I32)
    inp("d_sc", [INTER // 128, D], F32)
    inp("d_qz", [INTER // 128, D // 8], I32)
    inp("gidx", [128, 32], I32)
    out_my = nc.dram_tensor("out_my", [MY, D], F32, kind="ExternalOutput").ap()
    io["out_my"] = out_my

    xt_contrib = nc.dram_tensor("xt_contrib", [D, MY], BF16)
    xt_full = nc.dram_tensor("xt_full", [NC * D, MY], BF16)
    at_contrib = [nc.dram_tensor(f"at_contrib{i}", [2 * HD, T], BF16) for i in range(2)]
    at_full = [nc.dram_tensor(f"at_full{i}", [NC * 2 * HD, T], BF16) for i in range(2)]
    o_dram = nc.dram_tensor("o_dram", [MY, D], BF16)
    zs_gu_dram = nc.dram_tensor("zs_gu_dram", [32, 2 * INTER], F32R)
    zs_d_dram = nc.dram_tensor("zs_d_dram", [INTER // 128, D], F32R)
    y_dram = nc.dram_tensor("y_dram", [MY, D], BF16)
    hT_dram = nc.dram_tensor("hT_dram", [INTER, MY], BF16)
    io.update(xt_contrib=xt_contrib, xt_full=xt_full, at_contrib=at_contrib,
              at_full=at_full, o_dram=o_dram, y_dram=y_dram, hT_dram=hT_dram,
              zs_gu_dram=zs_gu_dram, zs_d_dram=zs_d_dram)

    C = {}
    C["ident_bf"] = nc.inline_tensor(np.eye(128, dtype=bf16), "ident_bf").ap()
    C["ident_f32"] = nc.inline_tensor(np.eye(128, dtype=np.float32), "ident_f32").ap()
    C["ones_col"] = nc.inline_tensor(np.ones((128, 1), dtype=bf16), "ones_col").ap()
    C["ones_row_f32"] = nc.inline_tensor(np.ones((1, 128), dtype=np.float32), "ones_row_f32").ap()
    m = np.zeros((128, 4 * 512), dtype=bf16)
    for o in range(4):
        kk = np.arange(128)[:, None] + o * 128
        qq = np.arange(512)[None, :]
        m[:, o * 512:(o + 1) * 512] = (kk <= qq).astype(bf16)
    C["masks"] = nc.inline_tensor(m, "masks").ap()
    e = np.zeros((128, 4 * 32), dtype=bf16)
    for c4 in range(4):
        for p in range(128):
            e[p, c4 * 32 + 8 * c4 + p // 16] = 1
    C["e_cmats"] = nc.inline_tensor(e, "e_cmats").ap()

    dbg_out = {}
    if dbg == "qkv":
        for name, shape, dt in [
            ("dbg_qT", [128, HPC * T], BF16),
            ("dbg_kT", [128, T], BF16),
            ("dbg_v", [128, T], BF16),
        ]:
            dbg_out[name] = nc.dram_tensor(name, shape, dt, kind="ExternalOutput").ap()
    elif dbg == "p2":
        dbg_out["dbg_at0"] = nc.dram_tensor("dbg_at0", [NC * 2 * HD, T], BF16,
                                            kind="ExternalOutput").ap()
        dbg_out["dbg_at1"] = nc.dram_tensor("dbg_at1", [NC * 2 * HD, T], BF16,
                                            kind="ExternalOutput").ap()
    elif dbg == "p5a":
        dbg_out["dbg_zs"] = nc.dram_tensor("dbg_zs", [32, 2 * INTER], F32R, kind="ExternalOutput").ap()
        dbg_out["dbg_zsd"] = nc.dram_tensor("dbg_zsd", [INTER // 128, D], F32R, kind="ExternalOutput").ap()
    elif dbg == "p5":
        dbg_out["dbg_hT"] = nc.dram_tensor("dbg_hT", [INTER, MY], BF16, kind="ExternalOutput").ap()
        dbg_out["dbg_gs"] = nc.dram_tensor("dbg_gs", [86, MY], F32R, kind="ExternalOutput").ap()
        dbg_out["dbg_y2"] = nc.dram_tensor("dbg_y2", [MY, D], BF16, kind="ExternalOutput").ap()
        dbg_out["dbg_zs"] = nc.dram_tensor("dbg_zs", [32, 2 * INTER], F32R, kind="ExternalOutput").ap()
        dbg_out["dbg_zsd"] = nc.dram_tensor("dbg_zsd", [INTER // 128, D], F32R, kind="ExternalOutput").ap()
    elif dbg == "p4":
        dbg_out["dbg_at0"] = nc.dram_tensor("dbg_at0", [NC * 2 * HD, T], BF16,
                                            kind="ExternalOutput").ap()
        dbg_out["dbg_at1"] = nc.dram_tensor("dbg_at1", [NC * 2 * HD, T], BF16,
                                            kind="ExternalOutput").ap()
        dbg_out["dbg_o"] = nc.dram_tensor("dbg_o", [MY, D], BF16, kind="ExternalOutput").ap()
        dbg_out["dbg_y"] = nc.dram_tensor("dbg_y", [MY, D], BF16, kind="ExternalOutput").ap()
        dbg_out["dbg_yg"] = nc.dram_tensor("dbg_yg", [32, MY], F32R, kind="ExternalOutput").ap()

    s = {k: Sem(nc, "s_" + k) for k in
         ["tp", "tpe", "ag", "w", "mmq", "mmkv", "evq", "evkv", "const"]}
    s["x"] = [Sem(nc, f"s_x{i}") for i in range(2)]
    s["ct"] = [Sem(nc, f"s_ct{i}") for i in range(4)]
    s["xt"] = [Sem(nc, f"s_xt{i}") for i in range(3)]

    with (
        nc.sbuf_tensor("cst", [128, 3456], BF16) as cst,
        nc.sbuf_tensor("cst32", [128, 256], F32) as cst32,
    ):
        SC = {
            "ident_bf": cst[:, 0:128],
            "ones_col": cst[:, 128:129],
            "masks": cst[:, 256:256 + 2048],
            "e_cmats": cst[:, 2304:2304 + 128],
            "ident_f32": cst32[:, 0:128],
            "ones_row_f32": cst32[0:1, 128:256],
        }
        with nc.Block("pconst") as block:
            @block.sync
            def _(sy):
                sy.dma_start(out=SC["ident_bf"], in_=C["ident_bf"]).then_inc(s["const"].h, 16)
                sy.dma_start(out=SC["ones_col"], in_=C["ones_col"]).then_inc(s["const"].h, 16)
                sy.dma_start(out=SC["masks"], in_=C["masks"]).then_inc(s["const"].h, 16)
                sy.dma_start(out=SC["e_cmats"], in_=C["e_cmats"]).then_inc(s["const"].h, 16)
                sy.dma_start(out=SC["ident_f32"], in_=C["ident_f32"]).then_inc(s["const"].h, 16)
                sy.dma_start(out=SC["ones_row_f32"], in_=C["ones_row_f32"]).then_inc(s["const"].h, 16)
                s["const"].inc(96)
        C = dict(C)
        C.update(SC)
        _build_main(nc, io, C, s, dbg, dbg_out)
    return nc, io


def _build_main(nc, io, C, s, dbg, dbg_out):
    with (
        nc.sbuf_tensor("qT", [128, HPC * T], BF16) as qT,
        nc.sbuf_tensor("kT", [128, T], BF16) as kT,
        nc.sbuf_tensor("v_sb", [128, T], BF16) as v_sb,
    ):
        with (
            nc.sbuf_tensor("wq_sb", [128, NDMT * HPC * HD], BF16) as wq_sb,
            nc.sbuf_tensor("wkv_sb", [128, NDMT * 2 * HD], BF16) as wkv_sb,
        ):
            _p0_transpose_gather(nc, io, C, s, wq_sb, wkv_sb)
            _p1_qkv(nc, io, C, s, qT, kT, v_sb, wq_sb, wkv_sb)
        _p2_attention(nc, io, C, s, qT, kT, v_sb)
        if dbg == "qkv":
            _p_dbg_qkv(nc, s, dbg_out, qT, kT, v_sb)
            return
    if dbg == "p2":
        _p_dbg_p2(nc, io, s, dbg_out)
        return
    with (
        nc.sbuf_tensor("yT", [128, NDMT * MY], BF16) as yT,
        nc.sbuf_tensor("ygT_sb", [32, MY], F32R) as ygT_sb,
    ):
        with nc.sbuf_tensor("at_my", [128, NDMT * MY], BF16) as at_my:
            _p3_oproj(nc, io, C, s, at_my)
        _p4_postln1(nc, io, C, s, yT, ygT_sb)
        if dbg == "p4":
            _p_dbg_p4(nc, io, s, dbg_out, ygT_sb)
            return
        with nc.sbuf_tensor("gs_sb", [86, 1024], F32R) as gs_sb:
            _p5pre_zeros(nc, io, C, s)
            if dbg == "p5a":
                _p_dbg_p5a(nc, io, s, dbg_out)
                return
            _p5_gateup(nc, io, C, s, yT, ygT_sb, gs_sb)
            if dbg == "p5":
                _p_dbg_p5(nc, io, s, dbg_out, gs_sb)
                return
            with nc.sbuf_tensor("o2_sb", [128, 8 * D], BF16) as o2_sb:
                _p6_down(nc, io, C, s, gs_sb, o2_sb)
                _p7_final(nc, io, C, s, o2_sb)


def _p_dbg_p5a(nc, io, s, dbg_out):
    sd = Sem(nc, "s_dbg")
    with nc.Block("pdbg5a") as block:
        @block.sync
        def _(sy):
            sy.wait_ge(s["zst"].h, 16 * 9)
            sy.wait_ge(s["yst"].h, 16 * 8)
            sy.dma_start(out=dbg_out["dbg_zs"], in_=io["zs_gu_dram"].ap()).then_inc(sd.h, 16)
            sy.dma_start(out=dbg_out["dbg_zsd"], in_=io["zs_d_dram"].ap()).then_inc(sd.h, 16)
            sy.wait_ge(sd.h, 32)


def _p_dbg_p5(nc, io, s, dbg_out, gs_sb):
    sd = Sem(nc, "s_dbg")
    with nc.Block("pdbg5") as block:
        @block.sync
        def _(sy):
            for sl in range(2):
                sy.wait_ge(s["hst"][sl].h, s["hst"][sl].n)
            sy.wait_ge(s["gse"].h, s["gse"].n)
            sy.wait_ge(s["yst"].h, 16 * 8)
            sy.dma_start(out=dbg_out["dbg_hT"], in_=io["hT_dram"].ap()).then_inc(sd.h, 16)
            sy.dma_start(out=dbg_out["dbg_gs"], in_=gs_sb[0:86, :]).then_inc(sd.h, 16)
            sy.dma_start(out=dbg_out["dbg_y2"], in_=io["y_dram"].ap()).then_inc(sd.h, 16)
            sy.dma_start(out=dbg_out["dbg_zs"], in_=io["zs_gu_dram"].ap()).then_inc(sd.h, 16)
            sy.dma_start(out=dbg_out["dbg_zsd"], in_=io["zs_d_dram"].ap()).then_inc(sd.h, 16)
            sy.wait_ge(sd.h, 80)


def _p_dbg_p4(nc, io, s, dbg_out, ygT_sb):
    sd = Sem(nc, "s_dbg")
    with nc.Block("pdbg4") as block:
        @block.vector
        def _(v):
            v.wait_ge(s["yge"].h, 2)
            v.engine_nop().then_inc(sd.h, 1)

        @block.sync
        def _(sy):
            sy.wait_ge(s["ag2"].h, 2)
            sy.wait_ge(s["yst"].h, 16 * 8)
            sy.wait_ge(sd.h, 1)
            sy.dma_start(out=dbg_out["dbg_at0"], in_=io["at_full"][0].ap()).then_inc(sd.h, 16)
            sy.dma_start(out=dbg_out["dbg_at1"], in_=io["at_full"][1].ap()).then_inc(sd.h, 16)
            sy.dma_start(out=dbg_out["dbg_o"], in_=io["o_dram"].ap()).then_inc(sd.h, 16)
            sy.dma_start(out=dbg_out["dbg_y"], in_=io["y_dram"].ap()).then_inc(sd.h, 16)
            sy.dma_start(out=dbg_out["dbg_yg"], in_=ygT_sb[:, :]).then_inc(sd.h, 16)
            sy.wait_ge(sd.h, 81)


def _p0_transpose_gather(nc, io, C, s, wq_sb, wkv_sb):
    with (
        nc.sbuf_tensor("x_my_bf", [128, 2 * D], BF16) as x_my_bf,
        nc.sbuf_tensor("tp_st", [128, 4 * 512], BF16) as tp_st,
        nc.psum_tensor("ps_tpA", [128, 1024], BF16) as ps_tpA,
        nc.psum_tensor("ps_tpB", [128, 1024], BF16) as ps_tpB,
        nc.Block("p0") as block,
    ):
        @block.gpsimd
        def _(g):
            for tt in range(8):
                if tt >= 2:
                    g.wait_ge(s["tp"].h, (tt - 1) * 8)
                sx = s["x"][tt % 2]
                g.dma_start(out=x_my_bf[:, (tt % 2) * D:(tt % 2 + 1) * D],
                            in_=io["x_my"][tt * 128:(tt + 1) * 128, :]).then_inc(sx.h, 16)
                sx.inc(16)
            g.dma_start(out=wq_sb[:, :].rearrange("p (d h) -> p d h", d=NDMT),
                        in_=io["wq_my"].rearrange("(d p) h -> p d h", p=128)).then_inc(s["w"].h, 16)
            g.dma_start(out=wkv_sb[:, 0:NDMT * HD].rearrange("p (d h) -> p d h", d=NDMT),
                        in_=io["wk_my"].rearrange("(d p) h -> p d h", p=128)).then_inc(s["w"].h, 16)
            g.dma_start(out=wkv_sb[:, NDMT * HD:2 * NDMT * HD].rearrange("p (d h) -> p d h", d=NDMT),
                        in_=io["wv_my"].rearrange("(d p) h -> p d h", p=128)).then_inc(s["w"].h, 16)
            s["w"].inc(48)
            for m4 in range(4):
                g.wait_ge(s["ct"][m4].h, 16 * 16)
            g.collective_compute(
                "AllGather", ALU.bypass,
                replica_groups=[list(range(NC))],
                ins=[io["xt_contrib"].ap().opt()],
                outs=[io["xt_full"].ap().opt()],
            ).then_inc(s["ag"].h, 1)
            s["ag"].inc()

        @block.tensor
        def _(t):
            t.wait_ge(s["const"].h, 96)
            n = 0
            for tt in range(8):
                t.wait_ge(s["x"][tt % 2].h, 16 * (tt // 2 + 1))
                for d4 in range(8):
                    pp = n % 2
                    ps_tp = ps_tpA if pp == 0 else ps_tpB
                    if n >= 2:
                        t.wait_ge(s["tpe"].h, n - 1)
                    for k in range(4):
                        d = d4 * 4 + k
                        ins = t.transpose(ps_tp[:, k * 128:(k + 1) * 128],
                                          x_my_bf[:, (tt % 2) * D + d * 128: (tt % 2) * D + (d + 1) * 128],
                                          C["ident_bf"])
                        if k == 3:
                            ins.then_inc(s["tp"].h, 1)
                            s["tp"].inc()
                    n += 1

        @block.vector
        def _(v):
            for g8 in range(64):
                pp = g8 % 2
                m4 = g8 % 4
                ps_tp = ps_tpA if pp == 0 else ps_tpB
                v.wait_ge(s["tp"].h, g8 + 1)
                if g8 >= 4:
                    v.wait_ge(s["ct"][m4].h, 16 * (g8 // 4))
                v.tensor_copy(tp_st[:, m4 * 512:(m4 + 1) * 512],
                              ps_tp[:, 0:512]).then_inc(s["tpe"].h, 1)
                s["tpe"].inc()

        @block.sync
        def _(sy):
            n = 0
            for tt in range(8):
                for d4 in range(8):
                    m4 = n % 4
                    n += 1
                    sy.wait_ge(s["tpe"].h, n)
                    sct = s["ct"][m4]
                    sy.dma_start(
                        out=io["xt_contrib"].ap().rearrange("(d p) t -> p d t", p=128)
                            [:, d4 * 4:(d4 + 1) * 4, tt * 128:(tt + 1) * 128],
                        in_=tp_st[:, m4 * 512:(m4 + 1) * 512]
                            .rearrange("p (d t) -> p d t", d=4),
                    ).then_inc(sct.h, 16)
                    sct.inc(16)


def _p1_qkv(nc, io, C, s, qT, kT, v_sb, wq_sb, wkv_sb):
    with (
        nc.sbuf_tensor("xt_buf", [128, 3 * 16 * 512], BF16) as xt_buf,
        nc.psum_tensor("ps_q", [128, 4 * 512], F32) as ps_q,
        nc.psum_tensor("ps_kv0", [128, 2 * 512], F32) as ps_kv0,
        nc.psum_tensor("ps_kv1", [128, 2 * 512], F32) as ps_kv1,
        nc.Block("p1") as block,
    ):
        xtv = io["xt_full"].ap().rearrange("(r d p) t -> r d p t", d=NDMT, p=128)

        @block.sync
        def _(sy):
            sy.wait_ge(s["ag"].h, 1)
            nb = 0
            for ch in range(16):
                r, half = ch // 2, ch % 2
                for hh in range(2):
                    b = nb % 3
                    if nb >= 3:
                        sy.wait_ge(s["mmkv"].h, (nb - 3) // 2 + 1)
                    sxt = s["xt"][b]
                    sy.dma_start(
                        out=xt_buf[:, b * 8192:(b + 1) * 8192]
                            .rearrange("p (d t) -> p d t", d=16),
                        in_=xtv[r, hh * 16:(hh + 1) * 16, :, half * 512:(half + 1) * 512]
                            .transpose([1, 0, 2]),
                    ).then_inc(sxt.h, 16)
                    sxt.inc(16)
                    nb += 1

        @block.tensor
        def _(t):
            t.wait_ge(s["w"].h, 48)
            nb = 0
            for ch in range(16):
                for hh in range(2):
                    b = nb % 3
                    nb += 1
                    t.wait_ge(s["xt"][b].h, 16 * ((nb - 1) // 3 + 1))
                    if hh == 0 and ch > 0:
                        t.wait_ge(s["evq"].h, ch)
                    for d16 in range(16):
                        d = hh * 16 + d16
                        xs = xt_buf[:, b * 8192 + d16 * 512:b * 8192 + (d16 + 1) * 512]
                        for h in range(HPC):
                            ins = t.matmul(ps_q[:, h * 512:(h + 1) * 512],
                                           wq_sb[:, (d * HPC + h) * HD:(d * HPC + h + 1) * HD],
                                           xs, start=(d == 0), stop=(d == NDMT - 1))
                            if d == NDMT - 1 and h == HPC - 1:
                                ins.then_inc(s["mmq"].h, 1)
                                s["mmq"].inc()
                ps_kv = ps_kv0 if ch % 2 == 0 else ps_kv1
                if ch >= 2:
                    t.wait_ge(s["evkv"].h, ch - 1)
                for hh in range(2):
                    b = (nb - 2 + hh) % 3
                    for d16 in range(16):
                        d = hh * 16 + d16
                        xs = xt_buf[:, b * 8192 + d16 * 512:b * 8192 + (d16 + 1) * 512]
                        t.matmul(ps_kv[:, 0:512],
                                 wkv_sb[:, d * HD:(d + 1) * HD],
                                 xs, start=(d == 0), stop=(d == NDMT - 1))
                        for q4 in range(4):
                            ins = t.matmul(ps_kv[:, 512 + q4 * 128:512 + (q4 + 1) * 128],
                                           xs[:, q4 * 128:(q4 + 1) * 128],
                                           wkv_sb[:, NDMT * HD + d * HD:NDMT * HD + (d + 1) * HD],
                                           start=(d == 0 and q4 == 0), stop=(d == NDMT - 1 and q4 == 3))
                            if d == NDMT - 1 and q4 == 3:
                                ins.then_inc(s["mmkv"].h, 1)
                                s["mmkv"].inc()

        @block.vector
        def _(v):
            for ch in range(16):
                v.wait_ge(s["mmq"].h, ch + 1)
                for h in range(HPC):
                    ins = v.tensor_scalar(qT[:, h * T + ch * 512:h * T + (ch + 1) * 512],
                                          ps_q[:, h * 512:(h + 1) * 512],
                                          SCALE, None, ALU.mult)
                    if h == HPC - 1:
                        ins.then_inc(s["evq"].h, 1)
                        s["evq"].inc()

        @block.scalar
        def _(sc):
            for ch in range(16):
                ps_kv = ps_kv0 if ch % 2 == 0 else ps_kv1
                sc.wait_ge(s["mmkv"].h, ch + 1)
                sc.copy(kT[:, ch * 512:(ch + 1) * 512], ps_kv[:, 0:512])
                sc.copy(v_sb[:, ch * 512:(ch + 1) * 512],
                        ps_kv[:, 512:1024]).then_inc(s["evkv"].h, 1)
                s["evkv"].inc()


def _p2_attention(nc, io, C, s, qT, kT, v_sb):
    for k in ["sc", "exp", "msk", "pav", "lav", "rec", "bca", "lrbe", "avv"]:
        s[k] = Sem(nc, "s_" + k)
    s["avst"] = [Sem(nc, f"s_avst{i}") for i in range(2)]
    s["ag2"] = Sem(nc, "s_ag2")

    groups = [(h, b, g) for h in range(HPC) for b in range(B) for g in range(8)]
    npairs = [2 * (g + 1) for (_, _, g) in groups]
    ppre = [0]
    for np_ in npairs:
        ppre.append(ppre[-1] + np_)

    with (
        nc.sbuf_tensor("probs", [128, 2 * 1024], BF16) as probs,
        nc.sbuf_tensor("lrec", [1, 512], F32) as lrec,
        nc.sbuf_tensor("lrb_sb", [128, 512], F32) as lrb_sb,
        nc.sbuf_tensor("av_sb", [128, 2 * 512], BF16) as av_sb,
        nc.psum_tensor("ps_sc0", [128, 1024], F32) as ps_sc0,
        nc.psum_tensor("ps_sc1", [128, 1024], F32) as ps_sc1,
        nc.psum_tensor("ps_av", [128, 512], F32) as ps_av,
        nc.psum_tensor("ps_l", [1, 512], F32) as ps_l,
        nc.psum_tensor("ps_lrb", [128, 512], F32) as ps_lrb,
        nc.Block("p2") as block,
    ):
        @block.tensor
        def _(t):
            def emit_qk(gi, kp):
                h, b, g = groups[gi]
                P = ppre[gi] + kp
                qs = qT[:, h * T + b * S + g * 512: h * T + b * S + (g + 1) * 512]
                ps = ps_sc0 if P % 2 == 0 else ps_sc1
                if P >= 2:
                    t.wait_ge(s["exp"].h, P - 1)
                for j in range(2):
                    kt = kp * 2 + j
                    ins = t.matmul(ps[:, j * 512:(j + 1) * 512],
                                   kT[:, b * S + kt * 128:b * S + (kt + 1) * 128],
                                   qs, start=True, stop=True)
                ins.then_inc(s["sc"].h, 1)
                s["sc"].inc()

            def emit_lav(gi, kp):
                h, b, g = groups[gi]
                npair = npairs[gi]
                nkt = 4 * (g + 1)
                P = ppre[gi] + kp
                d = kp - (npair - 2)
                if kp == 0:
                    t.wait_ge(s["rec"].h, gi)
                    t.wait_ge(s["avv"].h, gi)
                t.wait_ge(s["exp"].h, P + 1)
                if d >= 0:
                    t.wait_ge(s["msk"].h, 4 * gi + 2 * (d + 1))
                pb = probs[:, (P % 2) * 1024:(P % 2 + 1) * 1024]
                for j in range(2):
                    kt = kp * 2 + j
                    pc = pb[:, j * 512:(j + 1) * 512]
                    t.matmul(ps_l[0:1, :], C["ones_col"], pc,
                             start=(kt == 0), stop=(kt == nkt - 1))
                    ins = t.matmul(ps_av[:, :],
                                   v_sb[:, (b * 32 + kt) * 128:(b * 32 + kt + 1) * 128],
                                   pc, start=(kt == 0), stop=(kt == nkt - 1))
                ins.then_inc(s["pav"].h, 1)
                s["pav"].inc()

            for gi, (h, b, g) in enumerate(groups):
                npair = npairs[gi]
                for kp in range(npair):
                    emit_qk(gi, kp)
                    if kp >= 2:
                        emit_lav(gi, kp - 2)
                emit_lav(gi, npair - 2)
                emit_lav(gi, npair - 1)
                t.wait_ge(s["rec"].h, gi + 1)
                t.wait_ge(s["lrbe"].h, gi)
                t.matmul(ps_lrb[:, :], C["ones_row_f32"], lrec[0:1, :],
                         start=True, stop=True).then_inc(s["bca"].h, 1)
                s["bca"].inc()

        @block.scalar
        def _(sc):
            for gi, (h, b, g) in enumerate(groups):
                npair = npairs[gi]
                for kp in range(npair):
                    P = ppre[gi] + kp
                    ps = ps_sc0 if P % 2 == 0 else ps_sc1
                    sc.wait_ge(s["sc"].h, P + 1)
                    if P >= 2:
                        sc.wait_ge(s["pav"].h, P - 1)
                    sc.activation(probs[:, (P % 2) * 1024:(P % 2 + 1) * 1024],
                                  ps[:, :], AF.Exp).then_inc(s["exp"].h, 1)
                    s["exp"].inc()

        @block.vector
        def _(v):
            for gi, (h, b, g) in enumerate(groups):
                npair = npairs[gi]
                for d in range(2):
                    kp = npair - 2 + d
                    P = ppre[gi] + kp
                    v.wait_ge(s["exp"].h, P + 1)
                    pb = probs[:, (P % 2) * 1024:(P % 2 + 1) * 1024]
                    for j in range(2):
                        o = 2 * d + j
                        pc = pb[:, j * 512:(j + 1) * 512]
                        v.tensor_tensor(pc, pc, C["masks"][:, o * 512:(o + 1) * 512],
                                        ALU.mult).then_inc(s["msk"].h, 1)
                        s["msk"].inc()
                v.wait_ge(s["pav"].h, ppre[gi + 1])
                v.reciprocal(lrec[0:1, :], ps_l[0:1, :]).then_inc(s["rec"].h, 1)
                s["rec"].inc()
                v.wait_ge(s["bca"].h, gi + 1)
                v.tensor_copy(lrb_sb[:, :], ps_lrb[:, :]).then_inc(s["lrbe"].h, 1)
                s["lrbe"].inc()
                slot = gi % 2
                if gi >= 2:
                    v.wait_ge(s["avst"][slot].h, 16 * (gi // 2))
                v.wait_ge(s["lrbe"].h, gi + 1)
                v.tensor_tensor(av_sb[:, slot * 512:(slot + 1) * 512],
                                ps_av[:, :], lrb_sb[:, :], ALU.mult).then_inc(s["avv"].h, 1)
                s["avv"].inc()

        @block.sync
        def _(sy):
            for gi, (h, b, g) in enumerate(groups):
                slot = gi % 2
                sy.wait_ge(s["avv"].h, gi + 1)
                sy.dma_start(
                    out=io["at_contrib"][h // 2].ap()
                        [(h % 2) * 128:(h % 2) * 128 + 128,
                         b * S + g * 512:b * S + (g + 1) * 512],
                    in_=av_sb[:, slot * 512:(slot + 1) * 512],
                ).then_inc(s["avst"][slot].h, 16)
                s["avst"][slot].inc(16)

        @block.gpsimd
        def _(gp):
            for chunk in range(2):
                gp.wait_ge(s["avst"][0].h, 16 * 16 * (chunk + 1))
                gp.wait_ge(s["avst"][1].h, 16 * 16 * (chunk + 1))
                gp.collective_compute(
                    "AllGather", ALU.bypass,
                    replica_groups=[list(range(NC))],
                    ins=[io["at_contrib"][chunk].ap().opt()],
                    outs=[io["at_full"][chunk].ap().opt()],
                ).then_inc(s["ag2"].h, 1)
                s["ag2"].inc()


def _p_dbg_p2(nc, io, s, dbg_out):
    sd = Sem(nc, "s_dbg")
    with nc.Block("pdbg2") as block:
        @block.sync
        def _(sy):
            sy.wait_ge(s["ag2"].h, 2)
            sy.dma_start(out=dbg_out["dbg_at0"], in_=io["at_full"][0].ap()).then_inc(sd.h, 16)
            sy.dma_start(out=dbg_out["dbg_at1"], in_=io["at_full"][1].ap()).then_inc(sd.h, 16)
            sy.wait_ge(sd.h, 32)


def _p_dbg_qkv(nc, s, dbg_out, qT, kT, v_sb):
    sd = Sem(nc, "s_dbg")
    with nc.Block("pdbg") as block:
        @block.sync
        def _(sy):
            sy.wait_ge(s["evq"].h, 16)
            sy.wait_ge(s["evkv"].h, 16)
            sy.dma_start(out=dbg_out["dbg_qT"], in_=qT[:, :]).then_inc(sd.h, 16)
            sy.dma_start(out=dbg_out["dbg_kT"], in_=kT[:, :]).then_inc(sd.h, 16)
            sy.dma_start(out=dbg_out["dbg_v"], in_=v_sb[:, :]).then_inc(sd.h, 16)
            sy.wait_ge(sd.h, 48)


def _p3_oproj(nc, io, C, s, at_my):
    for k in ["gth", "omm", "oev", "gix"]:
        s[k] = Sem(nc, "s_" + k)
    s["wo"] = [Sem(nc, f"s_wo{i}") for i in range(2)]
    s["ost"] = [Sem(nc, f"s_ost{i}") for i in range(2)]

    with (
        nc.sbuf_tensor("gix_sb", [128, 32], I32) as gix_sb,
        nc.sbuf_tensor("wo_sl", [128, 2 * 32 * 256], BF16) as wo_sl,
        nc.sbuf_tensor("o_st", [128, 2 * 2048], BF16) as o_st,
        nc.psum_tensor("ps_oA", [128, 2048], F32) as ps_oA,
        nc.psum_tensor("ps_oB", [128, 2048], F32) as ps_oB,
        nc.Block("p3") as block,
    ):
        wov = io["wo"].rearrange("(d p) n -> p d n", p=128)

        @block.sync
        def _(sy):
            sy.dma_start(out=gix_sb[:, :], in_=io["gidx"]).then_inc(s["gix"].h, 16)
            s["gix"].inc(16)
            for hc in range(16):
                sl = hc % 2
                sy.wait_ge(s["oev"].h, hc + 1)
                sy.dma_start(
                    out=io["o_dram"].ap().rearrange("(t p) n -> p t n", p=128)
                        [:, :, hc * 256:(hc + 1) * 256],
                    in_=o_st[:, sl * 2048:(sl + 1) * 2048]
                        .rearrange("p (t n) -> p t n", t=8),
                ).then_inc(s["ost"][sl].h, 16)
                s["ost"][sl].inc(16)

        @block.gpsimd
        def _(g):
            g.wait_ge(s["gix"].h, 16)
            g.wait_ge(s["ag2"].h, 2)
            for chunk in range(2):
                src = io["at_full"][chunk].ap().rearrange("r (tb tk) -> (r tb) tk", tk=MY)
                for t in range(16):
                    head = 4 * (t // 2) + 2 * chunk + (t % 2)
                    i = chunk * 16 + t
                    g.indirect_dma_start(
                        at_my[:, head * MY:(head + 1) * MY], None,
                        src, IndirectOffsetOnAxis(ap=gix_sb[:, i:i + 1], axis=0),
                    ).then_inc(s["gth"].h, 16)
                    s["gth"].inc(16)
            for hc in range(16):
                sl = hc % 2
                if hc >= 2:
                    g.wait_ge(s["omm"].h, hc - 1)
                g.dma_start(
                    out=wo_sl[:, sl * 8192:(sl + 1) * 8192]
                        .rearrange("p (d n) -> p d n", d=NDMT),
                    in_=wov[:, :, hc * 256:(hc + 1) * 256],
                ).then_inc(s["wo"][sl].h, 16)
                s["wo"][sl].inc(16)

        @block.tensor
        def _(t):
            t.wait_ge(s["gth"].h, 16 * 32)
            for hc in range(16):
                sl = hc % 2
                ps_o = ps_oA if sl == 0 else ps_oB
                t.wait_ge(s["wo"][sl].h, 16 * (hc // 2 + 1))
                if hc >= 2:
                    t.wait_ge(s["oev"].h, hc - 1)
                for d in range(NDMT):
                    for tt in range(8):
                        ins = t.matmul(
                            ps_o[:, tt * 256:(tt + 1) * 256],
                            at_my[:, d * MY + tt * 128:d * MY + (tt + 1) * 128],
                            wo_sl[:, sl * 8192 + d * 256:sl * 8192 + (d + 1) * 256],
                            start=(d == 0 and tt % 2 == 0),
                            stop=(d == NDMT - 1 and tt % 2 == 1))
                ins.then_inc(s["omm"].h, 1)
                s["omm"].inc()

        @block.vector
        def _(v):
            for hc in range(16):
                sl = hc % 2
                ps_o = ps_oA if sl == 0 else ps_oB
                v.wait_ge(s["omm"].h, hc + 1)
                if hc >= 2:
                    v.wait_ge(s["ost"][sl].h, 16 * (hc // 2))
                ins = v.tensor_copy(o_st[:, sl * 2048:(sl + 1) * 2048], ps_o[:, :])
                ins.then_inc(s["oev"].h, 1)
                s["oev"].inc()


def _p4_postln1(nc, io, C, s, yT, ygT_sb):
    for k in ["lnv", "lna", "yt", "yte", "ygm", "yge", "xl", "ol", "yst"]:
        s[k] = Sem(nc, "s_" + k)
    NV, NA = 9, 2

    with (
        nc.sbuf_tensor("x_tile", [128, D], F32) as x_tile,
        nc.sbuf_tensor("o_tile", [128, D], BF16) as o_tile,
        nc.sbuf_tensor("z_sb", [128, D], F32) as z_sb,
        nc.sbuf_tensor("sq_sb", [128, D], F32) as sq_sb,
        nc.sbuf_tensor("y_tile", [128, D], BF16) as y_tile,
        nc.sbuf_tensor("ln_sc", [128, 8], F32) as ln_sc,
        nc.psum_tensor("ps_ytA", [128, 1024], BF16) as ps_ytA,
        nc.psum_tensor("ps_ytB", [128, 1024], BF16) as ps_ytB,
        nc.psum_tensor("ps_yg", [32, 1024], F32) as ps_yg,
        nc.Block("p4") as block,
    ):
        S1, S2, M, M2, VE, STD, RSTD, NEGMR = range(8)

        @block.sync
        def _(sy):
            for sl in range(2):
                sy.wait_ge(s["ost"][sl].h, 16 * 8)
            for tt in range(8):
                if tt > 0:
                    sy.wait_ge(s["lnv"].h, (tt - 1) * NV + 1)
                sy.dma_start(out=x_tile[:, :],
                             in_=io["x_my"][tt * 128:(tt + 1) * 128, :]).then_inc(s["xl"].h, 16)
                s["xl"].inc(16)
                sy.dma_start(out=o_tile[:, :],
                             in_=io["o_dram"].ap()[tt * 128:(tt + 1) * 128, :]).then_inc(s["ol"].h, 16)
                s["ol"].inc(16)
                sy.wait_ge(s["lnv"].h, tt * NV + 9)
                sy.dma_start(out=io["y_dram"].ap()[tt * 128:(tt + 1) * 128, :],
                             in_=y_tile[:, :]).then_inc(s["yst"].h, 16)
                s["yst"].inc(16)

        @block.vector
        def _(v):
            for tt in range(8):
                base = tt * NV
                abase = tt * NA
                v.wait_ge(s["xl"].h, 16 * (tt + 1))
                v.wait_ge(s["ol"].h, 16 * (tt + 1))

                v.tensor_tensor(z_sb[:, :], x_tile[:, :],
                                o_tile[:, :], ALU.add).then_inc(s["lnv"].h, 1)
                v.wait_ge(s["lnv"].h, base + 1)
                v.reduce_sum(ln_sc[:, S1:S1 + 1], z_sb[:, :], axis=AX.X).then_inc(s["lnv"].h, 1)
                v.wait_ge(s["lna"].h, abase + 1)
                v.reduce_sum(ln_sc[:, S2:S2 + 1], sq_sb[:, :], axis=AX.X).then_inc(s["lnv"].h, 1)
                v.wait_ge(s["lnv"].h, base + 2)
                v.tensor_scalar(ln_sc[:, M:M + 1], ln_sc[:, S1:S1 + 1],
                                1.0 / D, None, ALU.mult).then_inc(s["lnv"].h, 1)
                v.wait_ge(s["lnv"].h, base + 4)
                v.tensor_scalar(ln_sc[:, M2:M2 + 1], ln_sc[:, M:M + 1],
                                ln_sc[:, M:M + 1], -1e-5,
                                ALU.mult, ALU.add).then_inc(s["lnv"].h, 1)
                v.wait_ge(s["lnv"].h, base + 5)
                v.scalar_tensor_tensor(ln_sc[:, VE:VE + 1], ln_sc[:, S2:S2 + 1],
                                       1.0 / D, ln_sc[:, M2:M2 + 1],
                                       ALU.mult, ALU.subtract).then_inc(s["lnv"].h, 1)
                v.wait_ge(s["lna"].h, abase + 2)
                v.reciprocal(ln_sc[:, RSTD:RSTD + 1],
                             ln_sc[:, STD:STD + 1]).then_inc(s["lnv"].h, 1)
                v.wait_ge(s["lnv"].h, base + 7)
                v.scalar_tensor_tensor(ln_sc[:, NEGMR:NEGMR + 1], ln_sc[:, M:M + 1],
                                       -1.0, ln_sc[:, RSTD:RSTD + 1],
                                       ALU.mult, ALU.mult).then_inc(s["lnv"].h, 1)
                v.wait_ge(s["lnv"].h, base + 8)
                if tt > 0:
                    v.wait_ge(s["yst"].h, 16 * tt)
                    v.wait_ge(s["yt"].h, tt * 32)
                v.tensor_scalar(y_tile[:, :], z_sb[:, :],
                                ln_sc[:, RSTD:RSTD + 1], ln_sc[:, NEGMR:NEGMR + 1],
                                ALU.mult, ALU.add).then_inc(s["lnv"].h, 1)
                for cj in range(NDMT):
                    n = tt * NDMT + cj
                    pp = n % 2
                    ps_yt = ps_ytA if pp == 0 else ps_ytB
                    v.wait_ge(s["yt"].h, n + 1)
                    v.tensor_copy(yT[:, cj * MY + tt * 128:cj * MY + (tt + 1) * 128],
                                  ps_yt[:, 0:128]).then_inc(s["yte"].h, 1)
                    s["yte"].inc()
            for half in range(2):
                v.wait_ge(s["ygm"].h, half + 1)
                v.tensor_copy(ygT_sb[:, half * 512:(half + 1) * 512],
                              ps_yg[:, half * 512:(half + 1) * 512]).then_inc(s["yge"].h, 1)
                s["yge"].inc()

        @block.scalar
        def _(sc):
            for tt in range(8):
                base = tt * NV
                sc.wait_ge(s["lnv"].h, base + 1)
                sc.activation(sq_sb[:, :], z_sb[:, :], AF.Square).then_inc(s["lna"].h, 1)
                sc.wait_ge(s["lnv"].h, base + 6)
                sc.activation(ln_sc[:, STD:STD + 1], ln_sc[:, VE:VE + 1],
                              AF.Sqrt).then_inc(s["lna"].h, 1)
                s["lna"].inc(2)

        @block.tensor
        def _(t):
            n = 0
            for tt in range(8):
                t.wait_ge(s["lnv"].h, tt * NV + 9)
                for cj in range(NDMT):
                    c4, j = cj // 8, cj % 8
                    pp = n % 2
                    ps_yt = ps_ytA if pp == 0 else ps_ytB
                    if n >= 2:
                        t.wait_ge(s["yte"].h, n - 1)
                    src = y_tile[:, :].rearrange("p (a q e) -> p a q e", a=4, e=8)[:, c4, :, j]
                    ins = t.transpose(ps_yt[:, 0:128], src, C["ident_bf"])
                    ins.then_inc(s["yt"].h, 1)
                    s["yt"].inc()
                    n += 1
            t.wait_ge(s["yte"].h, 256)
            for half in range(2):
                for cj in range(NDMT):
                    c4 = cj // 8
                    ins = t.matmul(
                        ps_yg[:, half * 512:(half + 1) * 512],
                        C["e_cmats"][:, c4 * 32:(c4 + 1) * 32],
                        yT[:, cj * MY + half * 512:cj * MY + (half + 1) * 512],
                        start=(cj == 0), stop=(cj == NDMT - 1))
                ins.then_inc(s["ygm"].h, 1)
                s["ygm"].inc()



def _p5pre_zeros(nc, io, C, s):
    for k in ["zplq", "zpls", "zc", "zst"]:
        s[k] = Sem(nc, "s_" + k)

    CH = INTER // 4
    with (
        nc.sbuf_tensor("qz_sb", [86, 512], I32) as qz_sb,
        nc.sbuf_tensor("z4_sb", [86, 4096], I32) as z4_sb,
        nc.sbuf_tensor("tmp_sb", [86, 4096], F32) as tmp_sb,
        nc.sbuf_tensor("szs_sb", [86, 4096], BF16) as szs_sb,
        nc.sbuf_tensor("zs_out", [86, 4096], F32R) as zs_out,
        nc.Block("p5pre") as block,
    ):
        chunks = []
        for mat, (qz, sc) in enumerate([(io["g_qz"], io["g_sc"]), (io["u_qz"], io["u_sc"])]):
            for ci in range(4):
                chunks.append((qz, sc, io["zs_gu_dram"], mat * INTER + ci * CH,
                               32, CH, ci * (CH // 8), ci * CH))
        chunks.append((io["d_qz"], io["d_sc"], io["zs_d_dram"], 0, 86, D, 0, 0))

        @block.sync
        def _(sy):
            sy.wait_ge(s["yst"].h, 16 * 8)
            for i, (qz, sc, dst, col, rows, w, qzoff, scoff) in enumerate(chunks):
                if i >= 1:
                    sy.wait_ge(s["zc"].h, 10 * i)
                sy.dma_start(out=qz_sb[0:rows, 0:w // 8],
                             in_=qz[:, qzoff:qzoff + w // 8]).then_inc(s["zplq"].h, 16)
                s["zplq"].inc(16)

        @block.gpsimd
        def _(g):
            g.wait_ge(s["yst"].h, 16 * 8)

            def store(i):
                _, _, dst, col, rows, w, _, _ = chunks[i]
                g.wait_ge(s["zc"].h, 10 * (i + 1))
                g.dma_start(out=dst.ap()[0:rows, col:col + w],
                            in_=zs_out[0:rows, 0:w]).then_inc(s["zst"].h, 16)
                s["zst"].inc(16)

            for i, (qz, sc, dst, col, rows, w, qzoff, scoff) in enumerate(chunks):
                if i >= 1:
                    g.wait_ge(s["zc"].h, 10 * i)
                g.dma_start(out=szs_sb[0:rows, 0:w],
                            in_=sc[:, scoff:scoff + w]).then_inc(s["zpls"].h, 16)
                s["zpls"].inc(16)
                if i >= 1:
                    store(i - 1)
            store(len(chunks) - 1)

        @block.vector
        def _(v):
            v.wait_ge(s["yst"].h, 16 * 8)
            for i, (qz, sc, dst, col, rows, w, qzoff, scoff) in enumerate(chunks):
                v.wait_ge(s["zplq"].h, 16 * (i + 1))
                v.wait_ge(s["zpls"].h, 16 * (i + 1))
                v.wait_ge(s["zc"].h, 10 * i)
                if i >= 1:
                    v.wait_ge(s["zst"].h, 16 * i)
                z4v = z4_sb[0:rows, 0:w].rearrange("g (m e) -> g m e", e=8)
                for j in range(8):
                    ins = v.tensor_scalar(z4v[:, :, j], qz_sb[0:rows, 0:w // 8],
                                          4 * j, 15,
                                          ALU.logical_shift_right, ALU.bitwise_and)
                    ins.then_inc(s["zc"].h, 1)
                    s["zc"].inc()
                v.wait_ge(s["zc"].h, s["zc"].n)
                v.tensor_scalar(tmp_sb[0:rows, 0:w], z4_sb[0:rows, 0:w],
                                -7.0, -1.0, ALU.add, ALU.mult).then_inc(s["zc"].h, 1)
                s["zc"].inc()
                v.wait_ge(s["zc"].h, s["zc"].n)
                v.tensor_tensor(zs_out[0:rows, 0:w], tmp_sb[0:rows, 0:w],
                                szs_sb[0:rows, 0:w], ALU.mult).then_inc(s["zc"].h, 1)
                s["zc"].inc()


def _p5_gateup(nc, io, C, s, yT, ygT_sb, gs_sb):
    for k in ["dqa", "dqb", "gstp", "ustp", "silu", "mul", "gstm", "gtr", "gse", "zsl", "gev"]:
        s[k] = Sem(nc, "s_" + k)
    s["qw"] = [Sem(nc, f"s_qw{i}") for i in range(2)]
    s["stl"] = [Sem(nc, f"s_stl{i}") for i in range(2)]
    s["hst"] = [Sem(nc, f"s_hst{i}") for i in range(2)]
    s["zsl2"] = [Sem(nc, f"s_zsl{i}") for i in range(2)]

    NPN = INTER // 256
    NPN_R = P5_LIMIT if P5_LIMIT else NPN
    NIT_R = 2 * NPN_R
    trunc = NPN_R != NPN

    with (
        nc.sbuf_tensor("qw_sl", [128, 2 * 8 * 256], I32) as qw_sl,
        nc.sbuf_tensor("st_sl", [128, 2 * 8 * 256], F32) as st_sl,
        nc.sbuf_tensor("st_bf", [128, 2 * 8 * 256], BF16) as st_bf,
        nc.sbuf_tensor("panelG", [128, 2 * 8192], BF16) as panelG,
        nc.sbuf_tensor("panelU", [128, 2 * 8192], BF16) as panelU,
        nc.sbuf_tensor("zs_sl", [32, 2 * 512], F32R) as zs_sl,
        nc.sbuf_tensor("silu_sb", [128, 2 * 1024], BF16) as silu_sb,
        nc.sbuf_tensor("hT_st", [128, 2 * 1024], BF16) as hT_st,
        nc.sbuf_tensor("gst_sb", [128, 8 * 86], F32) as gst_sb,
        nc.sbuf_tensor("dqt", [128, 8 * 256], I32) as dqt,
        nc.psum_tensor("ps_g", [128, 1024], F32) as ps_g,
        nc.psum_tensor("ps_u", [128, 1024], F32) as ps_u,
        nc.psum_tensor("ps_gst", [128, 8], F32) as ps_gst,
        nc.psum_tensor("ps_gtr", [86, 1024], F32) as ps_gtr,
        nc.Block("p5") as block,
    ):
        qws = [io["g_qw"], io["u_qw"]]
        scs = [io["g_sc"], io["u_sc"]]
        panels = [panelG, panelU]

        @block.sync
        def _(sy):
            sy.wait_ge(s["yst"].h, 16 * 8)
            sy.wait_ge(s["zc"].h, 90)
            for pn in range(NPN_R):
                sl = pn % 2
                if pn >= 2:
                    sy.wait_ge(s["dqb"].h, 8 * (pn - 1))
                for mat in range(2):
                    for c4 in range(4):
                        sy.dma_start(
                            out=qw_sl[:, (sl * 8 + mat * 4 + c4) * 256:
                                      (sl * 8 + mat * 4 + c4 + 1) * 256],
                            in_=qws[mat][128 * c4:128 * (c4 + 1),
                                         pn * 256:(pn + 1) * 256],
                        ).then_inc(s["qw"][sl].h, 16)
                        s["qw"][sl].inc(16)
                        src = scs[mat][8 * c4:8 * (c4 + 1), pn * 256:(pn + 1) * 256]
                        src = src.unsqueeze(1).broadcast_to([8, 16, 256])
                        sy.dma_start(
                            out=st_sl[:, (sl * 8 + mat * 4 + c4) * 256:
                                      (sl * 8 + mat * 4 + c4 + 1) * 256],
                            in_=src,
                        ).then_inc(s["stl"][sl].h, 16)
                        s["stl"][sl].inc(16)
                if pn >= 2:
                    sy.wait_ge(s["ustp"].h, 2 * pn - 2)
                sy.dma_start(
                    out=zs_sl[:, sl * 512:sl * 512 + 256],
                    in_=io["zs_gu_dram"].ap()[0:32, pn * 256:(pn + 1) * 256],
                ).then_inc(s["zsl2"][sl].h, 16)
                sy.dma_start(
                    out=zs_sl[:, sl * 512 + 256:(sl + 1) * 512],
                    in_=io["zs_gu_dram"].ap()[0:32, INTER + pn * 256:INTER + (pn + 1) * 256],
                ).then_inc(s["zsl2"][sl].h, 16)
                s["zsl2"][sl].inc(32)

        @block.gpsimd
        def _(g):
            g.wait_ge(s["yst"].h, 16 * 8)
            g.wait_ge(s["zc"].h, 90)

            def hstore(it):
                sl = it % 2
                g.wait_ge(s["mul"].h, it + 1)
                g.dma_start(out=io["hT_dram"].ap()[it * 128:(it + 1) * 128, :],
                            in_=hT_st[:, sl * 1024:(sl + 1) * 1024]).then_inc(s["hst"][sl].h, 16)
                s["hst"][sl].inc(16)

            for it in range(NIT_R):
                hstore(it)

        @block.vector
        def _(v):
            v.wait_ge(s["yst"].h, 16 * 8)
            v.wait_ge(s["zst"].h, 16 * 9)
            def emit_dequant(pn):
                sl = pn % 2
                v.wait_ge(s["qw"][sl].h, 128 * (pn // 2 + 1))
                v.wait_ge(s["stl"][sl].h, 128 * (pn // 2 + 1))
                if pn >= 2:
                    v.wait_ge(s["ustp"].h, 2 * pn - 2)
                v.tensor_copy(st_bf[:, sl * 2048:(sl + 1) * 2048],
                              st_sl[:, sl * 2048:(sl + 1) * 2048]).then_inc(s["dqa"].h, 1)
                s["dqa"].inc()
                for r, (mat, c4) in enumerate([(m, c) for m in range(2) for c in range(4)]):
                    gr = pn * 8 + r
                    base = (sl * 8 + mat * 4 + c4) * 256
                    if gr >= 1:
                        v.wait_ge(s["dqb"].h, gr)
                    for j in range(8):
                        ins = v.tensor_scalar(dqt[:, j * 256:(j + 1) * 256],
                                              qw_sl[:, base:base + 256], 4 * j, 15,
                                              ALU.logical_shift_right, ALU.bitwise_and)
                    ins.then_inc(s["dqa"].h, 1)
                    s["dqa"].inc()
                    v.wait_ge(s["dqa"].h, s["dqa"].n)
                    pan = panels[mat]
                    for j in range(8):
                        cj = c4 * 8 + j
                        ins = v.scalar_tensor_tensor(
                            pan[:, sl * 8192 + cj * 256:sl * 8192 + (cj + 1) * 256],
                            dqt[:, j * 256:(j + 1) * 256], -8.0,
                            st_bf[:, base:base + 256], ALU.add, ALU.mult)
                    ins.then_inc(s["dqb"].h, 1)
                    s["dqb"].inc()

            def emit_gev(it):
                if NO_AUX:
                    return
                v.wait_ge(s["gstm"].h, it + 1)
                v.tensor_copy(gst_sb[:, it * 8:(it + 1) * 8], ps_gst[:, :]).then_inc(s["gev"].h, 1)
                s["gev"].inc()

            def emit_mul(it):
                sl = it % 2
                v.wait_ge(s["ustp"].h, it + 1)
                v.wait_ge(s["silu"].h, it + 1)
                if it >= 2:
                    v.wait_ge(s["hst"][sl].h, 16 * (it // 2))
                    if not NO_AUX:
                        v.wait_ge(s["gstm"].h, it - 1)
                v.tensor_tensor(hT_st[:, sl * 1024:sl * 1024 + 512],
                                ps_u[:, 0:512], silu_sb[:, sl * 1024:sl * 1024 + 512],
                                ALU.mult)
                v.tensor_tensor(hT_st[:, sl * 1024 + 512:(sl + 1) * 1024],
                                ps_u[:, 512:1024], silu_sb[:, sl * 1024 + 512:(sl + 1) * 1024],
                                ALU.mult).then_inc(s["mul"].h, 1)
                s["mul"].inc()

            for pn in range(NPN_R):
                emit_dequant(pn)
                if pn >= 2:
                    emit_gev(2 * (pn - 2))
                    emit_gev(2 * (pn - 2) + 1)
                if pn >= 1:
                    emit_mul(2 * (pn - 1))
                    emit_mul(2 * (pn - 1) + 1)
            if NPN_R >= 2:
                emit_gev(2 * NPN_R - 4)
                emit_gev(2 * NPN_R - 3)
            emit_mul(2 * (NPN_R - 1))
            emit_mul(2 * (NPN_R - 1) + 1)
            emit_gev(2 * NPN_R - 2)
            emit_gev(2 * NPN_R - 1)
            if not NO_AUX:
                v.wait_ge(s["gev"].h, NIT_R)
                v.engine_nop().then_inc(s["gse"].h, 1)
                s["gse"].inc()
                if not NO_TR:
                    v.wait_ge(s["gtr"].h, 8)
                    v.tensor_copy(gs_sb[0:86, :],
                                  ps_gtr[0:86, :]).then_inc(s["gse"].h, 1)
                    s["gse"].inc()

        @block.scalar
        def _(sc):
            for it in range(NIT_R):
                sl = it % 2
                sc.wait_ge(s["gstp"].h, it + 1)
                if it >= 2:
                    sc.wait_ge(s["mul"].h, it - 1)
                sc.activation(silu_sb[:, sl * 1024:(sl + 1) * 1024],
                              ps_g[:, :], AF.Silu).then_inc(s["silu"].h, 1)
                s["silu"].inc()

        @block.tensor
        def _(t):
            t.wait_ge(s["yge"].h, 2)
            t.wait_ge(s["zst"].h, 16 * 9)
            for pn in range(NPN_R):
                sl = pn % 2
                t.wait_ge(s["dqb"].h, 8 * (pn + 1))
                t.wait_ge(s["zsl2"][pn % 2].h, 32 * (pn // 2 + 1))
                for itp in range(2):
                    it = 2 * pn + itp
                    if it > 0:
                        t.wait_ge(s["silu"].h, it)
                    for cj in range(NDMT):
                        for th in range(2):
                            ins = t.matmul(ps_g[:, th * 512:(th + 1) * 512],
                                           panelG[:, sl * 8192 + cj * 256 + itp * 128:
                                                  sl * 8192 + cj * 256 + (itp + 1) * 128],
                                           yT[:, cj * MY + th * 512:cj * MY + (th + 1) * 512],
                                           start=(cj == 0), stop=(NO_Z and cj == NDMT - 1))
                    if not NO_Z:
                        for th in range(2):
                            ins = t.matmul(ps_g[:, th * 512:(th + 1) * 512],
                                           zs_sl[0:32, sl * 512 + itp * 128:sl * 512 + (itp + 1) * 128],
                                           ygT_sb[0:32, th * 512:(th + 1) * 512],
                                           start=False, stop=True)
                    ins.then_inc(s["gstp"].h, 1)
                    s["gstp"].inc()
                    if it > 0:
                        t.wait_ge(s["mul"].h, it)
                    for cj in range(NDMT):
                        for th in range(2):
                            ins = t.matmul(ps_u[:, th * 512:(th + 1) * 512],
                                           panelU[:, sl * 8192 + cj * 256 + itp * 128:
                                                  sl * 8192 + cj * 256 + (itp + 1) * 128],
                                           yT[:, cj * MY + th * 512:cj * MY + (th + 1) * 512],
                                           start=(cj == 0), stop=(NO_Z and cj == NDMT - 1))
                    if not NO_Z:
                        for th in range(2):
                            ins = t.matmul(ps_u[:, th * 512:(th + 1) * 512],
                                           zs_sl[0:32, sl * 512 + 256 + itp * 128:
                                                 sl * 512 + 256 + (itp + 1) * 128],
                                           ygT_sb[0:32, th * 512:(th + 1) * 512],
                                           start=False, stop=True)
                    ins.then_inc(s["ustp"].h, 1)
                    s["ustp"].inc()
                    if it > 0 and not NO_AUX:
                        t.wait_ge(s["mul"].h, it)
                        if it > 1:
                            t.wait_ge(s["gev"].h, it - 1)
                        psl = (it - 1) % 2
                        if NO_GST:
                            ins = t.matmul(ps_gst[:, 0:8],
                                           hT_st[:, psl * 1024:psl * 1024 + 128],
                                           hT_st[:, psl * 1024:psl * 1024 + 8],
                                           start=True, stop=True)
                        else:
                            for tt in range(8):
                                ins = t.matmul(ps_gst[:, tt:tt + 1],
                                               hT_st[:, psl * 1024 + tt * 128:psl * 1024 + (tt + 1) * 128],
                                               C["ones_col"], start=True, stop=True)
                        ins.then_inc(s["gstm"].h, 1)
                        s["gstm"].inc()
            if NO_AUX:
                return
            t.wait_ge(s["mul"].h, NIT_R)
            t.wait_ge(s["gev"].h, NIT_R - 1)
            psl = (NIT_R - 1) % 2
            if NO_GST:
                ins = t.matmul(ps_gst[:, 0:8],
                               hT_st[:, psl * 1024:psl * 1024 + 128],
                               hT_st[:, psl * 1024:psl * 1024 + 8],
                               start=True, stop=True)
            else:
                for tt in range(8):
                    ins = t.matmul(ps_gst[:, tt:tt + 1],
                                   hT_st[:, psl * 1024 + tt * 128:psl * 1024 + (tt + 1) * 128],
                                   C["ones_col"], start=True, stop=True)
            ins.then_inc(s["gstm"].h, 1)
            s["gstm"].inc()
            t.wait_ge(s["gse"].h, 1)
            if not NO_TR:
                gstv = gst_sb[:, :].rearrange("p (i t) -> p i t", t=8)
                for tt in range(8):
                    t.transpose(ps_gtr[0:86, tt * 128:(tt + 1) * 128],
                                gstv[:, :, tt],
                                C["ident_f32"]).then_inc(s["gtr"].h, 1)
                    s["gtr"].inc()


def _p6_down(nc, io, C, s, gs_sb, o2_sb):
    for k in ["ddq", "dmm", "dtm", "dev"]:
        s[k] = Sem(nc, "s_" + k)
    s["dqw"] = [Sem(nc, f"s_dqw{i}") for i in range(2)]
    s["dsd"] = [Sem(nc, f"s_dsd{i}") for i in range(2)]
    s["dht"] = [Sem(nc, f"s_dht{i}") for i in range(2)]
    s["dzs"] = [Sem(nc, f"s_dzs{i}") for i in range(2)]

    NCI = 11
    rows_of = lambda ci: 96 if ci == 10 else 128
    htv = io["hT_dram"].ap().rearrange("(q e) t -> e q t", e=8)

    with (
        nc.sbuf_tensor("dqw_sl", [128, 2 * 512], I32) as dqw_sl,
        nc.sbuf_tensor("dsd_sl", [128, 2 * 512], F32) as dsd_sl,
        nc.sbuf_tensor("dsd_bf", [128, 2 * 512], BF16) as dsd_bf,
        nc.sbuf_tensor("dht_sl", [128, 2 * 1024], BF16) as dht_sl,
        nc.sbuf_tensor("dzs_sl", [86, 2 * 512], F32R) as dzs_sl,
        nc.sbuf_tensor("wd_sb", [128, 2 * 512], BF16) as wd_sb,
        nc.sbuf_tensor("dqt_d", [128, 512], I32) as dqt_d,
        nc.psum_tensor("ps_o2", [128, 8 * 512], F32) as ps_o2,
        nc.Block("p6") as block,
    ):
        def nidx(hc, ci, j):
            return hc * 88 + ci * 8 + j

        @block.sync
        def _(sy):
            sy.wait_ge(s["ustp"].h, N_IT)
            for sl in range(2):
                sy.wait_ge(s["hst"][sl].h, 16 * 43)
            for hc in range(8):
                sl2 = hc % 2
                if hc >= 2:
                    sy.wait_ge(s["dmm"].h, hc - 1)
                sy.dma_start(out=dzs_sl[0:86, sl2 * 512:(sl2 + 1) * 512],
                             in_=io["zs_d_dram"].ap()[0:86, hc * 512:(hc + 1) * 512]
                             ).then_inc(s["dzs"][sl2].h, 16)
                s["dzs"][sl2].inc(16)
                for ci in range(NCI):
                    rows = rows_of(ci)
                    nq = hc * NCI + ci
                    slq = nq % 2
                    if nq >= 2:
                        sy.wait_ge(s["ddq"].h, 17 * (nq - 1))
                    sy.dma_start(out=dqw_sl[0:rows, slq * 512:(slq + 1) * 512],
                                 in_=io["d_qw"][128 * ci:128 * ci + rows,
                                                hc * 512:(hc + 1) * 512]
                                 ).then_inc(s["dqw"][slq].h, 16)
                    s["dqw"][slq].inc(16)
                    grows = 6 if ci == 10 else 8
                    srcs_ = io["d_sc"][8 * ci:8 * ci + grows, hc * 512:(hc + 1) * 512]
                    srcs_ = srcs_.unsqueeze(1).broadcast_to([grows, 16, 512])
                    sy.dma_start(out=dsd_sl[0:rows, slq * 512:(slq + 1) * 512],
                                 in_=srcs_).then_inc(s["dsd"][slq].h, 16)
                    s["dsd"][slq].inc(16)
                    for j in range(8):
                        n = nidx(hc, ci, j)
                        sl = n % 2
                        if n >= 2:
                            sy.wait_ge(s["dtm"].h, n - 1)
                        sy.dma_start(out=dht_sl[0:rows, sl * 1024:(sl + 1) * 1024],
                                     in_=htv[j, 128 * ci:128 * ci + rows, :]
                                     ).then_inc(s["dht"][sl].h, 16)
                        s["dht"][sl].inc(16)

        @block.gpsimd
        def _(g):
            pass

        @block.vector
        def _(v):
            v.wait_ge(s["ustp"].h, N_IT)
            for sl in range(2):
                v.wait_ge(s["hst"][sl].h, 16 * 43)
            for hc in range(8):
                for ci in range(NCI):
                    rows = rows_of(ci)
                    nq = hc * NCI + ci
                    slq = nq % 2
                    v.wait_ge(s["dqw"][slq].h, 16 * (nq // 2 + 1))
                    v.wait_ge(s["dsd"][slq].h, 16 * (nq // 2 + 1))
                    v.tensor_copy(dsd_bf[0:rows, slq * 512:(slq + 1) * 512],
                                  dsd_sl[0:rows, slq * 512:(slq + 1) * 512]).then_inc(s["ddq"].h, 1)
                    s["ddq"].inc()
                    for j in range(8):
                        n = nidx(hc, ci, j)
                        sl = n % 2
                        if n >= 2:
                            v.wait_ge(s["dtm"].h, n - 1)
                        v.wait_ge(s["ddq"].h, s["ddq"].n)
                        v.tensor_scalar(dqt_d[0:rows, :], dqw_sl[0:rows, slq * 512:
                                        (slq + 1) * 512], 4 * j, 15,
                                        ALU.logical_shift_right,
                                        ALU.bitwise_and).then_inc(s["ddq"].h, 1)
                        s["ddq"].inc()
                        v.wait_ge(s["ddq"].h, s["ddq"].n)
                        v.scalar_tensor_tensor(wd_sb[0:rows, sl * 512:(sl + 1) * 512],
                                               dqt_d[0:rows, :], -8.0,
                                               dsd_bf[0:rows, slq * 512:(slq + 1) * 512],
                                               ALU.add, ALU.mult).then_inc(s["ddq"].h, 1)
                        s["ddq"].inc()
                v.wait_ge(s["dmm"].h, hc + 1)
                for tt in range(8):
                    ins = v.tensor_copy(o2_sb[:, tt * D + hc * 512:tt * D + (hc + 1) * 512],
                                        ps_o2[:, tt * 512:(tt + 1) * 512])
                ins.then_inc(s["dev"].h, 1)
                s["dev"].inc()

        @block.tensor
        def _(t):
            t.wait_ge(s["gse"].h, 2)
            for hc in range(8):
                sl2 = hc % 2
                t.wait_ge(s["dzs"][sl2].h, 16 * (hc // 2 + 1))
                if hc >= 1:
                    t.wait_ge(s["dev"].h, hc)
                for ci in range(NCI):
                    rows = rows_of(ci)
                    for j in range(8):
                        n = nidx(hc, ci, j)
                        sl = n % 2
                        t.wait_ge(s["ddq"].h, 17 * (hc * NCI + ci) + 2 * j + 3)
                        t.wait_ge(s["dht"][sl].h, 16 * (n // 2 + 1))
                        for tt in range(8):
                            ins = t.matmul(ps_o2[:, tt * 512:(tt + 1) * 512],
                                           dht_sl[0:rows, sl * 1024 + tt * 128:sl * 1024 + (tt + 1) * 128],
                                           wd_sb[0:rows, sl * 512:(sl + 1) * 512],
                                           start=(ci == 0 and j == 0), stop=False)
                        ins.then_inc(s["dtm"].h, 1)
                        s["dtm"].inc()
                for tt in range(8):
                    ins = t.matmul(ps_o2[:, tt * 512:(tt + 1) * 512],
                                   gs_sb[0:86, tt * 128:(tt + 1) * 128],
                                   dzs_sl[0:86, sl2 * 512:(sl2 + 1) * 512],
                                   start=False, stop=True)
                ins.then_inc(s["dmm"].h, 1)
                s["dmm"].inc()


def _p7_final(nc, io, C, s, o2_sb):
    for k in ["f_v", "f_a", "f_yl", "f_st"]:
        s[k] = Sem(nc, "s_" + k)
    NV = 9

    with (
        nc.sbuf_tensor("y2_tile", [128, D], BF16) as y2_tile,
        nc.sbuf_tensor("z2_sb", [128, D], F32) as z2_sb,
        nc.sbuf_tensor("sq2_sb", [128, D], F32) as sq2_sb,
        nc.sbuf_tensor("out_t", [128, D], F32) as out_t,
        nc.sbuf_tensor("ln2_sc", [128, 8], F32) as ln2_sc,
        nc.Block("p7") as block,
    ):
        S1, S2, M, M2, VE, STD, RSTD, NEGMR = range(8)

        @block.sync
        def _(sy):
            sy.wait_ge(s["dmm"].h, 8)
            for tt in range(8):
                if tt > 0:
                    sy.wait_ge(s["f_v"].h, (tt - 1) * NV + 1)
                sy.dma_start(out=y2_tile[:, :],
                             in_=io["y_dram"].ap()[tt * 128:(tt + 1) * 128, :]
                             ).then_inc(s["f_yl"].h, 16)
                s["f_yl"].inc(16)
                sy.wait_ge(s["f_v"].h, tt * NV + 9)
                sy.dma_start(out=io["out_my"][tt * 128:(tt + 1) * 128, :],
                             in_=out_t[:, :]).then_inc(s["f_st"].h, 16)
                s["f_st"].inc(16)
            sy.wait_ge(s["f_st"].h, 16 * 8)

        @block.vector
        def _(v):
            v.wait_ge(s["dev"].h, 8)
            for tt in range(8):
                base = tt * NV
                abase = tt * 2
                v.wait_ge(s["f_yl"].h, 16 * (tt + 1))
                v.tensor_tensor(z2_sb[:, :], y2_tile[:, :],
                                o2_sb[:, tt * D:(tt + 1) * D], ALU.add).then_inc(s["f_v"].h, 1)
                v.wait_ge(s["f_v"].h, base + 1)
                v.reduce_sum(ln2_sc[:, S1:S1 + 1], z2_sb[:, :], axis=AX.X).then_inc(s["f_v"].h, 1)
                v.wait_ge(s["f_a"].h, abase + 1)
                v.reduce_sum(ln2_sc[:, S2:S2 + 1], sq2_sb[:, :], axis=AX.X).then_inc(s["f_v"].h, 1)
                v.wait_ge(s["f_v"].h, base + 2)
                v.tensor_scalar(ln2_sc[:, M:M + 1], ln2_sc[:, S1:S1 + 1],
                                1.0 / D, None, ALU.mult).then_inc(s["f_v"].h, 1)
                v.wait_ge(s["f_v"].h, base + 4)
                v.tensor_scalar(ln2_sc[:, M2:M2 + 1], ln2_sc[:, M:M + 1],
                                ln2_sc[:, M:M + 1], -1e-5,
                                ALU.mult, ALU.add).then_inc(s["f_v"].h, 1)
                v.wait_ge(s["f_v"].h, base + 5)
                v.scalar_tensor_tensor(ln2_sc[:, VE:VE + 1], ln2_sc[:, S2:S2 + 1],
                                       1.0 / D, ln2_sc[:, M2:M2 + 1],
                                       ALU.mult, ALU.subtract).then_inc(s["f_v"].h, 1)
                v.wait_ge(s["f_a"].h, abase + 2)
                v.reciprocal(ln2_sc[:, RSTD:RSTD + 1],
                             ln2_sc[:, STD:STD + 1]).then_inc(s["f_v"].h, 1)
                v.wait_ge(s["f_v"].h, base + 7)
                v.scalar_tensor_tensor(ln2_sc[:, NEGMR:NEGMR + 1], ln2_sc[:, M:M + 1],
                                       -1.0, ln2_sc[:, RSTD:RSTD + 1],
                                       ALU.mult, ALU.mult).then_inc(s["f_v"].h, 1)
                v.wait_ge(s["f_v"].h, base + 8)
                if tt > 0:
                    v.wait_ge(s["f_st"].h, 16 * tt)
                v.tensor_scalar(out_t[:, :], z2_sb[:, :],
                                ln2_sc[:, RSTD:RSTD + 1], ln2_sc[:, NEGMR:NEGMR + 1],
                                ALU.mult, ALU.add).then_inc(s["f_v"].h, 1)

        @block.scalar
        def _(sc):
            for tt in range(8):
                base = tt * NV
                sc.wait_ge(s["f_v"].h, base + 1)
                sc.activation(sq2_sb[:, :], z2_sb[:, :], AF.Square).then_inc(s["f_a"].h, 1)
                sc.wait_ge(s["f_v"].h, base + 6)
                sc.activation(ln2_sc[:, STD:STD + 1], ln2_sc[:, VE:VE + 1],
                              AF.Sqrt).then_inc(s["f_a"].h, 1)
                s["f_a"].inc(2)


def _shard_inputs(inputs):
    x = np.ascontiguousarray(np.asarray(inputs["L0_attn_x"]).reshape(T, D))
    wq = np.asarray(inputs["L0_attn_wq"])
    wk = np.asarray(inputs["L0_attn_wk"])
    wv = np.asarray(inputs["L0_attn_wv"])
    maps = []
    for c in range(NC):
        m = {
            "x_my": np.ascontiguousarray(x[c * MY:(c + 1) * MY]),
            "wq_my": np.ascontiguousarray(wq[:, c * HPC * HD:(c + 1) * HPC * HD]),
            "wk_my": np.ascontiguousarray(wk[:, c * HD:(c + 1) * HD]),
            "wv_my": np.ascontiguousarray(wv[:, c * HD:(c + 1) * HD]),
            "wo": np.asarray(inputs["L0_attn_wo"]),
            "g_qw": np.asarray(inputs["L0_ffn_gate_qweight"]),
            "g_sc": np.asarray(inputs["L0_ffn_gate_scales"]),
            "g_qz": np.asarray(inputs["L0_ffn_gate_qzeros"]),
            "u_qw": np.asarray(inputs["L0_ffn_up_qweight"]),
            "u_sc": np.asarray(inputs["L0_ffn_up_scales"]),
            "u_qz": np.asarray(inputs["L0_ffn_up_qzeros"]),
            "d_qw": np.asarray(inputs["L0_ffn_down_qweight"]),
            "d_sc": np.asarray(inputs["L0_ffn_down_scales"]),
            "d_qz": np.asarray(inputs["L0_ffn_down_qzeros"]),
        }
        gidx = np.zeros((128, 32), dtype=np.int32)
        for chunk in range(2):
            for t in range(16):
                p = np.arange(128)
                gidx[:, chunk * 16 + t] = (t * 128 + p) * NC + c
        m["gidx"] = gidx
        maps.append(m)
    return maps


def kernel(**inputs):
    nc, _ = build_program(dbg=False)
    maps = _shard_inputs(inputs)
    res = run_bass_kernel_spmd(nc, maps, core_ids=list(range(NC)))
    out = np.concatenate([r["out_my"] for r in res.results], axis=0)
    return out.reshape(B, S, D)


# version 07
# speedup vs baseline: 1.0604x; 1.0604x over previous
import numpy as np
import ml_dtypes
from contextlib import ExitStack
import concourse.bass as bass
import concourse.mybir as mybir
from concourse.bass import IndirectOffsetOnAxis
from concourse.bass_utils import run_bass_kernel_spmd

F32 = mybir.dt.float32
F32R = mybir.dt.float32r
BF16 = mybir.dt.bfloat16
I32 = mybir.dt.int32
AF = mybir.ActivationFunctionType
ALU = mybir.AluOpType
AX = mybir.AxisListType

NC = 8
B, S, D = 2, 4096, 4096
T = B * S
MY = T // NC
HD = 128
HPC = 4
INTER = 11008
SCALE = HD ** -0.5
NDMT = D // 128
N_IT = INTER // 128
bf16 = ml_dtypes.bfloat16
NO_Z = False
NO_GST = False
P5_LIMIT = None
NO_AUX = False
NO_TR = False


class Sem:
    def __init__(self, nc, name):
        self.h = nc.alloc_semaphore(name)
        self.n = 0

    def inc(self, k=1):
        self.n += k
        return self.n


def build_program(dbg=False):
    nc = bass.Bass("TRN2", target_bir_lowering=False)

    io = {}
    def inp(name, shape, dt):
        io[name] = nc.dram_tensor(name, shape, dt, kind="ExternalInput").ap()
    inp("x_my", [MY, D], F32)
    inp("wq_my", [D, HPC * HD], F32)
    inp("wk_my", [D, HD], F32)
    inp("wv_my", [D, HD], F32)
    inp("wo", [D, D], F32)
    inp("g_qw", [D // 8, INTER], I32)
    inp("g_sc", [D // 128, INTER], F32)
    inp("g_qz", [D // 128, INTER // 8], I32)
    inp("u_qw", [D // 8, INTER], I32)
    inp("u_sc", [D // 128, INTER], F32)
    inp("u_qz", [D // 128, INTER // 8], I32)
    inp("d_qw", [INTER // 8, D], I32)
    inp("d_sc", [INTER // 128, D], F32)
    inp("d_qz", [INTER // 128, D // 8], I32)
    inp("gidx", [128, 32], I32)
    out_my = nc.dram_tensor("out_my", [MY, D], F32, kind="ExternalOutput").ap()
    io["out_my"] = out_my

    xt_contrib = nc.dram_tensor("xt_contrib", [D, MY], BF16)
    xt_full = nc.dram_tensor("xt_full", [NC * D, MY], BF16)
    at_contrib = [nc.dram_tensor(f"at_contrib{i}", [2 * HD, T], BF16) for i in range(2)]
    at_full = [nc.dram_tensor(f"at_full{i}", [NC * 2 * HD, T], BF16) for i in range(2)]
    o_dram = nc.dram_tensor("o_dram", [MY, D], BF16)
    zs_gu_dram = nc.dram_tensor("zs_gu_dram", [32, 2 * INTER], F32R)
    zs_d_dram = nc.dram_tensor("zs_d_dram", [INTER // 128, D], F32R)
    y_dram = nc.dram_tensor("y_dram", [MY, D], BF16)
    hT_dram = nc.dram_tensor("hT_dram", [INTER, MY], BF16)
    io.update(xt_contrib=xt_contrib, xt_full=xt_full, at_contrib=at_contrib,
              at_full=at_full, o_dram=o_dram, y_dram=y_dram, hT_dram=hT_dram,
              zs_gu_dram=zs_gu_dram, zs_d_dram=zs_d_dram)

    C = {}
    C["ident_bf"] = nc.inline_tensor(np.eye(128, dtype=bf16), "ident_bf").ap()
    C["ident_f32"] = nc.inline_tensor(np.eye(128, dtype=np.float32), "ident_f32").ap()
    C["ones_col"] = nc.inline_tensor(np.ones((128, 1), dtype=bf16), "ones_col").ap()
    C["ones_row_f32"] = nc.inline_tensor(np.ones((1, 128), dtype=np.float32), "ones_row_f32").ap()
    m = np.zeros((128, 4 * 512), dtype=bf16)
    for o in range(4):
        kk = np.arange(128)[:, None] + o * 128
        qq = np.arange(512)[None, :]
        m[:, o * 512:(o + 1) * 512] = (kk <= qq).astype(bf16)
    C["masks"] = nc.inline_tensor(m, "masks").ap()
    e = np.zeros((128, 4 * 32), dtype=bf16)
    for c4 in range(4):
        for p in range(128):
            e[p, c4 * 32 + 8 * c4 + p // 16] = 1
    C["e_cmats"] = nc.inline_tensor(e, "e_cmats").ap()

    dbg_out = {}
    if dbg == "qkv":
        for name, shape, dt in [
            ("dbg_qT", [128, HPC * T], BF16),
            ("dbg_kT", [128, T], BF16),
            ("dbg_v", [128, T], BF16),
        ]:
            dbg_out[name] = nc.dram_tensor(name, shape, dt, kind="ExternalOutput").ap()
    elif dbg == "p2":
        dbg_out["dbg_at0"] = nc.dram_tensor("dbg_at0", [NC * 2 * HD, T], BF16,
                                            kind="ExternalOutput").ap()
        dbg_out["dbg_at1"] = nc.dram_tensor("dbg_at1", [NC * 2 * HD, T], BF16,
                                            kind="ExternalOutput").ap()
    elif dbg == "p5a":
        dbg_out["dbg_zs"] = nc.dram_tensor("dbg_zs", [32, 2 * INTER], F32R, kind="ExternalOutput").ap()
        dbg_out["dbg_zsd"] = nc.dram_tensor("dbg_zsd", [INTER // 128, D], F32R, kind="ExternalOutput").ap()
    elif dbg == "p5":
        dbg_out["dbg_hT"] = nc.dram_tensor("dbg_hT", [INTER, MY], BF16, kind="ExternalOutput").ap()
        dbg_out["dbg_gs"] = nc.dram_tensor("dbg_gs", [86, MY], F32R, kind="ExternalOutput").ap()
        dbg_out["dbg_y2"] = nc.dram_tensor("dbg_y2", [MY, D], BF16, kind="ExternalOutput").ap()
        dbg_out["dbg_zs"] = nc.dram_tensor("dbg_zs", [32, 2 * INTER], F32R, kind="ExternalOutput").ap()
        dbg_out["dbg_zsd"] = nc.dram_tensor("dbg_zsd", [INTER // 128, D], F32R, kind="ExternalOutput").ap()
    elif dbg == "p4":
        dbg_out["dbg_at0"] = nc.dram_tensor("dbg_at0", [NC * 2 * HD, T], BF16,
                                            kind="ExternalOutput").ap()
        dbg_out["dbg_at1"] = nc.dram_tensor("dbg_at1", [NC * 2 * HD, T], BF16,
                                            kind="ExternalOutput").ap()
        dbg_out["dbg_o"] = nc.dram_tensor("dbg_o", [MY, D], BF16, kind="ExternalOutput").ap()
        dbg_out["dbg_y"] = nc.dram_tensor("dbg_y", [MY, D], BF16, kind="ExternalOutput").ap()
        dbg_out["dbg_yg"] = nc.dram_tensor("dbg_yg", [32, MY], F32R, kind="ExternalOutput").ap()

    s = {k: Sem(nc, "s_" + k) for k in
         ["tp", "tpe", "ag", "w", "mmq", "mmkv", "evq", "evkv", "const"]}
    s["x"] = [Sem(nc, f"s_x{i}") for i in range(2)]
    s["ct"] = [Sem(nc, f"s_ct{i}") for i in range(4)]
    s["xt"] = [Sem(nc, f"s_xt{i}") for i in range(3)]

    with (
        nc.sbuf_tensor("cst", [128, 3456], BF16) as cst,
        nc.sbuf_tensor("cst32", [128, 256], F32) as cst32,
    ):
        SC = {
            "ident_bf": cst[:, 0:128],
            "ones_col": cst[:, 128:129],
            "masks": cst[:, 256:256 + 2048],
            "e_cmats": cst[:, 2304:2304 + 128],
            "ident_f32": cst32[:, 0:128],
            "ones_row_f32": cst32[0:1, 128:256],
        }
        with nc.Block("pconst") as block:
            @block.sync
            def _(sy):
                sy.dma_start(out=SC["ident_bf"], in_=C["ident_bf"]).then_inc(s["const"].h, 16)
                sy.dma_start(out=SC["ones_col"], in_=C["ones_col"]).then_inc(s["const"].h, 16)
                sy.dma_start(out=SC["masks"], in_=C["masks"]).then_inc(s["const"].h, 16)
                sy.dma_start(out=SC["e_cmats"], in_=C["e_cmats"]).then_inc(s["const"].h, 16)
                sy.dma_start(out=SC["ident_f32"], in_=C["ident_f32"]).then_inc(s["const"].h, 16)
                sy.dma_start(out=SC["ones_row_f32"], in_=C["ones_row_f32"]).then_inc(s["const"].h, 16)
                s["const"].inc(96)
        C = dict(C)
        C.update(SC)
        _build_main(nc, io, C, s, dbg, dbg_out)
    return nc, io


def _build_main(nc, io, C, s, dbg, dbg_out):
    with (
        nc.sbuf_tensor("qT", [128, HPC * T], BF16) as qT,
        nc.sbuf_tensor("kT", [128, T], BF16) as kT,
        nc.sbuf_tensor("v_sb", [128, T], BF16) as v_sb,
    ):
        with (
            nc.sbuf_tensor("wq_sb", [128, NDMT * HPC * HD], BF16) as wq_sb,
            nc.sbuf_tensor("wkv_sb", [128, NDMT * 2 * HD], BF16) as wkv_sb,
        ):
            _p0_transpose_gather(nc, io, C, s, wq_sb, wkv_sb)
            _p1_qkv(nc, io, C, s, qT, kT, v_sb, wq_sb, wkv_sb)
        _p2_attention(nc, io, C, s, qT, kT, v_sb)
        if dbg == "qkv":
            _p_dbg_qkv(nc, s, dbg_out, qT, kT, v_sb)
            return
    if dbg == "p2":
        _p_dbg_p2(nc, io, s, dbg_out)
        return
    with (
        nc.sbuf_tensor("yT", [128, NDMT * MY], BF16) as yT,
        nc.sbuf_tensor("ygT_sb", [32, MY], F32R) as ygT_sb,
    ):
        with nc.sbuf_tensor("at_my", [128, NDMT * MY], BF16) as at_my:
            _p3_oproj(nc, io, C, s, at_my)
        _p4_postln1(nc, io, C, s, yT, ygT_sb)
        if dbg == "p4":
            _p_dbg_p4(nc, io, s, dbg_out, ygT_sb)
            return
        with nc.sbuf_tensor("gs_sb", [86, 1024], F32R) as gs_sb:
            _p5pre_zeros(nc, io, C, s)
            if dbg == "p5a":
                _p_dbg_p5a(nc, io, s, dbg_out)
                return
            _p5_gateup(nc, io, C, s, yT, ygT_sb, gs_sb)
            if dbg == "p5":
                _p_dbg_p5(nc, io, s, dbg_out, gs_sb)
                return
            with nc.sbuf_tensor("o2_sb", [128, 8 * D], BF16) as o2_sb:
                _p6_down(nc, io, C, s, gs_sb, o2_sb)
                _p7_final(nc, io, C, s, o2_sb)


def _p_dbg_p5a(nc, io, s, dbg_out):
    sd = Sem(nc, "s_dbg")
    with nc.Block("pdbg5a") as block:
        @block.sync
        def _(sy):
            sy.wait_ge(s["zst"].h, 16 * 9)
            sy.wait_ge(s["yst"].h, 16 * 8)
            sy.dma_start(out=dbg_out["dbg_zs"], in_=io["zs_gu_dram"].ap()).then_inc(sd.h, 16)
            sy.dma_start(out=dbg_out["dbg_zsd"], in_=io["zs_d_dram"].ap()).then_inc(sd.h, 16)
            sy.wait_ge(sd.h, 32)


def _p_dbg_p5(nc, io, s, dbg_out, gs_sb):
    sd = Sem(nc, "s_dbg")
    with nc.Block("pdbg5") as block:
        @block.sync
        def _(sy):
            for sl in range(2):
                sy.wait_ge(s["hst"][sl].h, s["hst"][sl].n)
            sy.wait_ge(s["gse"].h, s["gse"].n)
            sy.wait_ge(s["yst"].h, 16 * 8)
            sy.dma_start(out=dbg_out["dbg_hT"], in_=io["hT_dram"].ap()).then_inc(sd.h, 16)
            sy.dma_start(out=dbg_out["dbg_gs"], in_=gs_sb[0:86, :]).then_inc(sd.h, 16)
            sy.dma_start(out=dbg_out["dbg_y2"], in_=io["y_dram"].ap()).then_inc(sd.h, 16)
            sy.dma_start(out=dbg_out["dbg_zs"], in_=io["zs_gu_dram"].ap()).then_inc(sd.h, 16)
            sy.dma_start(out=dbg_out["dbg_zsd"], in_=io["zs_d_dram"].ap()).then_inc(sd.h, 16)
            sy.wait_ge(sd.h, 80)


def _p_dbg_p4(nc, io, s, dbg_out, ygT_sb):
    sd = Sem(nc, "s_dbg")
    with nc.Block("pdbg4") as block:
        @block.vector
        def _(v):
            v.wait_ge(s["yge"].h, 2)
            v.engine_nop().then_inc(sd.h, 1)

        @block.sync
        def _(sy):
            sy.wait_ge(s["ag2"].h, 2)
            sy.wait_ge(s["yst"].h, 16 * 8)
            sy.wait_ge(sd.h, 1)
            sy.dma_start(out=dbg_out["dbg_at0"], in_=io["at_full"][0].ap()).then_inc(sd.h, 16)
            sy.dma_start(out=dbg_out["dbg_at1"], in_=io["at_full"][1].ap()).then_inc(sd.h, 16)
            sy.dma_start(out=dbg_out["dbg_o"], in_=io["o_dram"].ap()).then_inc(sd.h, 16)
            sy.dma_start(out=dbg_out["dbg_y"], in_=io["y_dram"].ap()).then_inc(sd.h, 16)
            sy.dma_start(out=dbg_out["dbg_yg"], in_=ygT_sb[:, :]).then_inc(sd.h, 16)
            sy.wait_ge(sd.h, 81)


def _p0_transpose_gather(nc, io, C, s, wq_sb, wkv_sb):
    with (
        nc.sbuf_tensor("x_my_bf", [128, 2 * D], BF16) as x_my_bf,
        nc.sbuf_tensor("tp_st", [128, 4 * 512], BF16) as tp_st,
        nc.psum_tensor("ps_tpA", [128, 1024], BF16) as ps_tpA,
        nc.psum_tensor("ps_tpB", [128, 1024], BF16) as ps_tpB,
        nc.Block("p0") as block,
    ):
        @block.gpsimd
        def _(g):
            for tt in range(8):
                if tt >= 2:
                    g.wait_ge(s["tp"].h, (tt - 1) * 8)
                sx = s["x"][tt % 2]
                g.dma_start(out=x_my_bf[:, (tt % 2) * D:(tt % 2 + 1) * D],
                            in_=io["x_my"][tt * 128:(tt + 1) * 128, :]).then_inc(sx.h, 16)
                sx.inc(16)
            g.dma_start(out=wq_sb[:, :].rearrange("p (d h) -> p d h", d=NDMT),
                        in_=io["wq_my"].rearrange("(d p) h -> p d h", p=128)).then_inc(s["w"].h, 16)
            g.dma_start(out=wkv_sb[:, 0:NDMT * HD].rearrange("p (d h) -> p d h", d=NDMT),
                        in_=io["wk_my"].rearrange("(d p) h -> p d h", p=128)).then_inc(s["w"].h, 16)
            g.dma_start(out=wkv_sb[:, NDMT * HD:2 * NDMT * HD].rearrange("p (d h) -> p d h", d=NDMT),
                        in_=io["wv_my"].rearrange("(d p) h -> p d h", p=128)).then_inc(s["w"].h, 16)
            s["w"].inc(48)
            for m4 in range(4):
                g.wait_ge(s["ct"][m4].h, 16 * 16)
            g.collective_compute(
                "AllGather", ALU.bypass,
                replica_groups=[list(range(NC))],
                ins=[io["xt_contrib"].ap().opt()],
                outs=[io["xt_full"].ap().opt()],
            ).then_inc(s["ag"].h, 1)
            s["ag"].inc()

        @block.tensor
        def _(t):
            t.wait_ge(s["const"].h, 96)
            n = 0
            for tt in range(8):
                t.wait_ge(s["x"][tt % 2].h, 16 * (tt // 2 + 1))
                for d4 in range(8):
                    pp = n % 2
                    ps_tp = ps_tpA if pp == 0 else ps_tpB
                    if n >= 2:
                        t.wait_ge(s["tpe"].h, n - 1)
                    for k in range(4):
                        d = d4 * 4 + k
                        ins = t.transpose(ps_tp[:, k * 128:(k + 1) * 128],
                                          x_my_bf[:, (tt % 2) * D + d * 128: (tt % 2) * D + (d + 1) * 128],
                                          C["ident_bf"])
                        if k == 3:
                            ins.then_inc(s["tp"].h, 1)
                            s["tp"].inc()
                    n += 1

        @block.vector
        def _(v):
            for g8 in range(64):
                pp = g8 % 2
                m4 = g8 % 4
                ps_tp = ps_tpA if pp == 0 else ps_tpB
                v.wait_ge(s["tp"].h, g8 + 1)
                if g8 >= 4:
                    v.wait_ge(s["ct"][m4].h, 16 * (g8 // 4))
                v.tensor_copy(tp_st[:, m4 * 512:(m4 + 1) * 512],
                              ps_tp[:, 0:512]).then_inc(s["tpe"].h, 1)
                s["tpe"].inc()

        @block.sync
        def _(sy):
            n = 0
            for tt in range(8):
                for d4 in range(8):
                    m4 = n % 4
                    n += 1
                    sy.wait_ge(s["tpe"].h, n)
                    sct = s["ct"][m4]
                    sy.dma_start(
                        out=io["xt_contrib"].ap().rearrange("(d p) t -> p d t", p=128)
                            [:, d4 * 4:(d4 + 1) * 4, tt * 128:(tt + 1) * 128],
                        in_=tp_st[:, m4 * 512:(m4 + 1) * 512]
                            .rearrange("p (d t) -> p d t", d=4),
                    ).then_inc(sct.h, 16)
                    sct.inc(16)


def _p1_qkv(nc, io, C, s, qT, kT, v_sb, wq_sb, wkv_sb):
    with (
        nc.sbuf_tensor("xt_buf", [128, 3 * 16 * 512], BF16) as xt_buf,
        nc.psum_tensor("ps_q", [128, 4 * 512], F32) as ps_q,
        nc.psum_tensor("ps_kv0", [128, 2 * 512], F32) as ps_kv0,
        nc.psum_tensor("ps_kv1", [128, 2 * 512], F32) as ps_kv1,
        nc.Block("p1") as block,
    ):
        xtv = io["xt_full"].ap().rearrange("(r d p) t -> r d p t", d=NDMT, p=128)

        @block.sync
        def _(sy):
            sy.wait_ge(s["ag"].h, 1)
            nb = 0
            for ch in range(16):
                r, half = ch // 2, ch % 2
                for hh in range(2):
                    b = nb % 3
                    if nb >= 3:
                        sy.wait_ge(s["mmkv"].h, (nb - 3) // 2 + 1)
                    sxt = s["xt"][b]
                    sy.dma_start(
                        out=xt_buf[:, b * 8192:(b + 1) * 8192]
                            .rearrange("p (d t) -> p d t", d=16),
                        in_=xtv[r, hh * 16:(hh + 1) * 16, :, half * 512:(half + 1) * 512]
                            .transpose([1, 0, 2]),
                    ).then_inc(sxt.h, 16)
                    sxt.inc(16)
                    nb += 1

        @block.tensor
        def _(t):
            t.wait_ge(s["w"].h, 48)
            nb = 0
            for ch in range(16):
                for hh in range(2):
                    b = nb % 3
                    nb += 1
                    t.wait_ge(s["xt"][b].h, 16 * ((nb - 1) // 3 + 1))
                    if hh == 0 and ch > 0:
                        t.wait_ge(s["evq"].h, ch)
                    for d16 in range(16):
                        d = hh * 16 + d16
                        xs = xt_buf[:, b * 8192 + d16 * 512:b * 8192 + (d16 + 1) * 512]
                        for h in range(HPC):
                            ins = t.matmul(ps_q[:, h * 512:(h + 1) * 512],
                                           wq_sb[:, (d * HPC + h) * HD:(d * HPC + h + 1) * HD],
                                           xs, start=(d == 0), stop=(d == NDMT - 1))
                            if d == NDMT - 1 and h == HPC - 1:
                                ins.then_inc(s["mmq"].h, 1)
                                s["mmq"].inc()
                ps_kv = ps_kv0 if ch % 2 == 0 else ps_kv1
                if ch >= 2:
                    t.wait_ge(s["evkv"].h, ch - 1)
                for hh in range(2):
                    b = (nb - 2 + hh) % 3
                    for d16 in range(16):
                        d = hh * 16 + d16
                        xs = xt_buf[:, b * 8192 + d16 * 512:b * 8192 + (d16 + 1) * 512]
                        t.matmul(ps_kv[:, 0:512],
                                 wkv_sb[:, d * HD:(d + 1) * HD],
                                 xs, start=(d == 0), stop=(d == NDMT - 1))
                        for q4 in range(4):
                            ins = t.matmul(ps_kv[:, 512 + q4 * 128:512 + (q4 + 1) * 128],
                                           xs[:, q4 * 128:(q4 + 1) * 128],
                                           wkv_sb[:, NDMT * HD + d * HD:NDMT * HD + (d + 1) * HD],
                                           start=(d == 0 and q4 == 0), stop=(d == NDMT - 1 and q4 == 3))
                            if d == NDMT - 1 and q4 == 3:
                                ins.then_inc(s["mmkv"].h, 1)
                                s["mmkv"].inc()

        @block.vector
        def _(v):
            for ch in range(16):
                v.wait_ge(s["mmq"].h, ch + 1)
                for h in range(HPC):
                    ins = v.tensor_scalar(qT[:, h * T + ch * 512:h * T + (ch + 1) * 512],
                                          ps_q[:, h * 512:(h + 1) * 512],
                                          SCALE, None, ALU.mult)
                    if h == HPC - 1:
                        ins.then_inc(s["evq"].h, 1)
                        s["evq"].inc()

        @block.scalar
        def _(sc):
            for ch in range(16):
                ps_kv = ps_kv0 if ch % 2 == 0 else ps_kv1
                sc.wait_ge(s["mmkv"].h, ch + 1)
                sc.copy(kT[:, ch * 512:(ch + 1) * 512], ps_kv[:, 0:512])
                sc.copy(v_sb[:, ch * 512:(ch + 1) * 512],
                        ps_kv[:, 512:1024]).then_inc(s["evkv"].h, 1)
                s["evkv"].inc()


def _p2_attention(nc, io, C, s, qT, kT, v_sb):
    for k in ["sc", "exp", "msk", "pav", "lav", "rec", "bca", "lrbe", "avv"]:
        s[k] = Sem(nc, "s_" + k)
    s["avst"] = [Sem(nc, f"s_avst{i}") for i in range(2)]
    s["ag2"] = Sem(nc, "s_ag2")

    groups = [(h, b, g) for h in range(HPC) for b in range(B) for g in range(8)]
    npairs = [2 * (g + 1) for (_, _, g) in groups]
    ppre = [0]
    for np_ in npairs:
        ppre.append(ppre[-1] + np_)

    with (
        nc.sbuf_tensor("probs", [128, 2 * 1024], BF16) as probs,
        nc.sbuf_tensor("lrec", [1, 512], F32) as lrec,
        nc.sbuf_tensor("lrb_sb", [128, 512], F32) as lrb_sb,
        nc.sbuf_tensor("av_sb", [128, 2 * 512], BF16) as av_sb,
        nc.psum_tensor("ps_sc0", [128, 1024], F32) as ps_sc0,
        nc.psum_tensor("ps_sc1", [128, 1024], F32) as ps_sc1,
        nc.psum_tensor("ps_av", [128, 512], F32) as ps_av,
        nc.psum_tensor("ps_l", [1, 512], F32) as ps_l,
        nc.psum_tensor("ps_lrb", [128, 512], F32) as ps_lrb,
        nc.Block("p2") as block,
    ):
        @block.tensor
        def _(t):
            def emit_qk(gi, kp):
                h, b, g = groups[gi]
                P = ppre[gi] + kp
                qs = qT[:, h * T + b * S + g * 512: h * T + b * S + (g + 1) * 512]
                ps = ps_sc0 if P % 2 == 0 else ps_sc1
                if P >= 2:
                    t.wait_ge(s["exp"].h, P - 1)
                for j in range(2):
                    kt = kp * 2 + j
                    ins = t.matmul(ps[:, j * 512:(j + 1) * 512],
                                   kT[:, b * S + kt * 128:b * S + (kt + 1) * 128],
                                   qs, start=True, stop=True)
                ins.then_inc(s["sc"].h, 1)
                s["sc"].inc()

            def emit_lav(gi, kp):
                h, b, g = groups[gi]
                npair = npairs[gi]
                nkt = 4 * (g + 1)
                P = ppre[gi] + kp
                d = kp - (npair - 2)
                if kp == 0:
                    t.wait_ge(s["rec"].h, gi)
                    t.wait_ge(s["avv"].h, gi)
                t.wait_ge(s["exp"].h, P + 1)
                if d >= 0:
                    t.wait_ge(s["msk"].h, 4 * gi + 2 * (d + 1))
                pb = probs[:, (P % 2) * 1024:(P % 2 + 1) * 1024]
                for j in range(2):
                    kt = kp * 2 + j
                    pc = pb[:, j * 512:(j + 1) * 512]
                    t.matmul(ps_l[0:1, :], C["ones_col"], pc,
                             start=(kt == 0), stop=(kt == nkt - 1))
                    ins = t.matmul(ps_av[:, :],
                                   v_sb[:, (b * 32 + kt) * 128:(b * 32 + kt + 1) * 128],
                                   pc, start=(kt == 0), stop=(kt == nkt - 1))
                ins.then_inc(s["pav"].h, 1)
                s["pav"].inc()

            for gi, (h, b, g) in enumerate(groups):
                npair = npairs[gi]
                for kp in range(npair):
                    emit_qk(gi, kp)
                    if kp >= 2:
                        emit_lav(gi, kp - 2)
                emit_lav(gi, npair - 2)
                emit_lav(gi, npair - 1)
                t.wait_ge(s["rec"].h, gi + 1)
                t.wait_ge(s["lrbe"].h, gi)
                t.matmul(ps_lrb[:, :], C["ones_row_f32"], lrec[0:1, :],
                         start=True, stop=True).then_inc(s["bca"].h, 1)
                s["bca"].inc()

        @block.scalar
        def _(sc):
            for gi, (h, b, g) in enumerate(groups):
                npair = npairs[gi]
                for kp in range(npair):
                    P = ppre[gi] + kp
                    ps = ps_sc0 if P % 2 == 0 else ps_sc1
                    sc.wait_ge(s["sc"].h, P + 1)
                    if P >= 2:
                        sc.wait_ge(s["pav"].h, P - 1)
                    sc.activation(probs[:, (P % 2) * 1024:(P % 2 + 1) * 1024],
                                  ps[:, :], AF.Exp).then_inc(s["exp"].h, 1)
                    s["exp"].inc()

        @block.vector
        def _(v):
            for gi, (h, b, g) in enumerate(groups):
                npair = npairs[gi]
                for d in range(2):
                    kp = npair - 2 + d
                    P = ppre[gi] + kp
                    v.wait_ge(s["exp"].h, P + 1)
                    pb = probs[:, (P % 2) * 1024:(P % 2 + 1) * 1024]
                    for j in range(2):
                        o = 2 * d + j
                        pc = pb[:, j * 512:(j + 1) * 512]
                        v.tensor_tensor(pc, pc, C["masks"][:, o * 512:(o + 1) * 512],
                                        ALU.mult).then_inc(s["msk"].h, 1)
                        s["msk"].inc()
                v.wait_ge(s["pav"].h, ppre[gi + 1])
                v.reciprocal(lrec[0:1, :], ps_l[0:1, :]).then_inc(s["rec"].h, 1)
                s["rec"].inc()
                v.wait_ge(s["bca"].h, gi + 1)
                v.tensor_copy(lrb_sb[:, :], ps_lrb[:, :]).then_inc(s["lrbe"].h, 1)
                s["lrbe"].inc()
                slot = gi % 2
                if gi >= 2:
                    v.wait_ge(s["avst"][slot].h, 16 * (gi // 2))
                v.wait_ge(s["lrbe"].h, gi + 1)
                v.tensor_tensor(av_sb[:, slot * 512:(slot + 1) * 512],
                                ps_av[:, :], lrb_sb[:, :], ALU.mult).then_inc(s["avv"].h, 1)
                s["avv"].inc()

        @block.sync
        def _(sy):
            for gi, (h, b, g) in enumerate(groups):
                slot = gi % 2
                sy.wait_ge(s["avv"].h, gi + 1)
                sy.dma_start(
                    out=io["at_contrib"][h // 2].ap()
                        [(h % 2) * 128:(h % 2) * 128 + 128,
                         b * S + g * 512:b * S + (g + 1) * 512],
                    in_=av_sb[:, slot * 512:(slot + 1) * 512],
                ).then_inc(s["avst"][slot].h, 16)
                s["avst"][slot].inc(16)

        @block.gpsimd
        def _(gp):
            for chunk in range(2):
                gp.wait_ge(s["avst"][0].h, 16 * 16 * (chunk + 1))
                gp.wait_ge(s["avst"][1].h, 16 * 16 * (chunk + 1))
                gp.collective_compute(
                    "AllGather", ALU.bypass,
                    replica_groups=[list(range(NC))],
                    ins=[io["at_contrib"][chunk].ap().opt()],
                    outs=[io["at_full"][chunk].ap().opt()],
                ).then_inc(s["ag2"].h, 1)
                s["ag2"].inc()


def _p_dbg_p2(nc, io, s, dbg_out):
    sd = Sem(nc, "s_dbg")
    with nc.Block("pdbg2") as block:
        @block.sync
        def _(sy):
            sy.wait_ge(s["ag2"].h, 2)
            sy.dma_start(out=dbg_out["dbg_at0"], in_=io["at_full"][0].ap()).then_inc(sd.h, 16)
            sy.dma_start(out=dbg_out["dbg_at1"], in_=io["at_full"][1].ap()).then_inc(sd.h, 16)
            sy.wait_ge(sd.h, 32)


def _p_dbg_qkv(nc, s, dbg_out, qT, kT, v_sb):
    sd = Sem(nc, "s_dbg")
    with nc.Block("pdbg") as block:
        @block.sync
        def _(sy):
            sy.wait_ge(s["evq"].h, 16)
            sy.wait_ge(s["evkv"].h, 16)
            sy.dma_start(out=dbg_out["dbg_qT"], in_=qT[:, :]).then_inc(sd.h, 16)
            sy.dma_start(out=dbg_out["dbg_kT"], in_=kT[:, :]).then_inc(sd.h, 16)
            sy.dma_start(out=dbg_out["dbg_v"], in_=v_sb[:, :]).then_inc(sd.h, 16)
            sy.wait_ge(sd.h, 48)


def _p3_oproj(nc, io, C, s, at_my):
    for k in ["gth", "omm", "oev", "gix"]:
        s[k] = Sem(nc, "s_" + k)
    s["wo"] = [Sem(nc, f"s_wo{i}") for i in range(2)]
    s["ost"] = [Sem(nc, f"s_ost{i}") for i in range(2)]

    with (
        nc.sbuf_tensor("gix_sb", [128, 32], I32) as gix_sb,
        nc.sbuf_tensor("wo_sl", [128, 2 * 32 * 256], BF16) as wo_sl,
        nc.sbuf_tensor("o_st", [128, 2 * 2048], BF16) as o_st,
        nc.psum_tensor("ps_oA", [128, 2048], F32) as ps_oA,
        nc.psum_tensor("ps_oB", [128, 2048], F32) as ps_oB,
        nc.Block("p3") as block,
    ):
        wov = io["wo"].rearrange("(d p) n -> p d n", p=128)

        @block.sync
        def _(sy):
            sy.dma_start(out=gix_sb[:, :], in_=io["gidx"]).then_inc(s["gix"].h, 16)
            s["gix"].inc(16)
            for hc in range(16):
                sl = hc % 2
                sy.wait_ge(s["oev"].h, hc + 1)
                sy.dma_start(
                    out=io["o_dram"].ap().rearrange("(t p) n -> p t n", p=128)
                        [:, :, hc * 256:(hc + 1) * 256],
                    in_=o_st[:, sl * 2048:(sl + 1) * 2048]
                        .rearrange("p (t n) -> p t n", t=8),
                ).then_inc(s["ost"][sl].h, 16)
                s["ost"][sl].inc(16)

        @block.gpsimd
        def _(g):
            g.wait_ge(s["gix"].h, 16)
            g.wait_ge(s["ag2"].h, 2)
            for chunk in range(2):
                src = io["at_full"][chunk].ap().rearrange("r (tb tk) -> (r tb) tk", tk=MY)
                for t in range(16):
                    head = 4 * (t // 2) + 2 * chunk + (t % 2)
                    i = chunk * 16 + t
                    g.indirect_dma_start(
                        at_my[:, head * MY:(head + 1) * MY], None,
                        src, IndirectOffsetOnAxis(ap=gix_sb[:, i:i + 1], axis=0),
                    ).then_inc(s["gth"].h, 16)
                    s["gth"].inc(16)
            for hc in range(16):
                sl = hc % 2
                if hc >= 2:
                    g.wait_ge(s["omm"].h, hc - 1)
                g.dma_start(
                    out=wo_sl[:, sl * 8192:(sl + 1) * 8192]
                        .rearrange("p (d n) -> p d n", d=NDMT),
                    in_=wov[:, :, hc * 256:(hc + 1) * 256],
                ).then_inc(s["wo"][sl].h, 16)
                s["wo"][sl].inc(16)

        @block.tensor
        def _(t):
            t.wait_ge(s["gth"].h, 16 * 32)
            for hc in range(16):
                sl = hc % 2
                ps_o = ps_oA if sl == 0 else ps_oB
                t.wait_ge(s["wo"][sl].h, 16 * (hc // 2 + 1))
                if hc >= 2:
                    t.wait_ge(s["oev"].h, hc - 1)
                for d in range(NDMT):
                    for tt in range(8):
                        ins = t.matmul(
                            ps_o[:, tt * 256:(tt + 1) * 256],
                            at_my[:, d * MY + tt * 128:d * MY + (tt + 1) * 128],
                            wo_sl[:, sl * 8192 + d * 256:sl * 8192 + (d + 1) * 256],
                            start=(d == 0 and tt % 2 == 0),
                            stop=(d == NDMT - 1 and tt % 2 == 1))
                ins.then_inc(s["omm"].h, 1)
                s["omm"].inc()

        @block.vector
        def _(v):
            for hc in range(16):
                sl = hc % 2
                ps_o = ps_oA if sl == 0 else ps_oB
                v.wait_ge(s["omm"].h, hc + 1)
                if hc >= 2:
                    v.wait_ge(s["ost"][sl].h, 16 * (hc // 2))
                ins = v.tensor_copy(o_st[:, sl * 2048:(sl + 1) * 2048], ps_o[:, :])
                ins.then_inc(s["oev"].h, 1)
                s["oev"].inc()


def _p4_postln1(nc, io, C, s, yT, ygT_sb):
    for k in ["lnv", "lna", "yt", "yte", "ygm", "yge", "xl", "ol", "yst"]:
        s[k] = Sem(nc, "s_" + k)
    NV, NA = 9, 2

    with (
        nc.sbuf_tensor("x_tile", [128, D], F32) as x_tile,
        nc.sbuf_tensor("o_tile", [128, D], BF16) as o_tile,
        nc.sbuf_tensor("z_sb", [128, D], F32) as z_sb,
        nc.sbuf_tensor("sq_sb", [128, D], F32) as sq_sb,
        nc.sbuf_tensor("y_tile", [128, D], BF16) as y_tile,
        nc.sbuf_tensor("ln_sc", [128, 8], F32) as ln_sc,
        nc.psum_tensor("ps_ytA", [128, 1024], BF16) as ps_ytA,
        nc.psum_tensor("ps_ytB", [128, 1024], BF16) as ps_ytB,
        nc.psum_tensor("ps_yg", [32, 1024], F32) as ps_yg,
        nc.Block("p4") as block,
    ):
        S1, S2, M, M2, VE, STD, RSTD, NEGMR = range(8)

        @block.sync
        def _(sy):
            for sl in range(2):
                sy.wait_ge(s["ost"][sl].h, 16 * 8)
            for tt in range(8):
                if tt > 0:
                    sy.wait_ge(s["lnv"].h, (tt - 1) * NV + 1)
                sy.dma_start(out=x_tile[:, :],
                             in_=io["x_my"][tt * 128:(tt + 1) * 128, :]).then_inc(s["xl"].h, 16)
                s["xl"].inc(16)
                sy.dma_start(out=o_tile[:, :],
                             in_=io["o_dram"].ap()[tt * 128:(tt + 1) * 128, :]).then_inc(s["ol"].h, 16)
                s["ol"].inc(16)
                sy.wait_ge(s["lnv"].h, tt * NV + 9)
                sy.dma_start(out=io["y_dram"].ap()[tt * 128:(tt + 1) * 128, :],
                             in_=y_tile[:, :]).then_inc(s["yst"].h, 16)
                s["yst"].inc(16)

        @block.vector
        def _(v):
            for tt in range(8):
                base = tt * NV
                abase = tt * NA
                v.wait_ge(s["xl"].h, 16 * (tt + 1))
                v.wait_ge(s["ol"].h, 16 * (tt + 1))

                v.tensor_tensor(z_sb[:, :], x_tile[:, :],
                                o_tile[:, :], ALU.add).then_inc(s["lnv"].h, 1)
                v.wait_ge(s["lnv"].h, base + 1)
                v.reduce_sum(ln_sc[:, S1:S1 + 1], z_sb[:, :], axis=AX.X).then_inc(s["lnv"].h, 1)
                v.wait_ge(s["lna"].h, abase + 1)
                v.reduce_sum(ln_sc[:, S2:S2 + 1], sq_sb[:, :], axis=AX.X).then_inc(s["lnv"].h, 1)
                v.wait_ge(s["lnv"].h, base + 2)
                v.tensor_scalar(ln_sc[:, M:M + 1], ln_sc[:, S1:S1 + 1],
                                1.0 / D, None, ALU.mult).then_inc(s["lnv"].h, 1)
                v.wait_ge(s["lnv"].h, base + 4)
                v.tensor_scalar(ln_sc[:, M2:M2 + 1], ln_sc[:, M:M + 1],
                                ln_sc[:, M:M + 1], -1e-5,
                                ALU.mult, ALU.add).then_inc(s["lnv"].h, 1)
                v.wait_ge(s["lnv"].h, base + 5)
                v.scalar_tensor_tensor(ln_sc[:, VE:VE + 1], ln_sc[:, S2:S2 + 1],
                                       1.0 / D, ln_sc[:, M2:M2 + 1],
                                       ALU.mult, ALU.subtract).then_inc(s["lnv"].h, 1)
                v.wait_ge(s["lna"].h, abase + 2)
                v.reciprocal(ln_sc[:, RSTD:RSTD + 1],
                             ln_sc[:, STD:STD + 1]).then_inc(s["lnv"].h, 1)
                v.wait_ge(s["lnv"].h, base + 7)
                v.scalar_tensor_tensor(ln_sc[:, NEGMR:NEGMR + 1], ln_sc[:, M:M + 1],
                                       -1.0, ln_sc[:, RSTD:RSTD + 1],
                                       ALU.mult, ALU.mult).then_inc(s["lnv"].h, 1)
                v.wait_ge(s["lnv"].h, base + 8)
                if tt > 0:
                    v.wait_ge(s["yst"].h, 16 * tt)
                    v.wait_ge(s["yt"].h, tt * 32)
                v.tensor_scalar(y_tile[:, :], z_sb[:, :],
                                ln_sc[:, RSTD:RSTD + 1], ln_sc[:, NEGMR:NEGMR + 1],
                                ALU.mult, ALU.add).then_inc(s["lnv"].h, 1)
                for cj in range(NDMT):
                    n = tt * NDMT + cj
                    pp = n % 2
                    ps_yt = ps_ytA if pp == 0 else ps_ytB
                    v.wait_ge(s["yt"].h, n + 1)
                    v.tensor_copy(yT[:, cj * MY + tt * 128:cj * MY + (tt + 1) * 128],
                                  ps_yt[:, 0:128]).then_inc(s["yte"].h, 1)
                    s["yte"].inc()
            for half in range(2):
                v.wait_ge(s["ygm"].h, half + 1)
                v.tensor_copy(ygT_sb[:, half * 512:(half + 1) * 512],
                              ps_yg[:, half * 512:(half + 1) * 512]).then_inc(s["yge"].h, 1)
                s["yge"].inc()

        @block.scalar
        def _(sc):
            for tt in range(8):
                base = tt * NV
                sc.wait_ge(s["lnv"].h, base + 1)
                sc.activation(sq_sb[:, :], z_sb[:, :], AF.Square).then_inc(s["lna"].h, 1)
                sc.wait_ge(s["lnv"].h, base + 6)
                sc.activation(ln_sc[:, STD:STD + 1], ln_sc[:, VE:VE + 1],
                              AF.Sqrt).then_inc(s["lna"].h, 1)
                s["lna"].inc(2)

        @block.tensor
        def _(t):
            n = 0
            for tt in range(8):
                t.wait_ge(s["lnv"].h, tt * NV + 9)
                for cj in range(NDMT):
                    c4, j = cj // 8, cj % 8
                    pp = n % 2
                    ps_yt = ps_ytA if pp == 0 else ps_ytB
                    if n >= 2:
                        t.wait_ge(s["yte"].h, n - 1)
                    src = y_tile[:, :].rearrange("p (a q e) -> p a q e", a=4, e=8)[:, c4, :, j]
                    ins = t.transpose(ps_yt[:, 0:128], src, C["ident_bf"])
                    ins.then_inc(s["yt"].h, 1)
                    s["yt"].inc()
                    n += 1
            t.wait_ge(s["yte"].h, 256)
            for half in range(2):
                for cj in range(NDMT):
                    c4 = cj // 8
                    ins = t.matmul(
                        ps_yg[:, half * 512:(half + 1) * 512],
                        C["e_cmats"][:, c4 * 32:(c4 + 1) * 32],
                        yT[:, cj * MY + half * 512:cj * MY + (half + 1) * 512],
                        start=(cj == 0), stop=(cj == NDMT - 1))
                ins.then_inc(s["ygm"].h, 1)
                s["ygm"].inc()



def _p5pre_zeros(nc, io, C, s):
    for k in ["zplq", "zpls", "zc", "zst"]:
        s[k] = Sem(nc, "s_" + k)

    CH = INTER // 4
    with (
        nc.sbuf_tensor("qz_sb", [86, 512], I32) as qz_sb,
        nc.sbuf_tensor("z4_sb", [86, 4096], I32) as z4_sb,
        nc.sbuf_tensor("tmp_sb", [86, 4096], F32) as tmp_sb,
        nc.sbuf_tensor("szs_sb", [86, 4096], BF16) as szs_sb,
        nc.sbuf_tensor("zs_out", [86, 4096], F32R) as zs_out,
        nc.Block("p5pre") as block,
    ):
        chunks = []
        for mat, (qz, sc) in enumerate([(io["g_qz"], io["g_sc"]), (io["u_qz"], io["u_sc"])]):
            for ci in range(4):
                chunks.append((qz, sc, io["zs_gu_dram"], mat * INTER + ci * CH,
                               32, CH, ci * (CH // 8), ci * CH))
        chunks.append((io["d_qz"], io["d_sc"], io["zs_d_dram"], 0, 86, D, 0, 0))

        @block.sync
        def _(sy):
            sy.wait_ge(s["yst"].h, 16 * 8)
            for i, (qz, sc, dst, col, rows, w, qzoff, scoff) in enumerate(chunks):
                if i >= 1:
                    sy.wait_ge(s["zc"].h, 10 * i)
                sy.dma_start(out=qz_sb[0:rows, 0:w // 8],
                             in_=qz[:, qzoff:qzoff + w // 8]).then_inc(s["zplq"].h, 16)
                s["zplq"].inc(16)

        @block.gpsimd
        def _(g):
            g.wait_ge(s["yst"].h, 16 * 8)

            def store(i):
                _, _, dst, col, rows, w, _, _ = chunks[i]
                g.wait_ge(s["zc"].h, 10 * (i + 1))
                g.dma_start(out=dst.ap()[0:rows, col:col + w],
                            in_=zs_out[0:rows, 0:w]).then_inc(s["zst"].h, 16)
                s["zst"].inc(16)

            for i, (qz, sc, dst, col, rows, w, qzoff, scoff) in enumerate(chunks):
                if i >= 1:
                    g.wait_ge(s["zc"].h, 10 * i)
                g.dma_start(out=szs_sb[0:rows, 0:w],
                            in_=sc[:, scoff:scoff + w]).then_inc(s["zpls"].h, 16)
                s["zpls"].inc(16)
                if i >= 1:
                    store(i - 1)
            store(len(chunks) - 1)

        @block.vector
        def _(v):
            v.wait_ge(s["yst"].h, 16 * 8)
            for i, (qz, sc, dst, col, rows, w, qzoff, scoff) in enumerate(chunks):
                v.wait_ge(s["zplq"].h, 16 * (i + 1))
                v.wait_ge(s["zpls"].h, 16 * (i + 1))
                v.wait_ge(s["zc"].h, 10 * i)
                if i >= 1:
                    v.wait_ge(s["zst"].h, 16 * i)
                z4v = z4_sb[0:rows, 0:w].rearrange("g (m e) -> g m e", e=8)
                for j in range(8):
                    ins = v.tensor_scalar(z4v[:, :, j], qz_sb[0:rows, 0:w // 8],
                                          4 * j, 15,
                                          ALU.logical_shift_right, ALU.bitwise_and)
                    ins.then_inc(s["zc"].h, 1)
                    s["zc"].inc()
                v.wait_ge(s["zc"].h, s["zc"].n)
                v.tensor_scalar(tmp_sb[0:rows, 0:w], z4_sb[0:rows, 0:w],
                                -7.0, -1.0, ALU.add, ALU.mult).then_inc(s["zc"].h, 1)
                s["zc"].inc()
                v.wait_ge(s["zc"].h, s["zc"].n)
                v.tensor_tensor(zs_out[0:rows, 0:w], tmp_sb[0:rows, 0:w],
                                szs_sb[0:rows, 0:w], ALU.mult).then_inc(s["zc"].h, 1)
                s["zc"].inc()


def _p5_gateup(nc, io, C, s, yT, ygT_sb, gs_sb):
    for k in ["dqa", "dqb", "gstp", "ustp", "silu", "mul", "gstm", "gtr", "gse", "zsl", "gev"]:
        s[k] = Sem(nc, "s_" + k)
    s["qw"] = [Sem(nc, f"s_qw{i}") for i in range(2)]
    s["stl"] = [Sem(nc, f"s_stl{i}") for i in range(2)]
    s["hst"] = [Sem(nc, f"s_hst{i}") for i in range(2)]
    s["zsl2"] = [Sem(nc, f"s_zsl{i}") for i in range(2)]

    NPN = INTER // 256
    NPN_R = P5_LIMIT if P5_LIMIT else NPN
    NIT_R = 2 * NPN_R
    trunc = NPN_R != NPN

    with (
        nc.sbuf_tensor("qw_sl", [128, 2 * 8 * 256], I32) as qw_sl,
        nc.sbuf_tensor("st_sl", [128, 2 * 8 * 256], F32) as st_sl,
        nc.sbuf_tensor("st_bf", [128, 2 * 8 * 256], BF16) as st_bf,
        nc.sbuf_tensor("panelG", [128, 2 * 8192], BF16) as panelG,
        nc.sbuf_tensor("panelU", [128, 2 * 8192], BF16) as panelU,
        nc.sbuf_tensor("zs_sl", [32, 2 * 512], F32R) as zs_sl,
        nc.sbuf_tensor("silu_sb", [128, 2 * 1024], BF16) as silu_sb,
        nc.sbuf_tensor("hT_st", [128, 2 * 1024], BF16) as hT_st,
        nc.sbuf_tensor("gst_sb", [128, 8 * 86], F32) as gst_sb,
        nc.sbuf_tensor("dqt", [128, 8 * 256], I32) as dqt,
        nc.psum_tensor("ps_g", [128, 1024], F32) as ps_g,
        nc.psum_tensor("ps_u", [128, 1024], F32) as ps_u,
        nc.psum_tensor("ps_gst", [128, 8], F32) as ps_gst,
        nc.psum_tensor("ps_gtr", [86, 1024], F32) as ps_gtr,
        nc.Block("p5") as block,
    ):
        qws = [io["g_qw"], io["u_qw"]]
        scs = [io["g_sc"], io["u_sc"]]
        panels = [panelG, panelU]

        @block.sync
        def _(sy):
            sy.wait_ge(s["yst"].h, 16 * 8)
            sy.wait_ge(s["zc"].h, 90)
            for pn in range(NPN_R):
                sl = pn % 2
                if pn >= 2:
                    sy.wait_ge(s["dqb"].h, 8 * (pn - 1))
                for mat in range(2):
                    for c4 in range(4):
                        sy.dma_start(
                            out=qw_sl[:, (sl * 8 + mat * 4 + c4) * 256:
                                      (sl * 8 + mat * 4 + c4 + 1) * 256],
                            in_=qws[mat][128 * c4:128 * (c4 + 1),
                                         pn * 256:(pn + 1) * 256],
                        ).then_inc(s["qw"][sl].h, 16)
                        s["qw"][sl].inc(16)
                        src = scs[mat][8 * c4:8 * (c4 + 1), pn * 256:(pn + 1) * 256]
                        src = src.unsqueeze(1).broadcast_to([8, 16, 256])
                        sy.dma_start(
                            out=st_sl[:, (sl * 8 + mat * 4 + c4) * 256:
                                      (sl * 8 + mat * 4 + c4 + 1) * 256],
                            in_=src,
                        ).then_inc(s["stl"][sl].h, 16)
                        s["stl"][sl].inc(16)
                if pn >= 2:
                    sy.wait_ge(s["ustp"].h, 2 * pn - 2)
                sy.dma_start(
                    out=zs_sl[:, sl * 512:sl * 512 + 256],
                    in_=io["zs_gu_dram"].ap()[0:32, pn * 256:(pn + 1) * 256],
                ).then_inc(s["zsl2"][sl].h, 16)
                sy.dma_start(
                    out=zs_sl[:, sl * 512 + 256:(sl + 1) * 512],
                    in_=io["zs_gu_dram"].ap()[0:32, INTER + pn * 256:INTER + (pn + 1) * 256],
                ).then_inc(s["zsl2"][sl].h, 16)
                s["zsl2"][sl].inc(32)

        @block.gpsimd
        def _(g):
            g.wait_ge(s["yst"].h, 16 * 8)
            g.wait_ge(s["zc"].h, 90)

            def hstore(it):
                sl = it % 2
                g.wait_ge(s["mul"].h, it + 1)
                g.dma_start(out=io["hT_dram"].ap()[it * 128:(it + 1) * 128, :],
                            in_=hT_st[:, sl * 1024:(sl + 1) * 1024]).then_inc(s["hst"][sl].h, 16)
                s["hst"][sl].inc(16)

            for it in range(NIT_R):
                hstore(it)

        @block.vector
        def _(v):
            v.wait_ge(s["yst"].h, 16 * 8)
            v.wait_ge(s["zst"].h, 16 * 9)
            def emit_dequant(pn):
                sl = pn % 2
                v.wait_ge(s["qw"][sl].h, 128 * (pn // 2 + 1))
                v.wait_ge(s["stl"][sl].h, 128 * (pn // 2 + 1))
                if pn >= 2:
                    v.wait_ge(s["gstp"].h, 2 * pn - 2)
                v.tensor_copy(st_bf[:, sl * 2048:(sl + 1) * 2048],
                              st_sl[:, sl * 2048:(sl + 1) * 2048]).then_inc(s["dqa"].h, 1)
                s["dqa"].inc()
                for r, (mat, c4) in enumerate([(m, c) for m in range(2) for c in range(4)]):
                    gr = pn * 8 + r
                    base = (sl * 8 + mat * 4 + c4) * 256
                    if gr >= 1:
                        v.wait_ge(s["dqb"].h, gr)
                    if pn >= 2 and mat == 1 and c4 == 0:
                        v.wait_ge(s["ustp"].h, 2 * pn - 2)
                    for j in range(8):
                        ins = v.tensor_scalar(dqt[:, j * 256:(j + 1) * 256],
                                              qw_sl[:, base:base + 256], 4 * j, 15,
                                              ALU.logical_shift_right, ALU.bitwise_and)
                    ins.then_inc(s["dqa"].h, 1)
                    s["dqa"].inc()
                    v.wait_ge(s["dqa"].h, s["dqa"].n)
                    pan = panels[mat]
                    for j in range(8):
                        cj = c4 * 8 + j
                        ins = v.scalar_tensor_tensor(
                            pan[:, sl * 8192 + cj * 256:sl * 8192 + (cj + 1) * 256],
                            dqt[:, j * 256:(j + 1) * 256], -8.0,
                            st_bf[:, base:base + 256], ALU.add, ALU.mult)
                    ins.then_inc(s["dqb"].h, 1)
                    s["dqb"].inc()

            def emit_gev(it):
                if NO_AUX:
                    return
                v.wait_ge(s["gstm"].h, it + 1)
                v.tensor_copy(gst_sb[:, it * 8:(it + 1) * 8], ps_gst[:, :]).then_inc(s["gev"].h, 1)
                s["gev"].inc()

            def emit_mul(it):
                sl = it % 2
                v.wait_ge(s["ustp"].h, it + 1)
                v.wait_ge(s["silu"].h, it + 1)
                if it >= 2:
                    v.wait_ge(s["hst"][sl].h, 16 * (it // 2))
                    if not NO_AUX:
                        v.wait_ge(s["gstm"].h, it - 1)
                v.tensor_tensor(hT_st[:, sl * 1024:sl * 1024 + 512],
                                ps_u[:, 0:512], silu_sb[:, sl * 1024:sl * 1024 + 512],
                                ALU.mult)
                v.tensor_tensor(hT_st[:, sl * 1024 + 512:(sl + 1) * 1024],
                                ps_u[:, 512:1024], silu_sb[:, sl * 1024 + 512:(sl + 1) * 1024],
                                ALU.mult).then_inc(s["mul"].h, 1)
                s["mul"].inc()

            for pn in range(NPN_R):
                emit_dequant(pn)
                if pn >= 2:
                    emit_gev(2 * (pn - 2))
                    emit_gev(2 * (pn - 2) + 1)
                if pn >= 1:
                    emit_mul(2 * (pn - 1))
                    emit_mul(2 * (pn - 1) + 1)
            if NPN_R >= 2:
                emit_gev(2 * NPN_R - 4)
                emit_gev(2 * NPN_R - 3)
            emit_mul(2 * (NPN_R - 1))
            emit_mul(2 * (NPN_R - 1) + 1)
            emit_gev(2 * NPN_R - 2)
            emit_gev(2 * NPN_R - 1)
            if not NO_AUX:
                v.wait_ge(s["gev"].h, NIT_R)
                v.engine_nop().then_inc(s["gse"].h, 1)
                s["gse"].inc()
                if not NO_TR:
                    v.wait_ge(s["gtr"].h, 8)
                    v.tensor_copy(gs_sb[0:86, :],
                                  ps_gtr[0:86, :]).then_inc(s["gse"].h, 1)
                    s["gse"].inc()

        @block.scalar
        def _(sc):
            for it in range(NIT_R):
                sl = it % 2
                sc.wait_ge(s["gstp"].h, it + 1)
                if it >= 2:
                    sc.wait_ge(s["mul"].h, it - 1)
                sc.activation(silu_sb[:, sl * 1024:(sl + 1) * 1024],
                              ps_g[:, :], AF.Silu).then_inc(s["silu"].h, 1)
                s["silu"].inc()

        @block.tensor
        def _(t):
            t.wait_ge(s["yge"].h, 2)
            t.wait_ge(s["zst"].h, 16 * 9)
            for pn in range(NPN_R):
                sl = pn % 2
                t.wait_ge(s["zsl2"][pn % 2].h, 32 * (pn // 2 + 1))
                for itp in range(2):
                    it = 2 * pn + itp
                    if it > 0:
                        t.wait_ge(s["silu"].h, it)
                    for cj in range(NDMT):
                        if itp == 0 and cj % 8 == 0:
                            t.wait_ge(s["dqb"].h, 8 * pn + cj // 8 + 1)
                        for th in range(2):
                            ins = t.matmul(ps_g[:, th * 512:(th + 1) * 512],
                                           panelG[:, sl * 8192 + cj * 256 + itp * 128:
                                                  sl * 8192 + cj * 256 + (itp + 1) * 128],
                                           yT[:, cj * MY + th * 512:cj * MY + (th + 1) * 512],
                                           start=(cj == 0), stop=(NO_Z and cj == NDMT - 1))
                    if not NO_Z:
                        for th in range(2):
                            ins = t.matmul(ps_g[:, th * 512:(th + 1) * 512],
                                           zs_sl[0:32, sl * 512 + itp * 128:sl * 512 + (itp + 1) * 128],
                                           ygT_sb[0:32, th * 512:(th + 1) * 512],
                                           start=False, stop=True)
                    ins.then_inc(s["gstp"].h, 1)
                    s["gstp"].inc()
                    if it > 0:
                        t.wait_ge(s["mul"].h, it)
                    for cj in range(NDMT):
                        if itp == 0 and cj % 8 == 0:
                            t.wait_ge(s["dqb"].h, 8 * pn + 4 + cj // 8 + 1)
                        for th in range(2):
                            ins = t.matmul(ps_u[:, th * 512:(th + 1) * 512],
                                           panelU[:, sl * 8192 + cj * 256 + itp * 128:
                                                  sl * 8192 + cj * 256 + (itp + 1) * 128],
                                           yT[:, cj * MY + th * 512:cj * MY + (th + 1) * 512],
                                           start=(cj == 0), stop=(NO_Z and cj == NDMT - 1))
                    if not NO_Z:
                        for th in range(2):
                            ins = t.matmul(ps_u[:, th * 512:(th + 1) * 512],
                                           zs_sl[0:32, sl * 512 + 256 + itp * 128:
                                                 sl * 512 + 256 + (itp + 1) * 128],
                                           ygT_sb[0:32, th * 512:(th + 1) * 512],
                                           start=False, stop=True)
                    ins.then_inc(s["ustp"].h, 1)
                    s["ustp"].inc()
                    if it > 0 and not NO_AUX:
                        t.wait_ge(s["mul"].h, it)
                        if it > 1:
                            t.wait_ge(s["gev"].h, it - 1)
                        psl = (it - 1) % 2
                        if NO_GST:
                            ins = t.matmul(ps_gst[:, 0:8],
                                           hT_st[:, psl * 1024:psl * 1024 + 128],
                                           hT_st[:, psl * 1024:psl * 1024 + 8],
                                           start=True, stop=True)
                        else:
                            for tt in range(8):
                                ins = t.matmul(ps_gst[:, tt:tt + 1],
                                               hT_st[:, psl * 1024 + tt * 128:psl * 1024 + (tt + 1) * 128],
                                               C["ones_col"], start=True, stop=True)
                        ins.then_inc(s["gstm"].h, 1)
                        s["gstm"].inc()
            if NO_AUX:
                return
            t.wait_ge(s["mul"].h, NIT_R)
            t.wait_ge(s["gev"].h, NIT_R - 1)
            psl = (NIT_R - 1) % 2
            if NO_GST:
                ins = t.matmul(ps_gst[:, 0:8],
                               hT_st[:, psl * 1024:psl * 1024 + 128],
                               hT_st[:, psl * 1024:psl * 1024 + 8],
                               start=True, stop=True)
            else:
                for tt in range(8):
                    ins = t.matmul(ps_gst[:, tt:tt + 1],
                                   hT_st[:, psl * 1024 + tt * 128:psl * 1024 + (tt + 1) * 128],
                                   C["ones_col"], start=True, stop=True)
            ins.then_inc(s["gstm"].h, 1)
            s["gstm"].inc()
            t.wait_ge(s["gse"].h, 1)
            if not NO_TR:
                gstv = gst_sb[:, :].rearrange("p (i t) -> p i t", t=8)
                for tt in range(8):
                    t.transpose(ps_gtr[0:86, tt * 128:(tt + 1) * 128],
                                gstv[:, :, tt],
                                C["ident_f32"]).then_inc(s["gtr"].h, 1)
                    s["gtr"].inc()


def _p6_down(nc, io, C, s, gs_sb, o2_sb):
    for k in ["ddq", "dmm", "dtm", "dev"]:
        s[k] = Sem(nc, "s_" + k)
    s["dqw"] = [Sem(nc, f"s_dqw{i}") for i in range(2)]
    s["dsd"] = [Sem(nc, f"s_dsd{i}") for i in range(2)]
    s["dht"] = [Sem(nc, f"s_dht{i}") for i in range(4)]
    s["dzs"] = [Sem(nc, f"s_dzs{i}") for i in range(2)]

    NCI = 11
    rows_of = lambda ci: 96 if ci == 10 else 128
    htv = io["hT_dram"].ap().rearrange("(q e) t -> e q t", e=8)

    with (
        nc.sbuf_tensor("dqw_sl", [128, 2 * 512], I32) as dqw_sl,
        nc.sbuf_tensor("dsd_sl", [128, 2 * 512], F32) as dsd_sl,
        nc.sbuf_tensor("dsd_bf", [128, 2 * 512], BF16) as dsd_bf,
        nc.sbuf_tensor("dht_sl", [128, 4 * 1024], BF16) as dht_sl,
        nc.sbuf_tensor("dzs_sl", [86, 2 * 512], F32R) as dzs_sl,
        nc.sbuf_tensor("wd_sb", [128, 4 * 512], BF16) as wd_sb,
        nc.sbuf_tensor("dqt_d", [128, 512], I32) as dqt_d,
        nc.psum_tensor("ps_o2", [128, 8 * 512], F32) as ps_o2,
        nc.Block("p6") as block,
    ):
        def nidx(hc, ci, j):
            return hc * 88 + ci * 8 + j

        @block.sync
        def _(sy):
            sy.wait_ge(s["ustp"].h, N_IT)
            for sl in range(2):
                sy.wait_ge(s["hst"][sl].h, 16 * 43)
            for hc in range(8):
                sl2 = hc % 2
                if hc >= 2:
                    sy.wait_ge(s["dmm"].h, hc - 1)
                sy.dma_start(out=dzs_sl[0:86, sl2 * 512:(sl2 + 1) * 512],
                             in_=io["zs_d_dram"].ap()[0:86, hc * 512:(hc + 1) * 512]
                             ).then_inc(s["dzs"][sl2].h, 16)
                s["dzs"][sl2].inc(16)
                for ci in range(NCI):
                    rows = rows_of(ci)
                    nq = hc * NCI + ci
                    slq = nq % 2
                    if nq >= 2:
                        sy.wait_ge(s["ddq"].h, 17 * (nq - 1))
                    sy.dma_start(out=dqw_sl[0:rows, slq * 512:(slq + 1) * 512],
                                 in_=io["d_qw"][128 * ci:128 * ci + rows,
                                                hc * 512:(hc + 1) * 512]
                                 ).then_inc(s["dqw"][slq].h, 16)
                    s["dqw"][slq].inc(16)
                    grows = 6 if ci == 10 else 8
                    srcs_ = io["d_sc"][8 * ci:8 * ci + grows, hc * 512:(hc + 1) * 512]
                    srcs_ = srcs_.unsqueeze(1).broadcast_to([grows, 16, 512])
                    sy.dma_start(out=dsd_sl[0:rows, slq * 512:(slq + 1) * 512],
                                 in_=srcs_).then_inc(s["dsd"][slq].h, 16)
                    s["dsd"][slq].inc(16)
                    for j in range(8):
                        n = nidx(hc, ci, j)
                        sl = n % 4
                        if n >= 4:
                            sy.wait_ge(s["dtm"].h, n - 3)
                        sy.dma_start(out=dht_sl[0:rows, sl * 1024:(sl + 1) * 1024],
                                     in_=htv[j, 128 * ci:128 * ci + rows, :]
                                     ).then_inc(s["dht"][sl].h, 16)
                        s["dht"][sl].inc(16)

        @block.gpsimd
        def _(g):
            pass

        @block.vector
        def _(v):
            v.wait_ge(s["ustp"].h, N_IT)
            for sl in range(2):
                v.wait_ge(s["hst"][sl].h, 16 * 43)
            for hc in range(8):
                for ci in range(NCI):
                    rows = rows_of(ci)
                    nq = hc * NCI + ci
                    slq = nq % 2
                    v.wait_ge(s["dqw"][slq].h, 16 * (nq // 2 + 1))
                    v.wait_ge(s["dsd"][slq].h, 16 * (nq // 2 + 1))
                    v.tensor_copy(dsd_bf[0:rows, slq * 512:(slq + 1) * 512],
                                  dsd_sl[0:rows, slq * 512:(slq + 1) * 512]).then_inc(s["ddq"].h, 1)
                    s["ddq"].inc()
                    for j in range(8):
                        n = nidx(hc, ci, j)
                        sl = n % 4
                        if n >= 4:
                            v.wait_ge(s["dtm"].h, n - 3)
                        v.wait_ge(s["ddq"].h, s["ddq"].n)
                        v.tensor_scalar(dqt_d[0:rows, :], dqw_sl[0:rows, slq * 512:
                                        (slq + 1) * 512], 4 * j, 15,
                                        ALU.logical_shift_right,
                                        ALU.bitwise_and).then_inc(s["ddq"].h, 1)
                        s["ddq"].inc()
                        v.wait_ge(s["ddq"].h, s["ddq"].n)
                        v.scalar_tensor_tensor(wd_sb[0:rows, sl * 512:(sl + 1) * 512],
                                               dqt_d[0:rows, :], -8.0,
                                               dsd_bf[0:rows, slq * 512:(slq + 1) * 512],
                                               ALU.add, ALU.mult).then_inc(s["ddq"].h, 1)
                        s["ddq"].inc()
                v.wait_ge(s["dmm"].h, hc + 1)
                for tt in range(8):
                    ins = v.tensor_copy(o2_sb[:, tt * D + hc * 512:tt * D + (hc + 1) * 512],
                                        ps_o2[:, tt * 512:(tt + 1) * 512])
                ins.then_inc(s["dev"].h, 1)
                s["dev"].inc()

        @block.tensor
        def _(t):
            t.wait_ge(s["gse"].h, 2)
            for hc in range(8):
                sl2 = hc % 2
                t.wait_ge(s["dzs"][sl2].h, 16 * (hc // 2 + 1))
                if hc >= 1:
                    t.wait_ge(s["dev"].h, hc)
                for ci in range(NCI):
                    rows = rows_of(ci)
                    for j in range(8):
                        n = nidx(hc, ci, j)
                        sl = n % 4
                        t.wait_ge(s["ddq"].h, 17 * (hc * NCI + ci) + 2 * j + 3)
                        t.wait_ge(s["dht"][sl].h, 16 * (n // 4 + 1))
                        for tt in range(8):
                            ins = t.matmul(ps_o2[:, tt * 512:(tt + 1) * 512],
                                           dht_sl[0:rows, sl * 1024 + tt * 128:sl * 1024 + (tt + 1) * 128],
                                           wd_sb[0:rows, sl * 512:(sl + 1) * 512],
                                           start=(ci == 0 and j == 0), stop=False)
                        ins.then_inc(s["dtm"].h, 1)
                        s["dtm"].inc()
                for tt in range(8):
                    ins = t.matmul(ps_o2[:, tt * 512:(tt + 1) * 512],
                                   gs_sb[0:86, tt * 128:(tt + 1) * 128],
                                   dzs_sl[0:86, sl2 * 512:(sl2 + 1) * 512],
                                   start=False, stop=True)
                ins.then_inc(s["dmm"].h, 1)
                s["dmm"].inc()


def _p7_final(nc, io, C, s, o2_sb):
    for k in ["f_v", "f_a", "f_yl", "f_st"]:
        s[k] = Sem(nc, "s_" + k)
    NV = 9

    with (
        nc.sbuf_tensor("y2_tile", [128, D], BF16) as y2_tile,
        nc.sbuf_tensor("z2_sb", [128, D], F32) as z2_sb,
        nc.sbuf_tensor("sq2_sb", [128, D], F32) as sq2_sb,
        nc.sbuf_tensor("out_t", [128, D], F32) as out_t,
        nc.sbuf_tensor("ln2_sc", [128, 8], F32) as ln2_sc,
        nc.Block("p7") as block,
    ):
        S1, S2, M, M2, VE, STD, RSTD, NEGMR = range(8)

        @block.sync
        def _(sy):
            sy.wait_ge(s["dmm"].h, 8)
            for tt in range(8):
                if tt > 0:
                    sy.wait_ge(s["f_v"].h, (tt - 1) * NV + 1)
                sy.dma_start(out=y2_tile[:, :],
                             in_=io["y_dram"].ap()[tt * 128:(tt + 1) * 128, :]
                             ).then_inc(s["f_yl"].h, 16)
                s["f_yl"].inc(16)
                sy.wait_ge(s["f_v"].h, tt * NV + 9)
                sy.dma_start(out=io["out_my"][tt * 128:(tt + 1) * 128, :],
                             in_=out_t[:, :]).then_inc(s["f_st"].h, 16)
                s["f_st"].inc(16)
            sy.wait_ge(s["f_st"].h, 16 * 8)

        @block.vector
        def _(v):
            v.wait_ge(s["dev"].h, 8)
            for tt in range(8):
                base = tt * NV
                abase = tt * 2
                v.wait_ge(s["f_yl"].h, 16 * (tt + 1))
                v.tensor_tensor(z2_sb[:, :], y2_tile[:, :],
                                o2_sb[:, tt * D:(tt + 1) * D], ALU.add).then_inc(s["f_v"].h, 1)
                v.wait_ge(s["f_v"].h, base + 1)
                v.reduce_sum(ln2_sc[:, S1:S1 + 1], z2_sb[:, :], axis=AX.X).then_inc(s["f_v"].h, 1)
                v.wait_ge(s["f_a"].h, abase + 1)
                v.reduce_sum(ln2_sc[:, S2:S2 + 1], sq2_sb[:, :], axis=AX.X).then_inc(s["f_v"].h, 1)
                v.wait_ge(s["f_v"].h, base + 2)
                v.tensor_scalar(ln2_sc[:, M:M + 1], ln2_sc[:, S1:S1 + 1],
                                1.0 / D, None, ALU.mult).then_inc(s["f_v"].h, 1)
                v.wait_ge(s["f_v"].h, base + 4)
                v.tensor_scalar(ln2_sc[:, M2:M2 + 1], ln2_sc[:, M:M + 1],
                                ln2_sc[:, M:M + 1], -1e-5,
                                ALU.mult, ALU.add).then_inc(s["f_v"].h, 1)
                v.wait_ge(s["f_v"].h, base + 5)
                v.scalar_tensor_tensor(ln2_sc[:, VE:VE + 1], ln2_sc[:, S2:S2 + 1],
                                       1.0 / D, ln2_sc[:, M2:M2 + 1],
                                       ALU.mult, ALU.subtract).then_inc(s["f_v"].h, 1)
                v.wait_ge(s["f_a"].h, abase + 2)
                v.reciprocal(ln2_sc[:, RSTD:RSTD + 1],
                             ln2_sc[:, STD:STD + 1]).then_inc(s["f_v"].h, 1)
                v.wait_ge(s["f_v"].h, base + 7)
                v.scalar_tensor_tensor(ln2_sc[:, NEGMR:NEGMR + 1], ln2_sc[:, M:M + 1],
                                       -1.0, ln2_sc[:, RSTD:RSTD + 1],
                                       ALU.mult, ALU.mult).then_inc(s["f_v"].h, 1)
                v.wait_ge(s["f_v"].h, base + 8)
                if tt > 0:
                    v.wait_ge(s["f_st"].h, 16 * tt)
                v.tensor_scalar(out_t[:, :], z2_sb[:, :],
                                ln2_sc[:, RSTD:RSTD + 1], ln2_sc[:, NEGMR:NEGMR + 1],
                                ALU.mult, ALU.add).then_inc(s["f_v"].h, 1)

        @block.scalar
        def _(sc):
            for tt in range(8):
                base = tt * NV
                sc.wait_ge(s["f_v"].h, base + 1)
                sc.activation(sq2_sb[:, :], z2_sb[:, :], AF.Square).then_inc(s["f_a"].h, 1)
                sc.wait_ge(s["f_v"].h, base + 6)
                sc.activation(ln2_sc[:, STD:STD + 1], ln2_sc[:, VE:VE + 1],
                              AF.Sqrt).then_inc(s["f_a"].h, 1)
                s["f_a"].inc(2)


def _shard_inputs(inputs):
    x = np.ascontiguousarray(np.asarray(inputs["L0_attn_x"]).reshape(T, D))
    wq = np.asarray(inputs["L0_attn_wq"])
    wk = np.asarray(inputs["L0_attn_wk"])
    wv = np.asarray(inputs["L0_attn_wv"])
    maps = []
    for c in range(NC):
        m = {
            "x_my": np.ascontiguousarray(x[c * MY:(c + 1) * MY]),
            "wq_my": np.ascontiguousarray(wq[:, c * HPC * HD:(c + 1) * HPC * HD]),
            "wk_my": np.ascontiguousarray(wk[:, c * HD:(c + 1) * HD]),
            "wv_my": np.ascontiguousarray(wv[:, c * HD:(c + 1) * HD]),
            "wo": np.asarray(inputs["L0_attn_wo"]),
            "g_qw": np.asarray(inputs["L0_ffn_gate_qweight"]),
            "g_sc": np.asarray(inputs["L0_ffn_gate_scales"]),
            "g_qz": np.asarray(inputs["L0_ffn_gate_qzeros"]),
            "u_qw": np.asarray(inputs["L0_ffn_up_qweight"]),
            "u_sc": np.asarray(inputs["L0_ffn_up_scales"]),
            "u_qz": np.asarray(inputs["L0_ffn_up_qzeros"]),
            "d_qw": np.asarray(inputs["L0_ffn_down_qweight"]),
            "d_sc": np.asarray(inputs["L0_ffn_down_scales"]),
            "d_qz": np.asarray(inputs["L0_ffn_down_qzeros"]),
        }
        gidx = np.zeros((128, 32), dtype=np.int32)
        for chunk in range(2):
            for t in range(16):
                p = np.arange(128)
                gidx[:, chunk * 16 + t] = (t * 128 + p) * NC + c
        m["gidx"] = gidx
        maps.append(m)
    return maps


def kernel(**inputs):
    nc, _ = build_program(dbg=False)
    maps = _shard_inputs(inputs)
    res = run_bass_kernel_spmd(nc, maps, core_ids=list(range(NC)))
    out = np.concatenate([r["out_my"] for r in res.results], axis=0)
    return out.reshape(B, S, D)


# version 09
# speedup vs baseline: 1.0788x; 1.0174x over previous
import numpy as np
import ml_dtypes
from contextlib import ExitStack
import concourse.bass as bass
import concourse.mybir as mybir
from concourse.bass import IndirectOffsetOnAxis
from concourse.bass_utils import run_bass_kernel_spmd

F32 = mybir.dt.float32
F32R = mybir.dt.float32r
BF16 = mybir.dt.bfloat16
I32 = mybir.dt.int32
AF = mybir.ActivationFunctionType
ALU = mybir.AluOpType
AX = mybir.AxisListType

NC = 8
B, S, D = 2, 4096, 4096
T = B * S
MY = T // NC
HD = 128
HPC = 4
INTER = 11008
SCALE = HD ** -0.5
NDMT = D // 128
N_IT = INTER // 128
bf16 = ml_dtypes.bfloat16
NO_Z = False
NO_GST = False
P5_LIMIT = None
NO_AUX = False
NO_TR = False


class Sem:
    def __init__(self, nc, name):
        self.h = nc.alloc_semaphore(name)
        self.n = 0

    def inc(self, k=1):
        self.n += k
        return self.n


def build_program(dbg=False):
    nc = bass.Bass("TRN2", target_bir_lowering=False)

    io = {}
    def inp(name, shape, dt):
        io[name] = nc.dram_tensor(name, shape, dt, kind="ExternalInput").ap()
    inp("x_my", [MY, D], F32)
    inp("wq_my", [D, HPC * HD], F32)
    inp("wk_my", [D, HD], F32)
    inp("wv_my", [D, HD], F32)
    inp("wo", [D, D], F32)
    inp("g_qw", [D // 8, INTER], I32)
    inp("g_sc", [D // 128, INTER], F32)
    inp("g_qz", [D // 128, INTER // 8], I32)
    inp("u_qw", [D // 8, INTER], I32)
    inp("u_sc", [D // 128, INTER], F32)
    inp("u_qz", [D // 128, INTER // 8], I32)
    inp("d_qw", [INTER // 8, D], I32)
    inp("d_sc", [INTER // 128, D], F32)
    inp("d_qz", [INTER // 128, D // 8], I32)
    inp("gidx", [128, 32], I32)
    out_my = nc.dram_tensor("out_my", [MY, D], F32, kind="ExternalOutput").ap()
    io["out_my"] = out_my

    xt_contrib = [nc.dram_tensor(f"xt_contrib{i}", [D, MY // 2], BF16) for i in range(2)]
    xt_full = [nc.dram_tensor(f"xt_full{i}", [NC * D, MY // 2], BF16) for i in range(2)]
    at_contrib = [nc.dram_tensor(f"at_contrib{i}", [2 * HD, T], BF16) for i in range(2)]
    at_full = [nc.dram_tensor(f"at_full{i}", [NC * 2 * HD, T], BF16) for i in range(2)]
    o_dram = nc.dram_tensor("o_dram", [MY, D], BF16)
    zs_gu_dram = nc.dram_tensor("zs_gu_dram", [32, 2 * INTER], F32R)
    zs_d_dram = nc.dram_tensor("zs_d_dram", [INTER // 128, D], F32R)
    y_dram = nc.dram_tensor("y_dram", [MY, D], BF16)
    hT_dram = nc.dram_tensor("hT_dram", [INTER, MY], BF16)
    io.update(xt_contrib=xt_contrib, xt_full=xt_full, at_contrib=at_contrib,
              at_full=at_full, o_dram=o_dram, y_dram=y_dram, hT_dram=hT_dram,
              zs_gu_dram=zs_gu_dram, zs_d_dram=zs_d_dram)

    C = {}
    C["ident_bf"] = nc.inline_tensor(np.eye(128, dtype=bf16), "ident_bf").ap()
    C["ident_f32"] = nc.inline_tensor(np.eye(128, dtype=np.float32), "ident_f32").ap()
    C["ones_col"] = nc.inline_tensor(np.ones((128, 1), dtype=bf16), "ones_col").ap()
    C["ones_row_f32"] = nc.inline_tensor(np.ones((1, 128), dtype=np.float32), "ones_row_f32").ap()
    m = np.zeros((128, 4 * 512), dtype=bf16)
    for o in range(4):
        kk = np.arange(128)[:, None] + o * 128
        qq = np.arange(512)[None, :]
        m[:, o * 512:(o + 1) * 512] = (kk <= qq).astype(bf16)
    C["masks"] = nc.inline_tensor(m, "masks").ap()
    e = np.zeros((128, 4 * 32), dtype=bf16)
    for c4 in range(4):
        for p in range(128):
            e[p, c4 * 32 + 8 * c4 + p // 16] = 1
    C["e_cmats"] = nc.inline_tensor(e, "e_cmats").ap()

    dbg_out = {}
    if dbg == "qkv":
        for name, shape, dt in [
            ("dbg_qT", [128, HPC * T], BF16),
            ("dbg_kT", [128, T], BF16),
            ("dbg_v", [128, T], BF16),
        ]:
            dbg_out[name] = nc.dram_tensor(name, shape, dt, kind="ExternalOutput").ap()
    elif dbg == "p2":
        dbg_out["dbg_at0"] = nc.dram_tensor("dbg_at0", [NC * 2 * HD, T], BF16,
                                            kind="ExternalOutput").ap()
        dbg_out["dbg_at1"] = nc.dram_tensor("dbg_at1", [NC * 2 * HD, T], BF16,
                                            kind="ExternalOutput").ap()
    elif dbg == "p5a":
        dbg_out["dbg_zs"] = nc.dram_tensor("dbg_zs", [32, 2 * INTER], F32R, kind="ExternalOutput").ap()
        dbg_out["dbg_zsd"] = nc.dram_tensor("dbg_zsd", [INTER // 128, D], F32R, kind="ExternalOutput").ap()
    elif dbg == "p5":
        dbg_out["dbg_hT"] = nc.dram_tensor("dbg_hT", [INTER, MY], BF16, kind="ExternalOutput").ap()
        dbg_out["dbg_gs"] = nc.dram_tensor("dbg_gs", [86, MY], F32R, kind="ExternalOutput").ap()
        dbg_out["dbg_y2"] = nc.dram_tensor("dbg_y2", [MY, D], BF16, kind="ExternalOutput").ap()
        dbg_out["dbg_zs"] = nc.dram_tensor("dbg_zs", [32, 2 * INTER], F32R, kind="ExternalOutput").ap()
        dbg_out["dbg_zsd"] = nc.dram_tensor("dbg_zsd", [INTER // 128, D], F32R, kind="ExternalOutput").ap()
    elif dbg == "p4":
        dbg_out["dbg_at0"] = nc.dram_tensor("dbg_at0", [NC * 2 * HD, T], BF16,
                                            kind="ExternalOutput").ap()
        dbg_out["dbg_at1"] = nc.dram_tensor("dbg_at1", [NC * 2 * HD, T], BF16,
                                            kind="ExternalOutput").ap()
        dbg_out["dbg_o"] = nc.dram_tensor("dbg_o", [MY, D], BF16, kind="ExternalOutput").ap()
        dbg_out["dbg_y"] = nc.dram_tensor("dbg_y", [MY, D], BF16, kind="ExternalOutput").ap()
        dbg_out["dbg_yg"] = nc.dram_tensor("dbg_yg", [32, MY], F32R, kind="ExternalOutput").ap()

    s = {k: Sem(nc, "s_" + k) for k in
         ["tp", "tpe", "ag", "w", "mmq", "mmkv", "evq", "evkv", "const"]}
    s["x"] = [Sem(nc, f"s_x{i}") for i in range(2)]
    s["ct"] = [Sem(nc, f"s_ct{i}") for i in range(4)]
    s["xt"] = [Sem(nc, f"s_xt{i}") for i in range(3)]

    with (
        nc.sbuf_tensor("cst", [128, 3456], BF16) as cst,
        nc.sbuf_tensor("cst32", [128, 256], F32) as cst32,
    ):
        SC = {
            "ident_bf": cst[:, 0:128],
            "ones_col": cst[:, 128:129],
            "masks": cst[:, 256:256 + 2048],
            "e_cmats": cst[:, 2304:2304 + 128],
            "ident_f32": cst32[:, 0:128],
            "ones_row_f32": cst32[0:1, 128:256],
        }
        with nc.Block("pconst") as block:
            @block.sync
            def _(sy):
                sy.dma_start(out=SC["ident_bf"], in_=C["ident_bf"]).then_inc(s["const"].h, 16)
                sy.dma_start(out=SC["ones_col"], in_=C["ones_col"]).then_inc(s["const"].h, 16)
                sy.dma_start(out=SC["masks"], in_=C["masks"]).then_inc(s["const"].h, 16)
                sy.dma_start(out=SC["e_cmats"], in_=C["e_cmats"]).then_inc(s["const"].h, 16)
                sy.dma_start(out=SC["ident_f32"], in_=C["ident_f32"]).then_inc(s["const"].h, 16)
                sy.dma_start(out=SC["ones_row_f32"], in_=C["ones_row_f32"]).then_inc(s["const"].h, 16)
                s["const"].inc(96)
        C = dict(C)
        C.update(SC)
        _p5pre_zeros(nc, io, C, s)
        _build_main(nc, io, C, s, dbg, dbg_out)
    return nc, io


def _build_main(nc, io, C, s, dbg, dbg_out):
    with (
        nc.sbuf_tensor("qT", [128, HPC * T], BF16) as qT,
        nc.sbuf_tensor("kT", [128, T], BF16) as kT,
        nc.sbuf_tensor("v_sb", [128, T], BF16) as v_sb,
    ):
        with (
            nc.sbuf_tensor("wq_sb", [128, NDMT * HPC * HD], BF16) as wq_sb,
            nc.sbuf_tensor("wkv_sb", [128, NDMT * 2 * HD], BF16) as wkv_sb,
        ):
            _p0_transpose_gather(nc, io, C, s, wq_sb, wkv_sb)
            _p1_qkv(nc, io, C, s, qT, kT, v_sb, wq_sb, wkv_sb)
        _p2_attention(nc, io, C, s, qT, kT, v_sb)
        if dbg == "qkv":
            _p_dbg_qkv(nc, s, dbg_out, qT, kT, v_sb)
            return
    if dbg == "p2":
        _p_dbg_p2(nc, io, s, dbg_out)
        return
    with (
        nc.sbuf_tensor("yT", [128, NDMT * MY], BF16) as yT,
        nc.sbuf_tensor("ygT_sb", [32, MY], F32R) as ygT_sb,
    ):
        with nc.sbuf_tensor("at_my", [128, NDMT * MY], BF16) as at_my:
            _p3_oproj(nc, io, C, s, at_my)
        _p4_postln1(nc, io, C, s, yT, ygT_sb)
        if dbg == "p4":
            _p_dbg_p4(nc, io, s, dbg_out, ygT_sb)
            return
        with nc.sbuf_tensor("gs_sb", [86, 1024], F32R) as gs_sb:
            if dbg == "p5a":
                _p_dbg_p5a(nc, io, s, dbg_out)
                return
            _p5_gateup(nc, io, C, s, yT, ygT_sb, gs_sb)
            if dbg == "p5":
                _p_dbg_p5(nc, io, s, dbg_out, gs_sb)
                return
            with nc.sbuf_tensor("o2_sb", [128, 8 * D], BF16) as o2_sb:
                _p6_down(nc, io, C, s, gs_sb, o2_sb)
                _p7_final(nc, io, C, s, o2_sb)


def _p_dbg_p5a(nc, io, s, dbg_out):
    sd = Sem(nc, "s_dbg")
    with nc.Block("pdbg5a") as block:
        @block.sync
        def _(sy):
            sy.wait_ge(s["zst"].h, 16 * 9)
            sy.wait_ge(s["yst"].h, 16 * 8)
            sy.dma_start(out=dbg_out["dbg_zs"], in_=io["zs_gu_dram"].ap()).then_inc(sd.h, 16)
            sy.dma_start(out=dbg_out["dbg_zsd"], in_=io["zs_d_dram"].ap()).then_inc(sd.h, 16)
            sy.wait_ge(sd.h, 32)


def _p_dbg_p5(nc, io, s, dbg_out, gs_sb):
    sd = Sem(nc, "s_dbg")
    with nc.Block("pdbg5") as block:
        @block.sync
        def _(sy):
            for sl in range(2):
                sy.wait_ge(s["hst"][sl].h, s["hst"][sl].n)
            sy.wait_ge(s["gse"].h, s["gse"].n)
            sy.wait_ge(s["yst"].h, 16 * 8)
            sy.dma_start(out=dbg_out["dbg_hT"], in_=io["hT_dram"].ap()).then_inc(sd.h, 16)
            sy.dma_start(out=dbg_out["dbg_gs"], in_=gs_sb[0:86, :]).then_inc(sd.h, 16)
            sy.dma_start(out=dbg_out["dbg_y2"], in_=io["y_dram"].ap()).then_inc(sd.h, 16)
            sy.dma_start(out=dbg_out["dbg_zs"], in_=io["zs_gu_dram"].ap()).then_inc(sd.h, 16)
            sy.dma_start(out=dbg_out["dbg_zsd"], in_=io["zs_d_dram"].ap()).then_inc(sd.h, 16)
            sy.wait_ge(sd.h, 80)


def _p_dbg_p4(nc, io, s, dbg_out, ygT_sb):
    sd = Sem(nc, "s_dbg")
    with nc.Block("pdbg4") as block:
        @block.vector
        def _(v):
            v.wait_ge(s["yge"].h, 2)
            v.engine_nop().then_inc(sd.h, 1)

        @block.sync
        def _(sy):
            sy.wait_ge(s["ag2"].h, 2)
            sy.wait_ge(s["yst"].h, 16 * 8)
            sy.wait_ge(sd.h, 1)
            sy.dma_start(out=dbg_out["dbg_at0"], in_=io["at_full"][0].ap()).then_inc(sd.h, 16)
            sy.dma_start(out=dbg_out["dbg_at1"], in_=io["at_full"][1].ap()).then_inc(sd.h, 16)
            sy.dma_start(out=dbg_out["dbg_o"], in_=io["o_dram"].ap()).then_inc(sd.h, 16)
            sy.dma_start(out=dbg_out["dbg_y"], in_=io["y_dram"].ap()).then_inc(sd.h, 16)
            sy.dma_start(out=dbg_out["dbg_yg"], in_=ygT_sb[:, :]).then_inc(sd.h, 16)
            sy.wait_ge(sd.h, 81)


def _p0_transpose_gather(nc, io, C, s, wq_sb, wkv_sb):
    with (
        nc.sbuf_tensor("x_my_bf", [128, 2 * D], BF16) as x_my_bf,
        nc.sbuf_tensor("tp_st", [128, 4 * 512], BF16) as tp_st,
        nc.psum_tensor("ps_tpA", [128, 1024], BF16) as ps_tpA,
        nc.psum_tensor("ps_tpB", [128, 1024], BF16) as ps_tpB,
        nc.Block("p0") as block,
    ):
        @block.gpsimd
        def _(g):
            g.wait_ge(s["zc"].h, 90)
            g.wait_ge(s["zst"].h, 16 * 9)
            for tt in range(8):
                if tt >= 2:
                    g.wait_ge(s["tp"].h, (tt - 1) * 8)
                sx = s["x"][tt % 2]
                g.dma_start(out=x_my_bf[:, (tt % 2) * D:(tt % 2 + 1) * D],
                            in_=io["x_my"][tt * 128:(tt + 1) * 128, :]).then_inc(sx.h, 16)
                sx.inc(16)
            g.dma_start(out=wq_sb[:, :].rearrange("p (d h) -> p d h", d=NDMT),
                        in_=io["wq_my"].rearrange("(d p) h -> p d h", p=128)).then_inc(s["w"].h, 16)
            g.dma_start(out=wkv_sb[:, 0:NDMT * HD].rearrange("p (d h) -> p d h", d=NDMT),
                        in_=io["wk_my"].rearrange("(d p) h -> p d h", p=128)).then_inc(s["w"].h, 16)
            g.dma_start(out=wkv_sb[:, NDMT * HD:2 * NDMT * HD].rearrange("p (d h) -> p d h", d=NDMT),
                        in_=io["wv_my"].rearrange("(d p) h -> p d h", p=128)).then_inc(s["w"].h, 16)
            s["w"].inc(48)
            for half in range(2):
                for m4 in range(4):
                    g.wait_ge(s["ct"][m4].h, 16 * 8 * (half + 1))
                g.collective_compute(
                    "AllGather", ALU.bypass,
                    replica_groups=[list(range(NC))],
                    ins=[io["xt_contrib"][half].ap().opt()],
                    outs=[io["xt_full"][half].ap().opt()],
                ).then_inc(s["ag"].h, 1)
                s["ag"].inc()

        @block.tensor
        def _(t):
            t.wait_ge(s["const"].h, 96)
            n = 0
            for tt in range(8):
                t.wait_ge(s["x"][tt % 2].h, 16 * (tt // 2 + 1))
                for d4 in range(8):
                    pp = n % 2
                    ps_tp = ps_tpA if pp == 0 else ps_tpB
                    if n >= 2:
                        t.wait_ge(s["tpe"].h, n - 1)
                    for k in range(4):
                        d = d4 * 4 + k
                        ins = t.transpose(ps_tp[:, k * 128:(k + 1) * 128],
                                          x_my_bf[:, (tt % 2) * D + d * 128: (tt % 2) * D + (d + 1) * 128],
                                          C["ident_bf"])
                        if k == 3:
                            ins.then_inc(s["tp"].h, 1)
                            s["tp"].inc()
                    n += 1

        @block.vector
        def _(v):
            for g8 in range(64):
                pp = g8 % 2
                m4 = g8 % 4
                ps_tp = ps_tpA if pp == 0 else ps_tpB
                v.wait_ge(s["tp"].h, g8 + 1)
                if g8 >= 4:
                    v.wait_ge(s["ct"][m4].h, 16 * (g8 // 4))
                v.tensor_copy(tp_st[:, m4 * 512:(m4 + 1) * 512],
                              ps_tp[:, 0:512]).then_inc(s["tpe"].h, 1)
                s["tpe"].inc()

        @block.sync
        def _(sy):
            n = 0
            for tt in range(8):
                for d4 in range(8):
                    m4 = n % 4
                    n += 1
                    sy.wait_ge(s["tpe"].h, n)
                    sct = s["ct"][m4]
                    sy.dma_start(
                        out=io["xt_contrib"][tt // 4].ap().rearrange("(d p) t -> p d t", p=128)
                            [:, d4 * 4:(d4 + 1) * 4, (tt % 4) * 128:(tt % 4 + 1) * 128],
                        in_=tp_st[:, m4 * 512:(m4 + 1) * 512]
                            .rearrange("p (d t) -> p d t", d=4),
                    ).then_inc(sct.h, 16)
                    sct.inc(16)


def _p1_qkv(nc, io, C, s, qT, kT, v_sb, wq_sb, wkv_sb):
    with (
        nc.sbuf_tensor("xt_buf", [128, 3 * 16 * 512], BF16) as xt_buf,
        nc.psum_tensor("ps_q", [128, 4 * 512], F32) as ps_q,
        nc.psum_tensor("ps_kv0", [128, 2 * 512], F32) as ps_kv0,
        nc.psum_tensor("ps_kv1", [128, 2 * 512], F32) as ps_kv1,
        nc.Block("p1") as block,
    ):
        xtv = [io["xt_full"][i].ap().rearrange("(r d p) t -> r d p t", d=NDMT, p=128)
               for i in range(2)]

        @block.sync
        def _(sy):
            nb = 0
            for ch in range(16):
                r, half = ch // 2, ch % 2
                sy.wait_ge(s["ag"].h, half + 1)
                for hh in range(2):
                    b = nb % 3
                    if nb >= 3:
                        sy.wait_ge(s["mmkv"].h, (nb - 3) // 2 + 1)
                    sxt = s["xt"][b]
                    sy.dma_start(
                        out=xt_buf[:, b * 8192:(b + 1) * 8192]
                            .rearrange("p (d t) -> p d t", d=16),
                        in_=xtv[half][r, hh * 16:(hh + 1) * 16, :, :]
                            .transpose([1, 0, 2]),
                    ).then_inc(sxt.h, 16)
                    sxt.inc(16)
                    nb += 1

        @block.tensor
        def _(t):
            t.wait_ge(s["w"].h, 48)
            nb = 0
            for ch in range(16):
                for hh in range(2):
                    b = nb % 3
                    nb += 1
                    t.wait_ge(s["xt"][b].h, 16 * ((nb - 1) // 3 + 1))
                    if hh == 0 and ch > 0:
                        t.wait_ge(s["evq"].h, ch)
                    for d16 in range(16):
                        d = hh * 16 + d16
                        xs = xt_buf[:, b * 8192 + d16 * 512:b * 8192 + (d16 + 1) * 512]
                        for h in range(HPC):
                            ins = t.matmul(ps_q[:, h * 512:(h + 1) * 512],
                                           wq_sb[:, (d * HPC + h) * HD:(d * HPC + h + 1) * HD],
                                           xs, start=(d == 0), stop=(d == NDMT - 1))
                            if d == NDMT - 1 and h == HPC - 1:
                                ins.then_inc(s["mmq"].h, 1)
                                s["mmq"].inc()
                ps_kv = ps_kv0 if ch % 2 == 0 else ps_kv1
                if ch >= 2:
                    t.wait_ge(s["evkv"].h, ch - 1)
                for hh in range(2):
                    b = (nb - 2 + hh) % 3
                    for d16 in range(16):
                        d = hh * 16 + d16
                        xs = xt_buf[:, b * 8192 + d16 * 512:b * 8192 + (d16 + 1) * 512]
                        t.matmul(ps_kv[:, 0:512],
                                 wkv_sb[:, d * HD:(d + 1) * HD],
                                 xs, start=(d == 0), stop=(d == NDMT - 1))
                        for q4 in range(4):
                            ins = t.matmul(ps_kv[:, 512 + q4 * 128:512 + (q4 + 1) * 128],
                                           xs[:, q4 * 128:(q4 + 1) * 128],
                                           wkv_sb[:, NDMT * HD + d * HD:NDMT * HD + (d + 1) * HD],
                                           start=(d == 0 and q4 == 0), stop=(d == NDMT - 1 and q4 == 3))
                            if d == NDMT - 1 and q4 == 3:
                                ins.then_inc(s["mmkv"].h, 1)
                                s["mmkv"].inc()

        @block.vector
        def _(v):
            for ch in range(16):
                v.wait_ge(s["mmq"].h, ch + 1)
                for h in range(HPC):
                    ins = v.tensor_scalar(qT[:, h * T + ch * 512:h * T + (ch + 1) * 512],
                                          ps_q[:, h * 512:(h + 1) * 512],
                                          SCALE, None, ALU.mult)
                    if h == HPC - 1:
                        ins.then_inc(s["evq"].h, 1)
                        s["evq"].inc()

        @block.scalar
        def _(sc):
            for ch in range(16):
                ps_kv = ps_kv0 if ch % 2 == 0 else ps_kv1
                sc.wait_ge(s["mmkv"].h, ch + 1)
                sc.copy(kT[:, ch * 512:(ch + 1) * 512], ps_kv[:, 0:512])
                sc.copy(v_sb[:, ch * 512:(ch + 1) * 512],
                        ps_kv[:, 512:1024]).then_inc(s["evkv"].h, 1)
                s["evkv"].inc()


def _p2_attention(nc, io, C, s, qT, kT, v_sb):
    for k in ["sc", "exp", "msk", "pav", "lav", "rec", "bca", "lrbe", "avv"]:
        s[k] = Sem(nc, "s_" + k)
    s["avst"] = [Sem(nc, f"s_avst{i}") for i in range(2)]
    s["ag2"] = Sem(nc, "s_ag2")

    groups = [(h, b, g) for h in range(HPC) for b in range(B) for g in range(8)]
    npairs = [2 * (g + 1) for (_, _, g) in groups]
    ppre = [0]
    for np_ in npairs:
        ppre.append(ppre[-1] + np_)

    with (
        nc.sbuf_tensor("probs", [128, 3 * 1024], BF16) as probs,
        nc.sbuf_tensor("lrec", [1, 512], F32) as lrec,
        nc.sbuf_tensor("lrb_sb", [128, 512], F32) as lrb_sb,
        nc.sbuf_tensor("av_sb", [128, 2 * 512], BF16) as av_sb,
        nc.psum_tensor("ps_sc0", [128, 1024], F32) as ps_sc0,
        nc.psum_tensor("ps_sc1", [128, 1024], F32) as ps_sc1,
        nc.psum_tensor("ps_sc2", [128, 1024], F32) as ps_sc2,
        nc.psum_tensor("ps_av", [128, 512], F32) as ps_av,
        nc.psum_tensor("ps_l", [1, 512], F32) as ps_l,
        nc.Block("p2") as block,
    ):
        @block.tensor
        def _(t):
            def emit_qk(gi, kp):
                h, b, g = groups[gi]
                P = ppre[gi] + kp
                qs = qT[:, h * T + b * S + g * 512: h * T + b * S + (g + 1) * 512]
                ps = [ps_sc0, ps_sc1, ps_sc2][P % 3]
                if kp == 0:
                    t.wait_ge(s["lrbe"].h, gi)
                if P >= 3:
                    t.wait_ge(s["exp"].h, P - 2)
                for j in range(2):
                    kt = kp * 2 + j
                    ins = t.matmul(ps[:, j * 512:(j + 1) * 512],
                                   kT[:, b * S + kt * 128:b * S + (kt + 1) * 128],
                                   qs, start=True, stop=True)
                ins.then_inc(s["sc"].h, 1)
                s["sc"].inc()

            def emit_lav(gi, kp):
                h, b, g = groups[gi]
                npair = npairs[gi]
                nkt = 4 * (g + 1)
                P = ppre[gi] + kp
                d = kp - (npair - 2)
                if kp == 0:
                    t.wait_ge(s["rec"].h, gi)
                    t.wait_ge(s["avv"].h, gi)
                t.wait_ge(s["exp"].h, P + 1)
                if d >= 0:
                    t.wait_ge(s["msk"].h, 4 * gi + 2 * (d + 1))
                pb = probs[:, (P % 3) * 1024:(P % 3 + 1) * 1024]
                for j in range(2):
                    kt = kp * 2 + j
                    pc = pb[:, j * 512:(j + 1) * 512]
                    t.matmul(ps_l[0:1, :], C["ones_col"], pc,
                             start=(kt == 0), stop=(kt == nkt - 1))
                    ins = t.matmul(ps_av[:, :],
                                   v_sb[:, (b * 32 + kt) * 128:(b * 32 + kt + 1) * 128],
                                   pc, start=(kt == 0), stop=(kt == nkt - 1))
                ins.then_inc(s["pav"].h, 1)
                s["pav"].inc()

            for gi, (h, b, g) in enumerate(groups):
                npair = npairs[gi]
                for kp in range(npair):
                    emit_qk(gi, kp)
                    if kp >= 2:
                        emit_lav(gi, kp - 2)
                emit_lav(gi, npair - 2)
                emit_lav(gi, npair - 1)
                t.wait_ge(s["rec"].h, gi + 1)
                t.matmul(ps_sc0[:, 0:512], C["ones_row_f32"], lrec[0:1, :],
                         start=True, stop=True).then_inc(s["bca"].h, 1)
                s["bca"].inc()

        @block.scalar
        def _(sc):
            for gi, (h, b, g) in enumerate(groups):
                npair = npairs[gi]
                for kp in range(npair):
                    P = ppre[gi] + kp
                    ps = [ps_sc0, ps_sc1, ps_sc2][P % 3]
                    sc.wait_ge(s["sc"].h, P + 1)
                    if P >= 3:
                        sc.wait_ge(s["pav"].h, P - 2)
                    sc.activation(probs[:, (P % 3) * 1024:(P % 3 + 1) * 1024],
                                  ps[:, :], AF.Exp).then_inc(s["exp"].h, 1)
                    s["exp"].inc()

        @block.vector
        def _(v):
            for gi, (h, b, g) in enumerate(groups):
                npair = npairs[gi]
                for d in range(2):
                    kp = npair - 2 + d
                    P = ppre[gi] + kp
                    v.wait_ge(s["exp"].h, P + 1)
                    pb = probs[:, (P % 3) * 1024:(P % 3 + 1) * 1024]
                    for j in range(2):
                        o = 2 * d + j
                        pc = pb[:, j * 512:(j + 1) * 512]
                        v.tensor_tensor(pc, pc, C["masks"][:, o * 512:(o + 1) * 512],
                                        ALU.mult).then_inc(s["msk"].h, 1)
                        s["msk"].inc()
                v.wait_ge(s["pav"].h, ppre[gi + 1])
                v.reciprocal(lrec[0:1, :], ps_l[0:1, :]).then_inc(s["rec"].h, 1)
                s["rec"].inc()
                v.wait_ge(s["bca"].h, gi + 1)
                v.tensor_copy(lrb_sb[:, :], ps_sc0[:, 0:512]).then_inc(s["lrbe"].h, 1)
                s["lrbe"].inc()
                slot = gi % 2
                if gi >= 2:
                    v.wait_ge(s["avst"][slot].h, 16 * (gi // 2))
                v.wait_ge(s["lrbe"].h, gi + 1)
                v.tensor_tensor(av_sb[:, slot * 512:(slot + 1) * 512],
                                ps_av[:, :], lrb_sb[:, :], ALU.mult).then_inc(s["avv"].h, 1)
                s["avv"].inc()

        @block.sync
        def _(sy):
            for gi, (h, b, g) in enumerate(groups):
                slot = gi % 2
                sy.wait_ge(s["avv"].h, gi + 1)
                sy.dma_start(
                    out=io["at_contrib"][h // 2].ap()
                        [(h % 2) * 128:(h % 2) * 128 + 128,
                         b * S + g * 512:b * S + (g + 1) * 512],
                    in_=av_sb[:, slot * 512:(slot + 1) * 512],
                ).then_inc(s["avst"][slot].h, 16)
                s["avst"][slot].inc(16)

        @block.gpsimd
        def _(gp):
            for chunk in range(2):
                gp.wait_ge(s["avst"][0].h, 16 * 16 * (chunk + 1))
                gp.wait_ge(s["avst"][1].h, 16 * 16 * (chunk + 1))
                gp.collective_compute(
                    "AllGather", ALU.bypass,
                    replica_groups=[list(range(NC))],
                    ins=[io["at_contrib"][chunk].ap().opt()],
                    outs=[io["at_full"][chunk].ap().opt()],
                ).then_inc(s["ag2"].h, 1)
                s["ag2"].inc()


def _p_dbg_p2(nc, io, s, dbg_out):
    sd = Sem(nc, "s_dbg")
    with nc.Block("pdbg2") as block:
        @block.sync
        def _(sy):
            sy.wait_ge(s["ag2"].h, 2)
            sy.dma_start(out=dbg_out["dbg_at0"], in_=io["at_full"][0].ap()).then_inc(sd.h, 16)
            sy.dma_start(out=dbg_out["dbg_at1"], in_=io["at_full"][1].ap()).then_inc(sd.h, 16)
            sy.wait_ge(sd.h, 32)


def _p_dbg_qkv(nc, s, dbg_out, qT, kT, v_sb):
    sd = Sem(nc, "s_dbg")
    with nc.Block("pdbg") as block:
        @block.sync
        def _(sy):
            sy.wait_ge(s["evq"].h, 16)
            sy.wait_ge(s["evkv"].h, 16)
            sy.dma_start(out=dbg_out["dbg_qT"], in_=qT[:, :]).then_inc(sd.h, 16)
            sy.dma_start(out=dbg_out["dbg_kT"], in_=kT[:, :]).then_inc(sd.h, 16)
            sy.dma_start(out=dbg_out["dbg_v"], in_=v_sb[:, :]).then_inc(sd.h, 16)
            sy.wait_ge(sd.h, 48)


def _p3_oproj(nc, io, C, s, at_my):
    for k in ["gth", "omm", "oev", "gix"]:
        s[k] = Sem(nc, "s_" + k)
    s["wo"] = [Sem(nc, f"s_wo{i}") for i in range(2)]
    s["ost"] = [Sem(nc, f"s_ost{i}") for i in range(2)]

    with (
        nc.sbuf_tensor("gix_sb", [128, 32], I32) as gix_sb,
        nc.sbuf_tensor("wo_sl", [128, 2 * 32 * 256], BF16) as wo_sl,
        nc.sbuf_tensor("o_st", [128, 2 * 2048], BF16) as o_st,
        nc.psum_tensor("ps_oA", [128, 2048], F32) as ps_oA,
        nc.psum_tensor("ps_oB", [128, 2048], F32) as ps_oB,
        nc.Block("p3") as block,
    ):
        wov = io["wo"].rearrange("(d p) n -> p d n", p=128)

        @block.sync
        def _(sy):
            sy.dma_start(out=gix_sb[:, :], in_=io["gidx"]).then_inc(s["gix"].h, 16)
            s["gix"].inc(16)
            for hc in range(16):
                sl = hc % 2
                sy.wait_ge(s["oev"].h, hc + 1)
                sy.dma_start(
                    out=io["o_dram"].ap().rearrange("(t p) n -> p t n", p=128)
                        [:, :, hc * 256:(hc + 1) * 256],
                    in_=o_st[:, sl * 2048:(sl + 1) * 2048]
                        .rearrange("p (t n) -> p t n", t=8),
                ).then_inc(s["ost"][sl].h, 16)
                s["ost"][sl].inc(16)

        @block.gpsimd
        def _(g):
            g.wait_ge(s["gix"].h, 16)
            g.wait_ge(s["ag2"].h, 2)
            for chunk in range(2):
                src = io["at_full"][chunk].ap().rearrange("r (tb tk) -> (r tb) tk", tk=MY)
                for t in range(16):
                    head = 4 * (t // 2) + 2 * chunk + (t % 2)
                    i = chunk * 16 + t
                    g.indirect_dma_start(
                        at_my[:, head * MY:(head + 1) * MY], None,
                        src, IndirectOffsetOnAxis(ap=gix_sb[:, i:i + 1], axis=0),
                    ).then_inc(s["gth"].h, 16)
                    s["gth"].inc(16)
            for hc in range(16):
                sl = hc % 2
                if hc >= 2:
                    g.wait_ge(s["omm"].h, hc - 1)
                g.dma_start(
                    out=wo_sl[:, sl * 8192:(sl + 1) * 8192]
                        .rearrange("p (d n) -> p d n", d=NDMT),
                    in_=wov[:, :, hc * 256:(hc + 1) * 256],
                ).then_inc(s["wo"][sl].h, 16)
                s["wo"][sl].inc(16)

        @block.tensor
        def _(t):
            t.wait_ge(s["gth"].h, 16 * 32)
            for hc in range(16):
                sl = hc % 2
                ps_o = ps_oA if sl == 0 else ps_oB
                t.wait_ge(s["wo"][sl].h, 16 * (hc // 2 + 1))
                if hc >= 2:
                    t.wait_ge(s["oev"].h, hc - 1)
                for d in range(NDMT):
                    for tt in range(8):
                        ins = t.matmul(
                            ps_o[:, tt * 256:(tt + 1) * 256],
                            at_my[:, d * MY + tt * 128:d * MY + (tt + 1) * 128],
                            wo_sl[:, sl * 8192 + d * 256:sl * 8192 + (d + 1) * 256],
                            start=(d == 0 and tt % 2 == 0),
                            stop=(d == NDMT - 1 and tt % 2 == 1))
                ins.then_inc(s["omm"].h, 1)
                s["omm"].inc()

        @block.vector
        def _(v):
            for hc in range(16):
                sl = hc % 2
                ps_o = ps_oA if sl == 0 else ps_oB
                v.wait_ge(s["omm"].h, hc + 1)
                if hc >= 2:
                    v.wait_ge(s["ost"][sl].h, 16 * (hc // 2))
                ins = v.tensor_copy(o_st[:, sl * 2048:(sl + 1) * 2048], ps_o[:, :])
                ins.then_inc(s["oev"].h, 1)
                s["oev"].inc()


def _p4_postln1(nc, io, C, s, yT, ygT_sb):
    for k in ["lnv", "lna", "yt", "yte", "ygm", "yge", "xl", "ol", "yst"]:
        s[k] = Sem(nc, "s_" + k)
    NV, NA = 8, 2

    with (
        nc.sbuf_tensor("x_tile", [128, D], F32) as x_tile,
        nc.sbuf_tensor("o_tile", [128, D], BF16) as o_tile,
        nc.sbuf_tensor("z_sb", [128, D], F32) as z_sb,
        nc.sbuf_tensor("sq_sb", [128, D], F32) as sq_sb,
        nc.sbuf_tensor("y_tile", [128, D], BF16) as y_tile,
        nc.sbuf_tensor("ln_sc", [128, 8], F32) as ln_sc,
        nc.psum_tensor("ps_ytA", [128, 1024], BF16) as ps_ytA,
        nc.psum_tensor("ps_ytB", [128, 1024], BF16) as ps_ytB,
        nc.psum_tensor("ps_yg", [32, 1024], F32) as ps_yg,
        nc.Block("p4") as block,
    ):
        S1, S2, M, M2, VE, STD, RSTD, NEGMR = range(8)

        @block.sync
        def _(sy):
            for sl in range(2):
                sy.wait_ge(s["ost"][sl].h, 16 * 8)
            for tt in range(8):
                if tt > 0:
                    sy.wait_ge(s["lnv"].h, (tt - 1) * NV + 1)
                sy.dma_start(out=x_tile[:, :],
                             in_=io["x_my"][tt * 128:(tt + 1) * 128, :]).then_inc(s["xl"].h, 16)
                s["xl"].inc(16)
                sy.dma_start(out=o_tile[:, :],
                             in_=io["o_dram"].ap()[tt * 128:(tt + 1) * 128, :]).then_inc(s["ol"].h, 16)
                s["ol"].inc(16)
                sy.wait_ge(s["lnv"].h, tt * NV + 8)
                sy.dma_start(out=io["y_dram"].ap()[tt * 128:(tt + 1) * 128, :],
                             in_=y_tile[:, :]).then_inc(s["yst"].h, 16)
                s["yst"].inc(16)

        @block.vector
        def _(v):
            for tt in range(8):
                base = tt * NV
                abase = tt * NA
                v.wait_ge(s["xl"].h, 16 * (tt + 1))
                v.wait_ge(s["ol"].h, 16 * (tt + 1))

                v.tensor_tensor(z_sb[:, :], x_tile[:, :],
                                o_tile[:, :], ALU.add).then_inc(s["lnv"].h, 1)
                v.wait_ge(s["lnv"].h, base + 1)
                v.reduce_sum(ln_sc[:, S1:S1 + 1], z_sb[:, :], axis=AX.X).then_inc(s["lnv"].h, 1)
                v.wait_ge(s["lnv"].h, base + 2)
                v.tensor_scalar(ln_sc[:, M:M + 1], ln_sc[:, S1:S1 + 1],
                                1.0 / D, None, ALU.mult).then_inc(s["lnv"].h, 1)
                v.wait_ge(s["lnv"].h, base + 3)
                v.tensor_scalar(ln_sc[:, M2:M2 + 1], ln_sc[:, M:M + 1],
                                ln_sc[:, M:M + 1], -1e-5,
                                ALU.mult, ALU.add).then_inc(s["lnv"].h, 1)
                v.wait_ge(s["lnv"].h, base + 4)
                v.wait_ge(s["lna"].h, abase + 1)
                v.scalar_tensor_tensor(ln_sc[:, VE:VE + 1], ln_sc[:, S2:S2 + 1],
                                       1.0 / D, ln_sc[:, M2:M2 + 1],
                                       ALU.mult, ALU.subtract).then_inc(s["lnv"].h, 1)
                v.wait_ge(s["lna"].h, abase + 2)
                v.reciprocal(ln_sc[:, RSTD:RSTD + 1],
                             ln_sc[:, STD:STD + 1]).then_inc(s["lnv"].h, 1)
                v.wait_ge(s["lnv"].h, base + 6)
                v.scalar_tensor_tensor(ln_sc[:, NEGMR:NEGMR + 1], ln_sc[:, M:M + 1],
                                       -1.0, ln_sc[:, RSTD:RSTD + 1],
                                       ALU.mult, ALU.mult).then_inc(s["lnv"].h, 1)
                v.wait_ge(s["lnv"].h, base + 7)
                if tt > 0:
                    v.wait_ge(s["yst"].h, 16 * tt)
                    v.wait_ge(s["yt"].h, tt * 32)
                v.tensor_scalar(y_tile[:, :], z_sb[:, :],
                                ln_sc[:, RSTD:RSTD + 1], ln_sc[:, NEGMR:NEGMR + 1],
                                ALU.mult, ALU.add).then_inc(s["lnv"].h, 1)
                for cj in range(NDMT):
                    n = tt * NDMT + cj
                    pp = n % 2
                    ps_yt = ps_ytA if pp == 0 else ps_ytB
                    v.wait_ge(s["yt"].h, n + 1)
                    v.tensor_copy(yT[:, cj * MY + tt * 128:cj * MY + (tt + 1) * 128],
                                  ps_yt[:, 0:128]).then_inc(s["yte"].h, 1)
                    s["yte"].inc()
            for half in range(2):
                v.wait_ge(s["ygm"].h, half + 1)
                v.tensor_copy(ygT_sb[:, half * 512:(half + 1) * 512],
                              ps_yg[:, half * 512:(half + 1) * 512]).then_inc(s["yge"].h, 1)
                s["yge"].inc()

        @block.scalar
        def _(sc):
            for tt in range(8):
                base = tt * NV
                sc.wait_ge(s["lnv"].h, base + 1)
                sc.activation(sq_sb[:, :], z_sb[:, :], AF.Square,
                              accum_out=ln_sc[:, S2:S2 + 1]).then_inc(s["lna"].h, 1)
                sc.wait_ge(s["lnv"].h, base + 5)
                sc.activation(ln_sc[:, STD:STD + 1], ln_sc[:, VE:VE + 1],
                              AF.Sqrt).then_inc(s["lna"].h, 1)
                s["lna"].inc(2)

        @block.tensor
        def _(t):
            n = 0
            for tt in range(8):
                t.wait_ge(s["lnv"].h, tt * NV + 8)
                for cj in range(NDMT):
                    c4, j = cj // 8, cj % 8
                    pp = n % 2
                    ps_yt = ps_ytA if pp == 0 else ps_ytB
                    if n >= 2:
                        t.wait_ge(s["yte"].h, n - 1)
                    src = y_tile[:, :].rearrange("p (a q e) -> p a q e", a=4, e=8)[:, c4, :, j]
                    ins = t.transpose(ps_yt[:, 0:128], src, C["ident_bf"])
                    ins.then_inc(s["yt"].h, 1)
                    s["yt"].inc()
                    n += 1
            t.wait_ge(s["yte"].h, 256)
            for half in range(2):
                for cj in range(NDMT):
                    c4 = cj // 8
                    ins = t.matmul(
                        ps_yg[:, half * 512:(half + 1) * 512],
                        C["e_cmats"][:, c4 * 32:(c4 + 1) * 32],
                        yT[:, cj * MY + half * 512:cj * MY + (half + 1) * 512],
                        start=(cj == 0), stop=(cj == NDMT - 1))
                ins.then_inc(s["ygm"].h, 1)
                s["ygm"].inc()



def _p5pre_zeros(nc, io, C, s):
    for k in ["zplq", "zpls", "zc", "zst"]:
        s[k] = Sem(nc, "s_" + k)

    CH = INTER // 4
    with (
        nc.sbuf_tensor("qz_sb", [86, 512], I32) as qz_sb,
        nc.sbuf_tensor("z4_sb", [86, 4096], I32) as z4_sb,
        nc.sbuf_tensor("tmp_sb", [86, 4096], F32) as tmp_sb,
        nc.sbuf_tensor("szs_sb", [86, 4096], BF16) as szs_sb,
        nc.sbuf_tensor("zs_out", [86, 4096], F32R) as zs_out,
        nc.Block("p5pre") as block,
    ):
        chunks = []
        for mat, (qz, sc) in enumerate([(io["g_qz"], io["g_sc"]), (io["u_qz"], io["u_sc"])]):
            for ci in range(4):
                chunks.append((qz, sc, io["zs_gu_dram"], mat * INTER + ci * CH,
                               32, CH, ci * (CH // 8), ci * CH))
        chunks.append((io["d_qz"], io["d_sc"], io["zs_d_dram"], 0, 86, D, 0, 0))

        @block.sync
        def _(sy):
            for i, (qz, sc, dst, col, rows, w, qzoff, scoff) in enumerate(chunks):
                if i >= 1:
                    sy.wait_ge(s["zc"].h, 10 * i)
                sy.dma_start(out=qz_sb[0:rows, 0:w // 8],
                             in_=qz[:, qzoff:qzoff + w // 8]).then_inc(s["zplq"].h, 16)
                s["zplq"].inc(16)

        @block.gpsimd
        def _(g):
            def store(i):
                _, _, dst, col, rows, w, _, _ = chunks[i]
                g.wait_ge(s["zc"].h, 10 * (i + 1))
                g.dma_start(out=dst.ap()[0:rows, col:col + w],
                            in_=zs_out[0:rows, 0:w]).then_inc(s["zst"].h, 16)
                s["zst"].inc(16)

            for i, (qz, sc, dst, col, rows, w, qzoff, scoff) in enumerate(chunks):
                if i >= 1:
                    g.wait_ge(s["zc"].h, 10 * i)
                g.dma_start(out=szs_sb[0:rows, 0:w],
                            in_=sc[:, scoff:scoff + w]).then_inc(s["zpls"].h, 16)
                s["zpls"].inc(16)
                if i >= 1:
                    store(i - 1)
            store(len(chunks) - 1)

        @block.vector
        def _(v):
            for i, (qz, sc, dst, col, rows, w, qzoff, scoff) in enumerate(chunks):
                v.wait_ge(s["zplq"].h, 16 * (i + 1))
                v.wait_ge(s["zpls"].h, 16 * (i + 1))
                v.wait_ge(s["zc"].h, 10 * i)
                if i >= 1:
                    v.wait_ge(s["zst"].h, 16 * i)
                z4v = z4_sb[0:rows, 0:w].rearrange("g (m e) -> g m e", e=8)
                for j in range(8):
                    ins = v.tensor_scalar(z4v[:, :, j], qz_sb[0:rows, 0:w // 8],
                                          4 * j, 15,
                                          ALU.logical_shift_right, ALU.bitwise_and)
                    ins.then_inc(s["zc"].h, 1)
                    s["zc"].inc()
                v.wait_ge(s["zc"].h, s["zc"].n)
                v.tensor_scalar(tmp_sb[0:rows, 0:w], z4_sb[0:rows, 0:w],
                                -7.0, -1.0, ALU.add, ALU.mult).then_inc(s["zc"].h, 1)
                s["zc"].inc()
                v.wait_ge(s["zc"].h, s["zc"].n)
                v.tensor_tensor(zs_out[0:rows, 0:w], tmp_sb[0:rows, 0:w],
                                szs_sb[0:rows, 0:w], ALU.mult).then_inc(s["zc"].h, 1)
                s["zc"].inc()


def _p5_gateup(nc, io, C, s, yT, ygT_sb, gs_sb):
    for k in ["dqa", "dqb", "gstp", "ustp", "silu", "mul", "gstm", "gtr", "gse", "zsl", "gev"]:
        s[k] = Sem(nc, "s_" + k)
    s["qw"] = [Sem(nc, f"s_qw{i}") for i in range(2)]
    s["stl"] = [Sem(nc, f"s_stl{i}") for i in range(2)]
    s["hst"] = [Sem(nc, f"s_hst{i}") for i in range(2)]
    s["zsl2"] = [Sem(nc, f"s_zsl{i}") for i in range(2)]

    NPN = INTER // 256
    NPN_R = P5_LIMIT if P5_LIMIT else NPN
    NIT_R = 2 * NPN_R
    trunc = NPN_R != NPN

    with (
        nc.sbuf_tensor("qw_sl", [128, 2 * 8 * 256], I32) as qw_sl,
        nc.sbuf_tensor("st_sl", [128, 2 * 8 * 256], F32) as st_sl,
        nc.sbuf_tensor("st_bf", [128, 2 * 8 * 256], BF16) as st_bf,
        nc.sbuf_tensor("panelG", [128, 2 * 8192], BF16) as panelG,
        nc.sbuf_tensor("panelU", [128, 2 * 8192], BF16) as panelU,
        nc.sbuf_tensor("zs_sl", [32, 2 * 512], F32R) as zs_sl,
        nc.sbuf_tensor("silu_sb", [128, 2 * 1024], BF16) as silu_sb,
        nc.sbuf_tensor("hT_st", [128, 2 * 1024], BF16) as hT_st,
        nc.sbuf_tensor("gst_sb", [128, 8 * 86], F32) as gst_sb,
        nc.sbuf_tensor("dqt", [128, 8 * 256], I32) as dqt,
        nc.psum_tensor("ps_g", [128, 1024], F32) as ps_g,
        nc.psum_tensor("ps_u", [128, 1024], F32) as ps_u,
        nc.psum_tensor("ps_gst", [128, 8], F32) as ps_gst,
        nc.psum_tensor("ps_gtr", [86, 1024], F32) as ps_gtr,
        nc.Block("p5") as block,
    ):
        qws = [io["g_qw"], io["u_qw"]]
        scs = [io["g_sc"], io["u_sc"]]
        panels = [panelG, panelU]

        @block.sync
        def _(sy):
            sy.wait_ge(s["yst"].h, 16 * 8)
            sy.wait_ge(s["zc"].h, 90)
            for pn in range(NPN_R):
                sl = pn % 2
                if pn >= 2:
                    sy.wait_ge(s["dqb"].h, 8 * (pn - 1))
                for mat in range(2):
                    for c4 in range(4):
                        sy.dma_start(
                            out=qw_sl[:, (sl * 8 + mat * 4 + c4) * 256:
                                      (sl * 8 + mat * 4 + c4 + 1) * 256],
                            in_=qws[mat][128 * c4:128 * (c4 + 1),
                                         pn * 256:(pn + 1) * 256],
                        ).then_inc(s["qw"][sl].h, 16)
                        s["qw"][sl].inc(16)
                        src = scs[mat][8 * c4:8 * (c4 + 1), pn * 256:(pn + 1) * 256]
                        src = src.unsqueeze(1).broadcast_to([8, 16, 256])
                        sy.dma_start(
                            out=st_sl[:, (sl * 8 + mat * 4 + c4) * 256:
                                      (sl * 8 + mat * 4 + c4 + 1) * 256],
                            in_=src,
                        ).then_inc(s["stl"][sl].h, 16)
                        s["stl"][sl].inc(16)
                if pn >= 2:
                    sy.wait_ge(s["ustp"].h, 2 * pn - 2)
                sy.dma_start(
                    out=zs_sl[:, sl * 512:sl * 512 + 256],
                    in_=io["zs_gu_dram"].ap()[0:32, pn * 256:(pn + 1) * 256],
                ).then_inc(s["zsl2"][sl].h, 16)
                sy.dma_start(
                    out=zs_sl[:, sl * 512 + 256:(sl + 1) * 512],
                    in_=io["zs_gu_dram"].ap()[0:32, INTER + pn * 256:INTER + (pn + 1) * 256],
                ).then_inc(s["zsl2"][sl].h, 16)
                s["zsl2"][sl].inc(32)

        @block.gpsimd
        def _(g):
            g.wait_ge(s["yst"].h, 16 * 8)
            g.wait_ge(s["zc"].h, 90)

            def hstore(it):
                sl = it % 2
                g.wait_ge(s["mul"].h, it + 1)
                g.dma_start(out=io["hT_dram"].ap()[it * 128:(it + 1) * 128, :],
                            in_=hT_st[:, sl * 1024:(sl + 1) * 1024]).then_inc(s["hst"][sl].h, 16)
                s["hst"][sl].inc(16)

            for it in range(NIT_R):
                hstore(it)

        @block.vector
        def _(v):
            v.wait_ge(s["yst"].h, 16 * 8)
            v.wait_ge(s["zst"].h, 16 * 9)
            def emit_dequant(pn):
                sl = pn % 2
                v.wait_ge(s["qw"][sl].h, 128 * (pn // 2 + 1))
                v.wait_ge(s["stl"][sl].h, 128 * (pn // 2 + 1))
                if pn >= 2:
                    v.wait_ge(s["gstp"].h, 2 * pn - 2)
                v.tensor_copy(st_bf[:, sl * 2048:(sl + 1) * 2048],
                              st_sl[:, sl * 2048:(sl + 1) * 2048]).then_inc(s["dqa"].h, 1)
                s["dqa"].inc()
                for r, (mat, c4) in enumerate([(m, c) for m in range(2) for c in range(4)]):
                    gr = pn * 8 + r
                    base = (sl * 8 + mat * 4 + c4) * 256
                    if gr >= 1:
                        v.wait_ge(s["dqb"].h, gr)
                    if pn >= 2 and mat == 1 and c4 == 0:
                        v.wait_ge(s["ustp"].h, 2 * pn - 2)
                    for j in range(8):
                        ins = v.tensor_scalar(dqt[:, j * 256:(j + 1) * 256],
                                              qw_sl[:, base:base + 256], 4 * j, 15,
                                              ALU.logical_shift_right, ALU.bitwise_and)
                    ins.then_inc(s["dqa"].h, 1)
                    s["dqa"].inc()
                    v.wait_ge(s["dqa"].h, s["dqa"].n)
                    pan = panels[mat]
                    for j in range(8):
                        cj = c4 * 8 + j
                        ins = v.scalar_tensor_tensor(
                            pan[:, sl * 8192 + cj * 256:sl * 8192 + (cj + 1) * 256],
                            dqt[:, j * 256:(j + 1) * 256], -8.0,
                            st_bf[:, base:base + 256], ALU.add, ALU.mult)
                    ins.then_inc(s["dqb"].h, 1)
                    s["dqb"].inc()

            def emit_gev(it):
                if NO_AUX:
                    return
                v.wait_ge(s["gstm"].h, it + 1)
                v.tensor_copy(gst_sb[:, it * 8:(it + 1) * 8], ps_gst[:, :]).then_inc(s["gev"].h, 1)
                s["gev"].inc()

            def emit_mul(it):
                sl = it % 2
                v.wait_ge(s["ustp"].h, it + 1)
                v.wait_ge(s["silu"].h, it + 1)
                if it >= 2:
                    v.wait_ge(s["hst"][sl].h, 16 * (it // 2))
                    if not NO_AUX:
                        v.wait_ge(s["gstm"].h, it - 1)
                v.tensor_tensor(hT_st[:, sl * 1024:sl * 1024 + 512],
                                ps_u[:, 0:512], silu_sb[:, sl * 1024:sl * 1024 + 512],
                                ALU.mult)
                v.tensor_tensor(hT_st[:, sl * 1024 + 512:(sl + 1) * 1024],
                                ps_u[:, 512:1024], silu_sb[:, sl * 1024 + 512:(sl + 1) * 1024],
                                ALU.mult).then_inc(s["mul"].h, 1)
                s["mul"].inc()

            for pn in range(NPN_R):
                emit_dequant(pn)
                if pn >= 2:
                    emit_gev(2 * (pn - 2))
                    emit_gev(2 * (pn - 2) + 1)
                if pn >= 1:
                    emit_mul(2 * (pn - 1))
                    emit_mul(2 * (pn - 1) + 1)
            if NPN_R >= 2:
                emit_gev(2 * NPN_R - 4)
                emit_gev(2 * NPN_R - 3)
            emit_mul(2 * (NPN_R - 1))
            emit_mul(2 * (NPN_R - 1) + 1)
            emit_gev(2 * NPN_R - 2)
            emit_gev(2 * NPN_R - 1)
            if not NO_AUX:
                v.wait_ge(s["gev"].h, NIT_R)
                v.engine_nop().then_inc(s["gse"].h, 1)
                s["gse"].inc()
                if not NO_TR:
                    v.wait_ge(s["gtr"].h, 8)
                    v.tensor_copy(gs_sb[0:86, :],
                                  ps_gtr[0:86, :]).then_inc(s["gse"].h, 1)
                    s["gse"].inc()

        @block.scalar
        def _(sc):
            for it in range(NIT_R):
                sl = it % 2
                sc.wait_ge(s["gstp"].h, it + 1)
                if it >= 2:
                    sc.wait_ge(s["mul"].h, it - 1)
                sc.activation(silu_sb[:, sl * 1024:(sl + 1) * 1024],
                              ps_g[:, :], AF.Silu).then_inc(s["silu"].h, 1)
                s["silu"].inc()

        @block.tensor
        def _(t):
            t.wait_ge(s["yge"].h, 2)
            t.wait_ge(s["zst"].h, 16 * 9)
            for pn in range(NPN_R):
                sl = pn % 2
                t.wait_ge(s["zsl2"][pn % 2].h, 32 * (pn // 2 + 1))
                for itp in range(2):
                    it = 2 * pn + itp
                    if it > 0:
                        t.wait_ge(s["silu"].h, it)
                    for cj in range(NDMT):
                        if itp == 0 and cj % 8 == 0:
                            t.wait_ge(s["dqb"].h, 8 * pn + cj // 8 + 1)
                        for th in range(2):
                            ins = t.matmul(ps_g[:, th * 512:(th + 1) * 512],
                                           panelG[:, sl * 8192 + cj * 256 + itp * 128:
                                                  sl * 8192 + cj * 256 + (itp + 1) * 128],
                                           yT[:, cj * MY + th * 512:cj * MY + (th + 1) * 512],
                                           start=(cj == 0), stop=(NO_Z and cj == NDMT - 1))
                    if not NO_Z:
                        for th in range(2):
                            ins = t.matmul(ps_g[:, th * 512:(th + 1) * 512],
                                           zs_sl[0:32, sl * 512 + itp * 128:sl * 512 + (itp + 1) * 128],
                                           ygT_sb[0:32, th * 512:(th + 1) * 512],
                                           start=False, stop=True)
                    ins.then_inc(s["gstp"].h, 1)
                    s["gstp"].inc()
                    if it > 0:
                        t.wait_ge(s["mul"].h, it)
                    for cj in range(NDMT):
                        if itp == 0 and cj % 8 == 0:
                            t.wait_ge(s["dqb"].h, 8 * pn + 4 + cj // 8 + 1)
                        for th in range(2):
                            ins = t.matmul(ps_u[:, th * 512:(th + 1) * 512],
                                           panelU[:, sl * 8192 + cj * 256 + itp * 128:
                                                  sl * 8192 + cj * 256 + (itp + 1) * 128],
                                           yT[:, cj * MY + th * 512:cj * MY + (th + 1) * 512],
                                           start=(cj == 0), stop=(NO_Z and cj == NDMT - 1))
                    if not NO_Z:
                        for th in range(2):
                            ins = t.matmul(ps_u[:, th * 512:(th + 1) * 512],
                                           zs_sl[0:32, sl * 512 + 256 + itp * 128:
                                                 sl * 512 + 256 + (itp + 1) * 128],
                                           ygT_sb[0:32, th * 512:(th + 1) * 512],
                                           start=False, stop=True)
                    ins.then_inc(s["ustp"].h, 1)
                    s["ustp"].inc()
                    if it > 0 and not NO_AUX:
                        t.wait_ge(s["mul"].h, it)
                        if it > 1:
                            t.wait_ge(s["gev"].h, it - 1)
                        psl = (it - 1) % 2
                        if NO_GST:
                            ins = t.matmul(ps_gst[:, 0:8],
                                           hT_st[:, psl * 1024:psl * 1024 + 128],
                                           hT_st[:, psl * 1024:psl * 1024 + 8],
                                           start=True, stop=True)
                        else:
                            for tt in range(8):
                                ins = t.matmul(ps_gst[:, tt:tt + 1],
                                               hT_st[:, psl * 1024 + tt * 128:psl * 1024 + (tt + 1) * 128],
                                               C["ones_col"], start=True, stop=True)
                        ins.then_inc(s["gstm"].h, 1)
                        s["gstm"].inc()
            if NO_AUX:
                return
            t.wait_ge(s["mul"].h, NIT_R)
            t.wait_ge(s["gev"].h, NIT_R - 1)
            psl = (NIT_R - 1) % 2
            if NO_GST:
                ins = t.matmul(ps_gst[:, 0:8],
                               hT_st[:, psl * 1024:psl * 1024 + 128],
                               hT_st[:, psl * 1024:psl * 1024 + 8],
                               start=True, stop=True)
            else:
                for tt in range(8):
                    ins = t.matmul(ps_gst[:, tt:tt + 1],
                                   hT_st[:, psl * 1024 + tt * 128:psl * 1024 + (tt + 1) * 128],
                                   C["ones_col"], start=True, stop=True)
            ins.then_inc(s["gstm"].h, 1)
            s["gstm"].inc()
            t.wait_ge(s["gse"].h, 1)
            if not NO_TR:
                gstv = gst_sb[:, :].rearrange("p (i t) -> p i t", t=8)
                for tt in range(8):
                    t.transpose(ps_gtr[0:86, tt * 128:(tt + 1) * 128],
                                gstv[:, :, tt],
                                C["ident_f32"]).then_inc(s["gtr"].h, 1)
                    s["gtr"].inc()


def _p6_down(nc, io, C, s, gs_sb, o2_sb):
    for k in ["ddq", "dmm", "dtm", "dev"]:
        s[k] = Sem(nc, "s_" + k)
    s["dqw"] = [Sem(nc, f"s_dqw{i}") for i in range(2)]
    s["dsd"] = [Sem(nc, f"s_dsd{i}") for i in range(2)]
    s["dht"] = [Sem(nc, f"s_dht{i}") for i in range(4)]
    s["dzs"] = [Sem(nc, f"s_dzs{i}") for i in range(2)]

    NCI = 11
    rows_of = lambda ci: 96 if ci == 10 else 128
    htv = io["hT_dram"].ap().rearrange("(q e) t -> e q t", e=8)

    with (
        nc.sbuf_tensor("dqw_sl", [128, 2 * 512], I32) as dqw_sl,
        nc.sbuf_tensor("dsd_sl", [128, 2 * 512], F32) as dsd_sl,
        nc.sbuf_tensor("dsd_bf", [128, 2 * 512], BF16) as dsd_bf,
        nc.sbuf_tensor("dht_sl", [128, 4 * 1024], BF16) as dht_sl,
        nc.sbuf_tensor("dzs_sl", [86, 2 * 512], F32R) as dzs_sl,
        nc.sbuf_tensor("wd_sb", [128, 4 * 512], BF16) as wd_sb,
        nc.sbuf_tensor("dqt_d", [128, 512], I32) as dqt_d,
        nc.psum_tensor("ps_o2", [128, 8 * 512], F32) as ps_o2,
        nc.Block("p6") as block,
    ):
        def nidx(hc, ci, j):
            return hc * 88 + ci * 8 + j

        @block.sync
        def _(sy):
            sy.wait_ge(s["ustp"].h, N_IT)
            for sl in range(2):
                sy.wait_ge(s["hst"][sl].h, 16 * 43)
            for hc in range(8):
                sl2 = hc % 2
                if hc >= 2:
                    sy.wait_ge(s["dmm"].h, hc - 1)
                sy.dma_start(out=dzs_sl[0:86, sl2 * 512:(sl2 + 1) * 512],
                             in_=io["zs_d_dram"].ap()[0:86, hc * 512:(hc + 1) * 512]
                             ).then_inc(s["dzs"][sl2].h, 16)
                s["dzs"][sl2].inc(16)
                for ci in range(NCI):
                    rows = rows_of(ci)
                    nq = hc * NCI + ci
                    slq = nq % 2
                    if nq >= 2:
                        sy.wait_ge(s["ddq"].h, 17 * (nq - 1))
                    sy.dma_start(out=dqw_sl[0:rows, slq * 512:(slq + 1) * 512],
                                 in_=io["d_qw"][128 * ci:128 * ci + rows,
                                                hc * 512:(hc + 1) * 512]
                                 ).then_inc(s["dqw"][slq].h, 16)
                    s["dqw"][slq].inc(16)
                    grows = 6 if ci == 10 else 8
                    srcs_ = io["d_sc"][8 * ci:8 * ci + grows, hc * 512:(hc + 1) * 512]
                    srcs_ = srcs_.unsqueeze(1).broadcast_to([grows, 16, 512])
                    sy.dma_start(out=dsd_sl[0:rows, slq * 512:(slq + 1) * 512],
                                 in_=srcs_).then_inc(s["dsd"][slq].h, 16)
                    s["dsd"][slq].inc(16)
                    for j in range(8):
                        n = nidx(hc, ci, j)
                        sl = n % 4
                        if n >= 4:
                            sy.wait_ge(s["dtm"].h, n - 3)
                        sy.dma_start(out=dht_sl[0:rows, sl * 1024:(sl + 1) * 1024],
                                     in_=htv[j, 128 * ci:128 * ci + rows, :]
                                     ).then_inc(s["dht"][sl].h, 16)
                        s["dht"][sl].inc(16)

        @block.gpsimd
        def _(g):
            pass

        @block.vector
        def _(v):
            v.wait_ge(s["ustp"].h, N_IT)
            for sl in range(2):
                v.wait_ge(s["hst"][sl].h, 16 * 43)
            for hc in range(8):
                for ci in range(NCI):
                    rows = rows_of(ci)
                    nq = hc * NCI + ci
                    slq = nq % 2
                    v.wait_ge(s["dqw"][slq].h, 16 * (nq // 2 + 1))
                    v.wait_ge(s["dsd"][slq].h, 16 * (nq // 2 + 1))
                    v.tensor_copy(dsd_bf[0:rows, slq * 512:(slq + 1) * 512],
                                  dsd_sl[0:rows, slq * 512:(slq + 1) * 512]).then_inc(s["ddq"].h, 1)
                    s["ddq"].inc()
                    for j in range(8):
                        n = nidx(hc, ci, j)
                        sl = n % 4
                        if n >= 4:
                            v.wait_ge(s["dtm"].h, n - 3)
                        v.wait_ge(s["ddq"].h, s["ddq"].n)
                        v.tensor_scalar(dqt_d[0:rows, :], dqw_sl[0:rows, slq * 512:
                                        (slq + 1) * 512], 4 * j, 15,
                                        ALU.logical_shift_right,
                                        ALU.bitwise_and).then_inc(s["ddq"].h, 1)
                        s["ddq"].inc()
                        v.wait_ge(s["ddq"].h, s["ddq"].n)
                        v.scalar_tensor_tensor(wd_sb[0:rows, sl * 512:(sl + 1) * 512],
                                               dqt_d[0:rows, :], -8.0,
                                               dsd_bf[0:rows, slq * 512:(slq + 1) * 512],
                                               ALU.add, ALU.mult).then_inc(s["ddq"].h, 1)
                        s["ddq"].inc()
                v.wait_ge(s["dmm"].h, hc + 1)
                for tt in range(8):
                    ins = v.tensor_copy(o2_sb[:, tt * D + hc * 512:tt * D + (hc + 1) * 512],
                                        ps_o2[:, tt * 512:(tt + 1) * 512])
                ins.then_inc(s["dev"].h, 1)
                s["dev"].inc()

        @block.tensor
        def _(t):
            t.wait_ge(s["gse"].h, 2)
            for hc in range(8):
                sl2 = hc % 2
                t.wait_ge(s["dzs"][sl2].h, 16 * (hc // 2 + 1))
                if hc >= 1:
                    t.wait_ge(s["dev"].h, hc)
                for ci in range(NCI):
                    rows = rows_of(ci)
                    for j in range(8):
                        n = nidx(hc, ci, j)
                        sl = n % 4
                        t.wait_ge(s["ddq"].h, 17 * (hc * NCI + ci) + 2 * j + 3)
                        t.wait_ge(s["dht"][sl].h, 16 * (n // 4 + 1))
                        for tt in range(8):
                            ins = t.matmul(ps_o2[:, tt * 512:(tt + 1) * 512],
                                           dht_sl[0:rows, sl * 1024 + tt * 128:sl * 1024 + (tt + 1) * 128],
                                           wd_sb[0:rows, sl * 512:(sl + 1) * 512],
                                           start=(ci == 0 and j == 0), stop=False)
                        ins.then_inc(s["dtm"].h, 1)
                        s["dtm"].inc()
                for tt in range(8):
                    ins = t.matmul(ps_o2[:, tt * 512:(tt + 1) * 512],
                                   gs_sb[0:86, tt * 128:(tt + 1) * 128],
                                   dzs_sl[0:86, sl2 * 512:(sl2 + 1) * 512],
                                   start=False, stop=True)
                ins.then_inc(s["dmm"].h, 1)
                s["dmm"].inc()


def _p7_final(nc, io, C, s, o2_sb):
    for k in ["f_v", "f_a", "f_yl", "f_st"]:
        s[k] = Sem(nc, "s_" + k)
    NV = 8

    with (
        nc.sbuf_tensor("y2_tile", [128, D], BF16) as y2_tile,
        nc.sbuf_tensor("z2_sb", [128, D], F32) as z2_sb,
        nc.sbuf_tensor("sq2_sb", [128, D], F32) as sq2_sb,
        nc.sbuf_tensor("out_t", [128, D], F32) as out_t,
        nc.sbuf_tensor("ln2_sc", [128, 8], F32) as ln2_sc,
        nc.Block("p7") as block,
    ):
        S1, S2, M, M2, VE, STD, RSTD, NEGMR = range(8)

        @block.sync
        def _(sy):
            sy.wait_ge(s["dmm"].h, 8)
            for tt in range(8):
                if tt > 0:
                    sy.wait_ge(s["f_v"].h, (tt - 1) * NV + 1)
                sy.dma_start(out=y2_tile[:, :],
                             in_=io["y_dram"].ap()[tt * 128:(tt + 1) * 128, :]
                             ).then_inc(s["f_yl"].h, 16)
                s["f_yl"].inc(16)
                sy.wait_ge(s["f_v"].h, tt * NV + 8)
                sy.dma_start(out=io["out_my"][tt * 128:(tt + 1) * 128, :],
                             in_=out_t[:, :]).then_inc(s["f_st"].h, 16)
                s["f_st"].inc(16)
            sy.wait_ge(s["f_st"].h, 16 * 8)

        @block.vector
        def _(v):
            v.wait_ge(s["dev"].h, 8)
            for tt in range(8):
                base = tt * NV
                abase = tt * 2
                v.wait_ge(s["f_yl"].h, 16 * (tt + 1))
                v.tensor_tensor(z2_sb[:, :], y2_tile[:, :],
                                o2_sb[:, tt * D:(tt + 1) * D], ALU.add).then_inc(s["f_v"].h, 1)
                v.wait_ge(s["f_v"].h, base + 1)
                v.reduce_sum(ln2_sc[:, S1:S1 + 1], z2_sb[:, :], axis=AX.X).then_inc(s["f_v"].h, 1)
                v.wait_ge(s["f_v"].h, base + 2)
                v.tensor_scalar(ln2_sc[:, M:M + 1], ln2_sc[:, S1:S1 + 1],
                                1.0 / D, None, ALU.mult).then_inc(s["f_v"].h, 1)
                v.wait_ge(s["f_v"].h, base + 3)
                v.tensor_scalar(ln2_sc[:, M2:M2 + 1], ln2_sc[:, M:M + 1],
                                ln2_sc[:, M:M + 1], -1e-5,
                                ALU.mult, ALU.add).then_inc(s["f_v"].h, 1)
                v.wait_ge(s["f_v"].h, base + 4)
                v.wait_ge(s["f_a"].h, abase + 1)
                v.scalar_tensor_tensor(ln2_sc[:, VE:VE + 1], ln2_sc[:, S2:S2 + 1],
                                       1.0 / D, ln2_sc[:, M2:M2 + 1],
                                       ALU.mult, ALU.subtract).then_inc(s["f_v"].h, 1)
                v.wait_ge(s["f_a"].h, abase + 2)
                v.reciprocal(ln2_sc[:, RSTD:RSTD + 1],
                             ln2_sc[:, STD:STD + 1]).then_inc(s["f_v"].h, 1)
                v.wait_ge(s["f_v"].h, base + 6)
                v.scalar_tensor_tensor(ln2_sc[:, NEGMR:NEGMR + 1], ln2_sc[:, M:M + 1],
                                       -1.0, ln2_sc[:, RSTD:RSTD + 1],
                                       ALU.mult, ALU.mult).then_inc(s["f_v"].h, 1)
                v.wait_ge(s["f_v"].h, base + 7)
                if tt > 0:
                    v.wait_ge(s["f_st"].h, 16 * tt)
                v.tensor_scalar(out_t[:, :], z2_sb[:, :],
                                ln2_sc[:, RSTD:RSTD + 1], ln2_sc[:, NEGMR:NEGMR + 1],
                                ALU.mult, ALU.add).then_inc(s["f_v"].h, 1)

        @block.scalar
        def _(sc):
            for tt in range(8):
                base = tt * NV
                sc.wait_ge(s["f_v"].h, base + 1)
                sc.activation(sq2_sb[:, :], z2_sb[:, :], AF.Square,
                              accum_out=ln2_sc[:, S2:S2 + 1]).then_inc(s["f_a"].h, 1)
                sc.wait_ge(s["f_v"].h, base + 5)
                sc.activation(ln2_sc[:, STD:STD + 1], ln2_sc[:, VE:VE + 1],
                              AF.Sqrt).then_inc(s["f_a"].h, 1)
                s["f_a"].inc(2)


def _shard_inputs(inputs):
    x = np.ascontiguousarray(np.asarray(inputs["L0_attn_x"]).reshape(T, D))
    wq = np.asarray(inputs["L0_attn_wq"])
    wk = np.asarray(inputs["L0_attn_wk"])
    wv = np.asarray(inputs["L0_attn_wv"])
    maps = []
    for c in range(NC):
        m = {
            "x_my": np.ascontiguousarray(x[c * MY:(c + 1) * MY]),
            "wq_my": np.ascontiguousarray(wq[:, c * HPC * HD:(c + 1) * HPC * HD]),
            "wk_my": np.ascontiguousarray(wk[:, c * HD:(c + 1) * HD]),
            "wv_my": np.ascontiguousarray(wv[:, c * HD:(c + 1) * HD]),
            "wo": np.asarray(inputs["L0_attn_wo"]),
            "g_qw": np.asarray(inputs["L0_ffn_gate_qweight"]),
            "g_sc": np.asarray(inputs["L0_ffn_gate_scales"]),
            "g_qz": np.asarray(inputs["L0_ffn_gate_qzeros"]),
            "u_qw": np.asarray(inputs["L0_ffn_up_qweight"]),
            "u_sc": np.asarray(inputs["L0_ffn_up_scales"]),
            "u_qz": np.asarray(inputs["L0_ffn_up_qzeros"]),
            "d_qw": np.asarray(inputs["L0_ffn_down_qweight"]),
            "d_sc": np.asarray(inputs["L0_ffn_down_scales"]),
            "d_qz": np.asarray(inputs["L0_ffn_down_qzeros"]),
        }
        gidx = np.zeros((128, 32), dtype=np.int32)
        for chunk in range(2):
            for t in range(16):
                p = np.arange(128)
                gidx[:, chunk * 16 + t] = (t * 128 + p) * NC + c
        m["gidx"] = gidx
        maps.append(m)
    return maps


def kernel(**inputs):
    nc, _ = build_program(dbg=False)
    maps = _shard_inputs(inputs)
    res = run_bass_kernel_spmd(nc, maps, core_ids=list(range(NC)))
    out = np.concatenate([r["out_my"] for r in res.results], axis=0)
    return out.reshape(B, S, D)


# version 10 (final)
# speedup vs baseline: 1.0959x; 1.0159x over previous
import numpy as np
import ml_dtypes
from contextlib import ExitStack
import concourse.bass as bass
import concourse.mybir as mybir
from concourse.bass import IndirectOffsetOnAxis
from concourse.bass_utils import run_bass_kernel_spmd

F32 = mybir.dt.float32
F32R = mybir.dt.float32r
BF16 = mybir.dt.bfloat16
I32 = mybir.dt.int32
AF = mybir.ActivationFunctionType
ALU = mybir.AluOpType
AX = mybir.AxisListType

NC = 8
B, S, D = 2, 4096, 4096
T = B * S
MY = T // NC
HD = 128
HPC = 4
INTER = 11008
SCALE = HD ** -0.5
NDMT = D // 128
N_IT = INTER // 128
bf16 = ml_dtypes.bfloat16
NO_Z = False
NO_GST = False
P5_LIMIT = None
NO_AUX = False
NO_TR = False


class Sem:
    def __init__(self, nc, name):
        self.h = nc.alloc_semaphore(name)
        self.n = 0

    def inc(self, k=1):
        self.n += k
        return self.n


def build_program(dbg=False):
    nc = bass.Bass("TRN2", target_bir_lowering=False)

    io = {}
    def inp(name, shape, dt):
        io[name] = nc.dram_tensor(name, shape, dt, kind="ExternalInput").ap()
    inp("x_my", [MY, D], F32)
    inp("wq_my", [D, HPC * HD], F32)
    inp("wk_my", [D, HD], F32)
    inp("wv_my", [D, HD], F32)
    inp("wo", [D, D], F32)
    inp("g_qw", [D // 8, INTER], I32)
    inp("g_sc", [D // 128, INTER], F32)
    inp("g_qz", [D // 128, INTER // 8], I32)
    inp("u_qw", [D // 8, INTER], I32)
    inp("u_sc", [D // 128, INTER], F32)
    inp("u_qz", [D // 128, INTER // 8], I32)
    inp("d_qw", [INTER // 8, D], I32)
    inp("d_sc", [INTER // 128, D], F32)
    inp("d_qz", [INTER // 128, D // 8], I32)
    inp("gidx", [128, 32], I32)
    out_my = nc.dram_tensor("out_my", [MY, D], F32, kind="ExternalOutput").ap()
    io["out_my"] = out_my

    xt_contrib = [nc.dram_tensor(f"xt_contrib{i}", [D, MY // 4], BF16) for i in range(4)]
    xt_full = [nc.dram_tensor(f"xt_full{i}", [NC * D, MY // 4], BF16) for i in range(4)]
    at_contrib = [[nc.dram_tensor(f"at_contrib{h}_{b}", [HD, S], BF16) for b in range(2)]
                  for h in range(4)]
    at_full = [nc.dram_tensor(f"at_full{h}", [2 * NC * HD, S], BF16) for h in range(4)]
    o_dram = nc.dram_tensor("o_dram", [MY, D], BF16)
    zs_gu_dram = nc.dram_tensor("zs_gu_dram", [32, 2 * INTER], F32R)
    zs_d_dram = nc.dram_tensor("zs_d_dram", [INTER // 128, D], F32R)
    y_dram = nc.dram_tensor("y_dram", [MY, D], BF16)
    hT_dram = nc.dram_tensor("hT_dram", [INTER, MY], BF16)
    io.update(xt_contrib=xt_contrib, xt_full=xt_full, at_contrib=at_contrib,
              at_full=at_full, o_dram=o_dram, y_dram=y_dram, hT_dram=hT_dram,
              zs_gu_dram=zs_gu_dram, zs_d_dram=zs_d_dram)

    C = {}
    C["ident_bf"] = nc.inline_tensor(np.eye(128, dtype=bf16), "ident_bf").ap()
    C["ident_f32"] = nc.inline_tensor(np.eye(128, dtype=np.float32), "ident_f32").ap()
    C["ones_col"] = nc.inline_tensor(np.ones((128, 1), dtype=bf16), "ones_col").ap()
    C["ones_row_f32"] = nc.inline_tensor(np.ones((1, 128), dtype=np.float32), "ones_row_f32").ap()
    m = np.zeros((128, 4 * 512), dtype=bf16)
    for o in range(4):
        kk = np.arange(128)[:, None] + o * 128
        qq = np.arange(512)[None, :]
        m[:, o * 512:(o + 1) * 512] = (kk <= qq).astype(bf16)
    C["masks"] = nc.inline_tensor(m, "masks").ap()
    e = np.zeros((128, 4 * 32), dtype=bf16)
    for c4 in range(4):
        for p in range(128):
            e[p, c4 * 32 + 8 * c4 + p // 16] = 1
    C["e_cmats"] = nc.inline_tensor(e, "e_cmats").ap()

    dbg_out = {}
    if dbg == "qkv":
        for name, shape, dt in [
            ("dbg_qT", [128, HPC * T], BF16),
            ("dbg_kT", [128, T], BF16),
            ("dbg_v", [128, T], BF16),
        ]:
            dbg_out[name] = nc.dram_tensor(name, shape, dt, kind="ExternalOutput").ap()
    elif dbg == "p2":
        dbg_out["dbg_at0"] = nc.dram_tensor("dbg_at0", [2 * NC * HD, S], BF16,
                                            kind="ExternalOutput").ap()
        dbg_out["dbg_at1"] = nc.dram_tensor("dbg_at1", [2 * NC * HD, S], BF16,
                                            kind="ExternalOutput").ap()
    elif dbg == "p5a":
        dbg_out["dbg_zs"] = nc.dram_tensor("dbg_zs", [32, 2 * INTER], F32R, kind="ExternalOutput").ap()
        dbg_out["dbg_zsd"] = nc.dram_tensor("dbg_zsd", [INTER // 128, D], F32R, kind="ExternalOutput").ap()
    elif dbg == "p5":
        dbg_out["dbg_hT"] = nc.dram_tensor("dbg_hT", [INTER, MY], BF16, kind="ExternalOutput").ap()
        dbg_out["dbg_gs"] = nc.dram_tensor("dbg_gs", [86, MY], F32R, kind="ExternalOutput").ap()
        dbg_out["dbg_y2"] = nc.dram_tensor("dbg_y2", [MY, D], BF16, kind="ExternalOutput").ap()
        dbg_out["dbg_zs"] = nc.dram_tensor("dbg_zs", [32, 2 * INTER], F32R, kind="ExternalOutput").ap()
        dbg_out["dbg_zsd"] = nc.dram_tensor("dbg_zsd", [INTER // 128, D], F32R, kind="ExternalOutput").ap()
    elif dbg == "p4":
        dbg_out["dbg_at0"] = nc.dram_tensor("dbg_at0", [2 * NC * HD, S], BF16,
                                            kind="ExternalOutput").ap()
        dbg_out["dbg_at1"] = nc.dram_tensor("dbg_at1", [2 * NC * HD, S], BF16,
                                            kind="ExternalOutput").ap()
        dbg_out["dbg_o"] = nc.dram_tensor("dbg_o", [MY, D], BF16, kind="ExternalOutput").ap()
        dbg_out["dbg_y"] = nc.dram_tensor("dbg_y", [MY, D], BF16, kind="ExternalOutput").ap()
        dbg_out["dbg_yg"] = nc.dram_tensor("dbg_yg", [32, MY], F32R, kind="ExternalOutput").ap()

    s = {k: Sem(nc, "s_" + k) for k in
         ["tp", "tpe", "ag", "w", "mmq", "mmkv", "evq", "evkv", "const"]}
    s["x"] = [Sem(nc, f"s_x{i}") for i in range(2)]
    s["ct"] = [Sem(nc, f"s_ct{i}") for i in range(4)]
    s["xt"] = [Sem(nc, f"s_xt{i}") for i in range(3)]

    with (
        nc.sbuf_tensor("cst", [128, 3456], BF16) as cst,
        nc.sbuf_tensor("cst32", [128, 256], F32) as cst32,
    ):
        SC = {
            "ident_bf": cst[:, 0:128],
            "ones_col": cst[:, 128:129],
            "masks": cst[:, 256:256 + 2048],
            "e_cmats": cst[:, 2304:2304 + 128],
            "ident_f32": cst32[:, 0:128],
            "ones_row_f32": cst32[0:1, 128:256],
        }
        with nc.Block("pconst") as block:
            @block.sync
            def _(sy):
                sy.dma_start(out=SC["ident_bf"], in_=C["ident_bf"]).then_inc(s["const"].h, 16)
                sy.dma_start(out=SC["ones_col"], in_=C["ones_col"]).then_inc(s["const"].h, 16)
                sy.dma_start(out=SC["masks"], in_=C["masks"]).then_inc(s["const"].h, 16)
                sy.dma_start(out=SC["e_cmats"], in_=C["e_cmats"]).then_inc(s["const"].h, 16)
                sy.dma_start(out=SC["ident_f32"], in_=C["ident_f32"]).then_inc(s["const"].h, 16)
                sy.dma_start(out=SC["ones_row_f32"], in_=C["ones_row_f32"]).then_inc(s["const"].h, 16)
                s["const"].inc(96)
        C = dict(C)
        C.update(SC)
        _build_main(nc, io, C, s, dbg, dbg_out)
    return nc, io


def _build_main(nc, io, C, s, dbg, dbg_out):
    with (
        nc.sbuf_tensor("qT", [128, HPC * T], BF16) as qT,
        nc.sbuf_tensor("kT", [128, T], BF16) as kT,
        nc.sbuf_tensor("v_sb", [128, T], BF16) as v_sb,
    ):
        with (
            nc.sbuf_tensor("wq_sb", [128, NDMT * HPC * HD], BF16) as wq_sb,
            nc.sbuf_tensor("wkv_sb", [128, NDMT * 2 * HD], BF16) as wkv_sb,
        ):
            _p0_transpose_gather(nc, io, C, s, wq_sb, wkv_sb)
            _p1_qkv(nc, io, C, s, qT, kT, v_sb, wq_sb, wkv_sb)
        _p2_attention(nc, io, C, s, qT, kT, v_sb)
        if dbg == "qkv":
            _p_dbg_qkv(nc, s, dbg_out, qT, kT, v_sb)
            return
    if dbg == "p2":
        _p_dbg_p2(nc, io, s, dbg_out)
        return
    with (
        nc.sbuf_tensor("yT", [128, NDMT * MY], BF16) as yT,
        nc.sbuf_tensor("ygT_sb", [32, MY], F32R) as ygT_sb,
    ):
        with (
            nc.sbuf_tensor("qz_sb", [86, 172], I32) as qz_sb,
            nc.sbuf_tensor("z4_sb", [86, 1376], I32) as z4_sb,
            nc.sbuf_tensor("tmp_sb", [86, 1376], F32) as tmp_sb,
            nc.sbuf_tensor("szs_sb", [86, 1376], F32) as szs_sb,
            nc.sbuf_tensor("zs_out", [86, 1376], F32R) as zs_out,
        ):
            _p5pre_zeros(nc, io, C, s, qz_sb, z4_sb, tmp_sb, szs_sb, zs_out)
            with nc.sbuf_tensor("at_my", [128, NDMT * MY], BF16) as at_my:
                _p3_oproj(nc, io, C, s, at_my)
            _p4_postln1(nc, io, C, s, yT, ygT_sb)
        if dbg == "p4":
            _p_dbg_p4(nc, io, s, dbg_out, ygT_sb)
            return
        with nc.sbuf_tensor("gs_sb", [86, 1024], F32R) as gs_sb:
            if dbg == "p5a":
                _p_dbg_p5a(nc, io, s, dbg_out)
                return
            _p5_gateup(nc, io, C, s, yT, ygT_sb, gs_sb)
            if dbg == "p5":
                _p_dbg_p5(nc, io, s, dbg_out, gs_sb)
                return
            with nc.sbuf_tensor("o2_sb", [128, 8 * D], BF16) as o2_sb:
                _p6_down(nc, io, C, s, gs_sb, o2_sb)
                _p7_final(nc, io, C, s, o2_sb)


def _p_dbg_p5a(nc, io, s, dbg_out):
    sd = Sem(nc, "s_dbg")
    with nc.Block("pdbg5a") as block:
        @block.sync
        def _(sy):
            sy.wait_ge(s["zst"].h, 16 * 20)
            sy.wait_ge(s["yst"].h, 16 * 8)
            sy.dma_start(out=dbg_out["dbg_zs"], in_=io["zs_gu_dram"].ap()).then_inc(sd.h, 16)
            sy.dma_start(out=dbg_out["dbg_zsd"], in_=io["zs_d_dram"].ap()).then_inc(sd.h, 16)
            sy.wait_ge(sd.h, 32)


def _p_dbg_p5(nc, io, s, dbg_out, gs_sb):
    sd = Sem(nc, "s_dbg")
    with nc.Block("pdbg5") as block:
        @block.sync
        def _(sy):
            for sl in range(2):
                sy.wait_ge(s["hst"][sl].h, s["hst"][sl].n)
            sy.wait_ge(s["gse"].h, s["gse"].n)
            sy.wait_ge(s["yst"].h, 16 * 8)
            sy.dma_start(out=dbg_out["dbg_hT"], in_=io["hT_dram"].ap()).then_inc(sd.h, 16)
            sy.dma_start(out=dbg_out["dbg_gs"], in_=gs_sb[0:86, :]).then_inc(sd.h, 16)
            sy.dma_start(out=dbg_out["dbg_y2"], in_=io["y_dram"].ap()).then_inc(sd.h, 16)
            sy.dma_start(out=dbg_out["dbg_zs"], in_=io["zs_gu_dram"].ap()).then_inc(sd.h, 16)
            sy.dma_start(out=dbg_out["dbg_zsd"], in_=io["zs_d_dram"].ap()).then_inc(sd.h, 16)
            sy.wait_ge(sd.h, 80)


def _p_dbg_p4(nc, io, s, dbg_out, ygT_sb):
    sd = Sem(nc, "s_dbg")
    with nc.Block("pdbg4") as block:
        @block.vector
        def _(v):
            v.wait_ge(s["yge"].h, 2)
            v.engine_nop().then_inc(sd.h, 1)

        @block.sync
        def _(sy):
            sy.wait_ge(s["ag2"].h, 8)
            sy.wait_ge(s["yst"].h, 16 * 8)
            sy.wait_ge(sd.h, 1)
            sy.dma_start(out=dbg_out["dbg_at0"], in_=io["at_full"][0].ap()).then_inc(sd.h, 16)
            sy.dma_start(out=dbg_out["dbg_at1"], in_=io["at_full"][1].ap()).then_inc(sd.h, 16)
            sy.dma_start(out=dbg_out["dbg_o"], in_=io["o_dram"].ap()).then_inc(sd.h, 16)
            sy.dma_start(out=dbg_out["dbg_y"], in_=io["y_dram"].ap()).then_inc(sd.h, 16)
            sy.dma_start(out=dbg_out["dbg_yg"], in_=ygT_sb[:, :]).then_inc(sd.h, 16)
            sy.wait_ge(sd.h, 81)


def _p0_transpose_gather(nc, io, C, s, wq_sb, wkv_sb):
    with (
        nc.sbuf_tensor("x_my_bf", [128, 2 * D], BF16) as x_my_bf,
        nc.sbuf_tensor("tp_st", [128, 4 * 512], BF16) as tp_st,
        nc.psum_tensor("ps_tpA", [128, 1024], BF16) as ps_tpA,
        nc.psum_tensor("ps_tpB", [128, 1024], BF16) as ps_tpB,
        nc.Block("p0") as block,
    ):
        @block.gpsimd
        def _(g):
            for tt in range(8):
                if tt >= 2:
                    g.wait_ge(s["tp"].h, (tt - 1) * 8)
                sx = s["x"][tt % 2]
                g.dma_start(out=x_my_bf[:, (tt % 2) * D:(tt % 2 + 1) * D],
                            in_=io["x_my"][tt * 128:(tt + 1) * 128, :]).then_inc(sx.h, 16)
                sx.inc(16)
            g.dma_start(out=wq_sb[:, :].rearrange("p (d h) -> p d h", d=NDMT),
                        in_=io["wq_my"].rearrange("(d p) h -> p d h", p=128)).then_inc(s["w"].h, 16)
            g.dma_start(out=wkv_sb[:, 0:NDMT * HD].rearrange("p (d h) -> p d h", d=NDMT),
                        in_=io["wk_my"].rearrange("(d p) h -> p d h", p=128)).then_inc(s["w"].h, 16)
            g.dma_start(out=wkv_sb[:, NDMT * HD:2 * NDMT * HD].rearrange("p (d h) -> p d h", d=NDMT),
                        in_=io["wv_my"].rearrange("(d p) h -> p d h", p=128)).then_inc(s["w"].h, 16)
            s["w"].inc(48)
            for q in range(4):
                for m4 in range(4):
                    g.wait_ge(s["ct"][m4].h, 16 * 4 * (q + 1))
                g.collective_compute(
                    "AllGather", ALU.bypass,
                    replica_groups=[list(range(NC))],
                    ins=[io["xt_contrib"][q].ap().opt()],
                    outs=[io["xt_full"][q].ap().opt()],
                ).then_inc(s["ag"].h, 1)
                s["ag"].inc()

        @block.tensor
        def _(t):
            t.wait_ge(s["const"].h, 96)
            n = 0
            for tt in range(8):
                t.wait_ge(s["x"][tt % 2].h, 16 * (tt // 2 + 1))
                for d4 in range(8):
                    pp = n % 2
                    ps_tp = ps_tpA if pp == 0 else ps_tpB
                    if n >= 2:
                        t.wait_ge(s["tpe"].h, n - 1)
                    for k in range(4):
                        d = d4 * 4 + k
                        ins = t.transpose(ps_tp[:, k * 128:(k + 1) * 128],
                                          x_my_bf[:, (tt % 2) * D + d * 128: (tt % 2) * D + (d + 1) * 128],
                                          C["ident_bf"])
                        if k == 3:
                            ins.then_inc(s["tp"].h, 1)
                            s["tp"].inc()
                    n += 1

        @block.vector
        def _(v):
            for g8 in range(64):
                pp = g8 % 2
                m4 = g8 % 4
                ps_tp = ps_tpA if pp == 0 else ps_tpB
                v.wait_ge(s["tp"].h, g8 + 1)
                if g8 >= 4:
                    v.wait_ge(s["ct"][m4].h, 16 * (g8 // 4))
                v.tensor_copy(tp_st[:, m4 * 512:(m4 + 1) * 512],
                              ps_tp[:, 0:512]).then_inc(s["tpe"].h, 1)
                s["tpe"].inc()

        @block.sync
        def _(sy):
            n = 0
            for tt in range(8):
                for d4 in range(8):
                    m4 = n % 4
                    n += 1
                    sy.wait_ge(s["tpe"].h, n)
                    sct = s["ct"][m4]
                    sy.dma_start(
                        out=io["xt_contrib"][tt // 2].ap().rearrange("(d p) t -> p d t", p=128)
                            [:, d4 * 4:(d4 + 1) * 4, (tt % 2) * 128:(tt % 2 + 1) * 128],
                        in_=tp_st[:, m4 * 512:(m4 + 1) * 512]
                            .rearrange("p (d t) -> p d t", d=4),
                    ).then_inc(sct.h, 16)
                    sct.inc(16)


def _p1_qkv(nc, io, C, s, qT, kT, v_sb, wq_sb, wkv_sb):
    with (
        nc.sbuf_tensor("xt_buf", [128, 3 * 16 * 512], BF16) as xt_buf,
        nc.psum_tensor("ps_q", [128, 4 * 512], F32) as ps_q,
        nc.psum_tensor("ps_kv0", [128, 2 * 512], F32) as ps_kv0,
        nc.psum_tensor("ps_kv1", [128, 2 * 512], F32) as ps_kv1,
        nc.Block("p1") as block,
    ):
        xtv = [io["xt_full"][i].ap().rearrange("(r d p) t -> r d p t", d=NDMT, p=128)
               for i in range(4)]

        @block.sync
        def _(sy):
            nb = 0
            for ch in range(16):
                r, half = ch // 2, ch % 2
                for hh in range(2):
                    b = nb % 3
                    if nb >= 3:
                        sy.wait_ge(s["mmkv"].h, (nb - 3) // 2 + 1)
                    sxt = s["xt"][b]
                    dst = xt_buf[:, b * 8192:(b + 1) * 8192] \
                        .rearrange("p (d t) -> p d t", d=16)
                    for qq in range(2):
                        q = 2 * half + qq
                        sy.wait_ge(s["ag"].h, q + 1)
                        sy.dma_start(
                            out=dst[:, :, qq * 256:(qq + 1) * 256],
                            in_=xtv[q][r, hh * 16:(hh + 1) * 16, :, :]
                                .transpose([1, 0, 2]),
                        ).then_inc(sxt.h, 16)
                        sxt.inc(16)
                    nb += 1

        @block.tensor
        def _(t):
            t.wait_ge(s["w"].h, 48)
            nb = 0
            for ch in range(16):
                for hh in range(2):
                    b = nb % 3
                    nb += 1
                    t.wait_ge(s["xt"][b].h, 32 * ((nb - 1) // 3 + 1))
                    if hh == 0 and ch > 0:
                        t.wait_ge(s["evq"].h, ch)
                    for d16 in range(16):
                        d = hh * 16 + d16
                        xs = xt_buf[:, b * 8192 + d16 * 512:b * 8192 + (d16 + 1) * 512]
                        for h in range(HPC):
                            ins = t.matmul(ps_q[:, h * 512:(h + 1) * 512],
                                           wq_sb[:, (d * HPC + h) * HD:(d * HPC + h + 1) * HD],
                                           xs, start=(d == 0), stop=(d == NDMT - 1))
                            if d == NDMT - 1 and h == HPC - 1:
                                ins.then_inc(s["mmq"].h, 1)
                                s["mmq"].inc()
                ps_kv = ps_kv0 if ch % 2 == 0 else ps_kv1
                if ch >= 2:
                    t.wait_ge(s["evkv"].h, ch - 1)
                for hh in range(2):
                    b = (nb - 2 + hh) % 3
                    for d16 in range(16):
                        d = hh * 16 + d16
                        xs = xt_buf[:, b * 8192 + d16 * 512:b * 8192 + (d16 + 1) * 512]
                        t.matmul(ps_kv[:, 0:512],
                                 wkv_sb[:, d * HD:(d + 1) * HD],
                                 xs, start=(d == 0), stop=(d == NDMT - 1))
                        for q4 in range(4):
                            ins = t.matmul(ps_kv[:, 512 + q4 * 128:512 + (q4 + 1) * 128],
                                           xs[:, q4 * 128:(q4 + 1) * 128],
                                           wkv_sb[:, NDMT * HD + d * HD:NDMT * HD + (d + 1) * HD],
                                           start=(d == 0 and q4 == 0), stop=(d == NDMT - 1 and q4 == 3))
                            if d == NDMT - 1 and q4 == 3:
                                ins.then_inc(s["mmkv"].h, 1)
                                s["mmkv"].inc()

        @block.vector
        def _(v):
            for ch in range(16):
                v.wait_ge(s["mmq"].h, ch + 1)
                for h in range(HPC):
                    ins = v.tensor_scalar(qT[:, h * T + ch * 512:h * T + (ch + 1) * 512],
                                          ps_q[:, h * 512:(h + 1) * 512],
                                          SCALE, None, ALU.mult)
                    if h == HPC - 1:
                        ins.then_inc(s["evq"].h, 1)
                        s["evq"].inc()

        @block.scalar
        def _(sc):
            for ch in range(16):
                ps_kv = ps_kv0 if ch % 2 == 0 else ps_kv1
                sc.wait_ge(s["mmkv"].h, ch + 1)
                sc.copy(kT[:, ch * 512:(ch + 1) * 512], ps_kv[:, 0:512])
                sc.copy(v_sb[:, ch * 512:(ch + 1) * 512],
                        ps_kv[:, 512:1024]).then_inc(s["evkv"].h, 1)
                s["evkv"].inc()


def _p2_attention(nc, io, C, s, qT, kT, v_sb):
    for k in ["sc", "exp", "msk", "pav", "lav", "rec", "bca", "lrbe", "avv"]:
        s[k] = Sem(nc, "s_" + k)
    s["avst"] = [Sem(nc, f"s_avst{i}") for i in range(2)]
    s["ag2"] = Sem(nc, "s_ag2")

    groups = [(h, b, g) for h in range(HPC) for b in range(B) for g in range(8)]
    npairs = [2 * (g + 1) for (_, _, g) in groups]
    ppre = [0]
    for np_ in npairs:
        ppre.append(ppre[-1] + np_)

    with (
        nc.sbuf_tensor("probs", [128, 3 * 1024], BF16) as probs,
        nc.sbuf_tensor("lrec", [1, 512], F32) as lrec,
        nc.sbuf_tensor("lrb_sb", [128, 512], F32) as lrb_sb,
        nc.sbuf_tensor("av_sb", [128, 2 * 512], BF16) as av_sb,
        nc.psum_tensor("ps_sc0", [128, 1024], F32) as ps_sc0,
        nc.psum_tensor("ps_sc1", [128, 1024], F32) as ps_sc1,
        nc.psum_tensor("ps_sc2", [128, 1024], F32) as ps_sc2,
        nc.psum_tensor("ps_av", [128, 512], F32) as ps_av,
        nc.psum_tensor("ps_l", [1, 512], F32) as ps_l,
        nc.Block("p2") as block,
    ):
        @block.tensor
        def _(t):
            def emit_qk(gi, kp):
                h, b, g = groups[gi]
                P = ppre[gi] + kp
                qs = qT[:, h * T + b * S + g * 512: h * T + b * S + (g + 1) * 512]
                ps = [ps_sc0, ps_sc1, ps_sc2][P % 3]
                if kp == 0:
                    t.wait_ge(s["lrbe"].h, gi)
                if P >= 3:
                    t.wait_ge(s["exp"].h, P - 2)
                for j in range(2):
                    kt = kp * 2 + j
                    ins = t.matmul(ps[:, j * 512:(j + 1) * 512],
                                   kT[:, b * S + kt * 128:b * S + (kt + 1) * 128],
                                   qs, start=True, stop=True)
                ins.then_inc(s["sc"].h, 1)
                s["sc"].inc()

            def emit_lav(gi, kp):
                h, b, g = groups[gi]
                npair = npairs[gi]
                nkt = 4 * (g + 1)
                P = ppre[gi] + kp
                d = kp - (npair - 2)
                if kp == 0:
                    t.wait_ge(s["rec"].h, gi)
                    t.wait_ge(s["avv"].h, gi)
                t.wait_ge(s["exp"].h, P + 1)
                if d >= 0:
                    t.wait_ge(s["msk"].h, 4 * gi + 2 * (d + 1))
                pb = probs[:, (P % 3) * 1024:(P % 3 + 1) * 1024]
                for j in range(2):
                    kt = kp * 2 + j
                    pc = pb[:, j * 512:(j + 1) * 512]
                    t.matmul(ps_l[0:1, :], C["ones_col"], pc,
                             start=(kt == 0), stop=(kt == nkt - 1))
                    ins = t.matmul(ps_av[:, :],
                                   v_sb[:, (b * 32 + kt) * 128:(b * 32 + kt + 1) * 128],
                                   pc, start=(kt == 0), stop=(kt == nkt - 1))
                ins.then_inc(s["pav"].h, 1)
                s["pav"].inc()

            for gi, (h, b, g) in enumerate(groups):
                npair = npairs[gi]
                for kp in range(npair):
                    emit_qk(gi, kp)
                    if kp >= 2:
                        emit_lav(gi, kp - 2)
                emit_lav(gi, npair - 2)
                emit_lav(gi, npair - 1)
                t.wait_ge(s["rec"].h, gi + 1)
                t.matmul(ps_sc0[:, 0:512], C["ones_row_f32"], lrec[0:1, :],
                         start=True, stop=True).then_inc(s["bca"].h, 1)
                s["bca"].inc()

        @block.scalar
        def _(sc):
            for gi, (h, b, g) in enumerate(groups):
                npair = npairs[gi]
                for kp in range(npair):
                    P = ppre[gi] + kp
                    ps = [ps_sc0, ps_sc1, ps_sc2][P % 3]
                    sc.wait_ge(s["sc"].h, P + 1)
                    if P >= 3:
                        sc.wait_ge(s["pav"].h, P - 2)
                    sc.activation(probs[:, (P % 3) * 1024:(P % 3 + 1) * 1024],
                                  ps[:, :], AF.Exp).then_inc(s["exp"].h, 1)
                    s["exp"].inc()

        @block.vector
        def _(v):
            for gi, (h, b, g) in enumerate(groups):
                npair = npairs[gi]
                for d in range(2):
                    kp = npair - 2 + d
                    P = ppre[gi] + kp
                    v.wait_ge(s["exp"].h, P + 1)
                    pb = probs[:, (P % 3) * 1024:(P % 3 + 1) * 1024]
                    for j in range(2):
                        o = 2 * d + j
                        pc = pb[:, j * 512:(j + 1) * 512]
                        v.tensor_tensor(pc, pc, C["masks"][:, o * 512:(o + 1) * 512],
                                        ALU.mult).then_inc(s["msk"].h, 1)
                        s["msk"].inc()
                v.wait_ge(s["pav"].h, ppre[gi + 1])
                v.reciprocal(lrec[0:1, :], ps_l[0:1, :]).then_inc(s["rec"].h, 1)
                s["rec"].inc()
                v.wait_ge(s["bca"].h, gi + 1)
                v.tensor_copy(lrb_sb[:, :], ps_sc0[:, 0:512]).then_inc(s["lrbe"].h, 1)
                s["lrbe"].inc()
                slot = gi % 2
                if gi >= 2:
                    v.wait_ge(s["avst"][slot].h, 16 * (gi // 2))
                v.wait_ge(s["lrbe"].h, gi + 1)
                v.tensor_tensor(av_sb[:, slot * 512:(slot + 1) * 512],
                                ps_av[:, :], lrb_sb[:, :], ALU.mult).then_inc(s["avv"].h, 1)
                s["avv"].inc()

        @block.sync
        def _(sy):
            for gi, (h, b, g) in enumerate(groups):
                slot = gi % 2
                sy.wait_ge(s["avv"].h, gi + 1)
                sy.dma_start(
                    out=io["at_contrib"][h][b].ap()[:, g * 512:(g + 1) * 512],
                    in_=av_sb[:, slot * 512:(slot + 1) * 512],
                ).then_inc(s["avst"][slot].h, 16)
                s["avst"][slot].inc(16)

        @block.gpsimd
        def _(gp):
            for h in range(4):
                for b in range(2):
                    ng = 16 * h + 8 * b + 8
                    gp.wait_ge(s["avst"][0].h, 16 * ((ng + 1) // 2))
                    gp.wait_ge(s["avst"][1].h, 16 * (ng // 2))
                    gp.collective_compute(
                        "AllGather", ALU.bypass,
                        replica_groups=[list(range(NC))],
                        ins=[io["at_contrib"][h][b].ap().opt()],
                        outs=[io["at_full"][h].ap()[b * NC * HD:(b + 1) * NC * HD, :].opt()],
                    ).then_inc(s["ag2"].h, 1)
                    s["ag2"].inc()


def _p_dbg_p2(nc, io, s, dbg_out):
    sd = Sem(nc, "s_dbg")
    with nc.Block("pdbg2") as block:
        @block.sync
        def _(sy):
            sy.wait_ge(s["ag2"].h, 8)
            sy.dma_start(out=dbg_out["dbg_at0"], in_=io["at_full"][0].ap()).then_inc(sd.h, 16)
            sy.dma_start(out=dbg_out["dbg_at1"], in_=io["at_full"][1].ap()).then_inc(sd.h, 16)
            sy.wait_ge(sd.h, 32)


def _p_dbg_qkv(nc, s, dbg_out, qT, kT, v_sb):
    sd = Sem(nc, "s_dbg")
    with nc.Block("pdbg") as block:
        @block.sync
        def _(sy):
            sy.wait_ge(s["evq"].h, 16)
            sy.wait_ge(s["evkv"].h, 16)
            sy.dma_start(out=dbg_out["dbg_qT"], in_=qT[:, :]).then_inc(sd.h, 16)
            sy.dma_start(out=dbg_out["dbg_kT"], in_=kT[:, :]).then_inc(sd.h, 16)
            sy.dma_start(out=dbg_out["dbg_v"], in_=v_sb[:, :]).then_inc(sd.h, 16)
            sy.wait_ge(sd.h, 48)


def _p3_oproj(nc, io, C, s, at_my):
    for k in ["omm", "oev", "gix"]:
        s[k] = Sem(nc, "s_" + k)
    s["gth"] = [Sem(nc, f"s_gth{i}") for i in range(4)]
    s["wo"] = [Sem(nc, f"s_wo{i}") for i in range(2)]
    s["ost"] = [Sem(nc, f"s_ost{i}") for i in range(2)]

    with (
        nc.sbuf_tensor("gix_sb", [128, 32], I32) as gix_sb,
        nc.sbuf_tensor("wo_sl", [128, 2 * 32 * 256], BF16) as wo_sl,
        nc.sbuf_tensor("o_st", [128, 2 * 2048], BF16) as o_st,
        nc.psum_tensor("ps_oA", [128, 2048], F32) as ps_oA,
        nc.psum_tensor("ps_oB", [128, 2048], F32) as ps_oB,
        nc.Block("p3") as block,
    ):
        wov = io["wo"].rearrange("(d p) n -> p d n", p=128)

        @block.sync
        def _(sy):
            sy.dma_start(out=gix_sb[:, :], in_=io["gidx"]).then_inc(s["gix"].h, 16)
            s["gix"].inc(16)
            for hc in range(16):
                sl = hc % 2
                sy.wait_ge(s["oev"].h, hc + 1)
                sy.dma_start(
                    out=io["o_dram"].ap().rearrange("(t p) n -> p t n", p=128)
                        [:, :, hc * 256:(hc + 1) * 256],
                    in_=o_st[:, sl * 2048:(sl + 1) * 2048]
                        .rearrange("p (t n) -> p t n", t=8),
                ).then_inc(s["ost"][sl].h, 16)
                s["ost"][sl].inc(16)

        @block.gpsimd
        def _(g):
            g.wait_ge(s["gix"].h, 16)
            g.wait_ge(s["evq"].h, 16)
            g.wait_ge(s["evkv"].h, 16)
            for chunk in range(4):
                g.wait_ge(s["ag2"].h, 2 * chunk + 2)
                src = io["at_full"][chunk].ap().rearrange("r (tb tk) -> (r tb) tk", tk=MY)
                for cc in range(8):
                    head = 4 * cc + chunk
                    i = chunk * 8 + cc
                    g.indirect_dma_start(
                        at_my[:, head * MY:(head + 1) * MY], None,
                        src, IndirectOffsetOnAxis(ap=gix_sb[:, i:i + 1], axis=0),
                    ).then_inc(s["gth"][chunk].h, 16)
                    s["gth"][chunk].inc(16)
            for hc in range(16):
                sl = hc % 2
                if hc >= 2:
                    g.wait_ge(s["omm"].h, hc - 1)
                g.dma_start(
                    out=wo_sl[:, sl * 8192:(sl + 1) * 8192]
                        .rearrange("p (d n) -> p d n", d=NDMT),
                    in_=wov[:, :, hc * 256:(hc + 1) * 256],
                ).then_inc(s["wo"][sl].h, 16)
                s["wo"][sl].inc(16)

        @block.tensor
        def _(t):
            d_list = [4 * cc + ch for ch in range(4) for cc in range(8)]
            for hc in range(16):
                sl = hc % 2
                ps_o = ps_oA if sl == 0 else ps_oB
                t.wait_ge(s["wo"][sl].h, 16 * (hc // 2 + 1))
                if hc >= 2:
                    t.wait_ge(s["oev"].h, hc - 1)
                for di, d in enumerate(d_list):
                    if di % 8 == 0:
                        t.wait_ge(s["gth"][di // 8].h, 16 * 8)
                    for tt in range(8):
                        ins = t.matmul(
                            ps_o[:, tt * 256:(tt + 1) * 256],
                            at_my[:, d * MY + tt * 128:d * MY + (tt + 1) * 128],
                            wo_sl[:, sl * 8192 + d * 256:sl * 8192 + (d + 1) * 256],
                            start=(di == 0 and tt % 2 == 0),
                            stop=(di == NDMT - 1 and tt % 2 == 1))
                ins.then_inc(s["omm"].h, 1)
                s["omm"].inc()

        @block.vector
        def _(v):
            for hc in range(16):
                sl = hc % 2
                ps_o = ps_oA if sl == 0 else ps_oB
                v.wait_ge(s["omm"].h, hc + 1)
                if hc >= 2:
                    v.wait_ge(s["ost"][sl].h, 16 * (hc // 2))
                ins = v.tensor_copy(o_st[:, sl * 2048:(sl + 1) * 2048], ps_o[:, :])
                ins.then_inc(s["oev"].h, 1)
                s["oev"].inc()


def _p4_postln1(nc, io, C, s, yT, ygT_sb):
    for k in ["lnv", "lna", "yt", "yte", "ygm", "yge", "xl", "ol", "yst"]:
        s[k] = Sem(nc, "s_" + k)
    NV, NA = 8, 2

    with (
        nc.sbuf_tensor("x_tile", [128, D], F32) as x_tile,
        nc.sbuf_tensor("o_tile", [128, D], BF16) as o_tile,
        nc.sbuf_tensor("z_sb", [128, D], F32) as z_sb,
        nc.sbuf_tensor("sq_sb", [128, D], F32) as sq_sb,
        nc.sbuf_tensor("y_tile", [128, D], BF16) as y_tile,
        nc.sbuf_tensor("ln_sc", [128, 8], F32) as ln_sc,
        nc.psum_tensor("ps_ytA", [128, 1024], BF16) as ps_ytA,
        nc.psum_tensor("ps_ytB", [128, 1024], BF16) as ps_ytB,
        nc.psum_tensor("ps_yg", [32, 1024], F32) as ps_yg,
        nc.Block("p4") as block,
    ):
        S1, S2, M, M2, VE, STD, RSTD, NEGMR = range(8)

        @block.sync
        def _(sy):
            for sl in range(2):
                sy.wait_ge(s["ost"][sl].h, 16 * 8)
            for tt in range(8):
                if tt > 0:
                    sy.wait_ge(s["lnv"].h, (tt - 1) * NV + 1)
                sy.dma_start(out=x_tile[:, :],
                             in_=io["x_my"][tt * 128:(tt + 1) * 128, :]).then_inc(s["xl"].h, 16)
                s["xl"].inc(16)
                sy.dma_start(out=o_tile[:, :],
                             in_=io["o_dram"].ap()[tt * 128:(tt + 1) * 128, :]).then_inc(s["ol"].h, 16)
                s["ol"].inc(16)
                sy.wait_ge(s["lnv"].h, tt * NV + 8)
                sy.dma_start(out=io["y_dram"].ap()[tt * 128:(tt + 1) * 128, :],
                             in_=y_tile[:, :]).then_inc(s["yst"].h, 16)
                s["yst"].inc(16)

        @block.vector
        def _(v):
            for tt in range(8):
                base = tt * NV
                abase = tt * NA
                v.wait_ge(s["xl"].h, 16 * (tt + 1))
                v.wait_ge(s["ol"].h, 16 * (tt + 1))

                v.tensor_tensor(z_sb[:, :], x_tile[:, :],
                                o_tile[:, :], ALU.add).then_inc(s["lnv"].h, 1)
                v.wait_ge(s["lnv"].h, base + 1)
                v.reduce_sum(ln_sc[:, S1:S1 + 1], z_sb[:, :], axis=AX.X).then_inc(s["lnv"].h, 1)
                v.wait_ge(s["lnv"].h, base + 2)
                v.tensor_scalar(ln_sc[:, M:M + 1], ln_sc[:, S1:S1 + 1],
                                1.0 / D, None, ALU.mult).then_inc(s["lnv"].h, 1)
                v.wait_ge(s["lnv"].h, base + 3)
                v.tensor_scalar(ln_sc[:, M2:M2 + 1], ln_sc[:, M:M + 1],
                                ln_sc[:, M:M + 1], -1e-5,
                                ALU.mult, ALU.add).then_inc(s["lnv"].h, 1)
                v.wait_ge(s["lnv"].h, base + 4)
                v.wait_ge(s["lna"].h, abase + 1)
                v.scalar_tensor_tensor(ln_sc[:, VE:VE + 1], ln_sc[:, S2:S2 + 1],
                                       1.0 / D, ln_sc[:, M2:M2 + 1],
                                       ALU.mult, ALU.subtract).then_inc(s["lnv"].h, 1)
                v.wait_ge(s["lna"].h, abase + 2)
                v.reciprocal(ln_sc[:, RSTD:RSTD + 1],
                             ln_sc[:, STD:STD + 1]).then_inc(s["lnv"].h, 1)
                v.wait_ge(s["lnv"].h, base + 6)
                v.scalar_tensor_tensor(ln_sc[:, NEGMR:NEGMR + 1], ln_sc[:, M:M + 1],
                                       -1.0, ln_sc[:, RSTD:RSTD + 1],
                                       ALU.mult, ALU.mult).then_inc(s["lnv"].h, 1)
                v.wait_ge(s["lnv"].h, base + 7)
                if tt > 0:
                    v.wait_ge(s["yst"].h, 16 * tt)
                    v.wait_ge(s["yt"].h, tt * 32)
                v.tensor_scalar(y_tile[:, :], z_sb[:, :],
                                ln_sc[:, RSTD:RSTD + 1], ln_sc[:, NEGMR:NEGMR + 1],
                                ALU.mult, ALU.add).then_inc(s["lnv"].h, 1)
                for cj in range(NDMT):
                    n = tt * NDMT + cj
                    pp = n % 2
                    ps_yt = ps_ytA if pp == 0 else ps_ytB
                    v.wait_ge(s["yt"].h, n + 1)
                    v.tensor_copy(yT[:, cj * MY + tt * 128:cj * MY + (tt + 1) * 128],
                                  ps_yt[:, 0:128]).then_inc(s["yte"].h, 1)
                    s["yte"].inc()
            for half in range(2):
                v.wait_ge(s["ygm"].h, half + 1)
                v.tensor_copy(ygT_sb[:, half * 512:(half + 1) * 512],
                              ps_yg[:, half * 512:(half + 1) * 512]).then_inc(s["yge"].h, 1)
                s["yge"].inc()

        @block.scalar
        def _(sc):
            for tt in range(8):
                base = tt * NV
                sc.wait_ge(s["lnv"].h, base + 1)
                sc.activation(sq_sb[:, :], z_sb[:, :], AF.Square,
                              accum_out=ln_sc[:, S2:S2 + 1]).then_inc(s["lna"].h, 1)
                sc.wait_ge(s["lnv"].h, base + 5)
                sc.activation(ln_sc[:, STD:STD + 1], ln_sc[:, VE:VE + 1],
                              AF.Sqrt).then_inc(s["lna"].h, 1)
                s["lna"].inc(2)

        @block.tensor
        def _(t):
            n = 0
            for tt in range(8):
                t.wait_ge(s["lnv"].h, tt * NV + 8)
                for cj in range(NDMT):
                    c4, j = cj // 8, cj % 8
                    pp = n % 2
                    ps_yt = ps_ytA if pp == 0 else ps_ytB
                    if n >= 2:
                        t.wait_ge(s["yte"].h, n - 1)
                    src = y_tile[:, :].rearrange("p (a q e) -> p a q e", a=4, e=8)[:, c4, :, j]
                    ins = t.transpose(ps_yt[:, 0:128], src, C["ident_bf"])
                    ins.then_inc(s["yt"].h, 1)
                    s["yt"].inc()
                    n += 1
            t.wait_ge(s["yte"].h, 256)
            for half in range(2):
                for cj in range(NDMT):
                    c4 = cj // 8
                    ins = t.matmul(
                        ps_yg[:, half * 512:(half + 1) * 512],
                        C["e_cmats"][:, c4 * 32:(c4 + 1) * 32],
                        yT[:, cj * MY + half * 512:cj * MY + (half + 1) * 512],
                        start=(cj == 0), stop=(cj == NDMT - 1))
                ins.then_inc(s["ygm"].h, 1)
                s["ygm"].inc()



def _p5pre_zeros(nc, io, C, s, qz_sb, z4_sb, tmp_sb, szs_sb, zs_out):
    for k in ["zplq", "zpls", "zc", "zst"]:
        s[k] = Sem(nc, "s_" + k)

    CH = INTER // 8
    with nc.Block("p5pre") as block:
        chunks = []
        for mat, (qz, sc) in enumerate([(io["g_qz"], io["g_sc"]), (io["u_qz"], io["u_sc"])]):
            for ci in range(8):
                chunks.append((qz, sc, io["zs_gu_dram"], mat * INTER + ci * CH,
                               32, CH, ci * (CH // 8), ci * CH))
        for ci in range(4):
            chunks.append((io["d_qz"], io["d_sc"], io["zs_d_dram"], ci * 1024,
                           86, 1024, ci * 128, ci * 1024))

        @block.scalar
        def _(sc_):
            sc_.wait_ge(s["avst"][0].h, 16 * 32)
            sc_.wait_ge(s["avst"][1].h, 16 * 32)

            def store(i):
                _, _, dst, col, rows, w, _, _ = chunks[i]
                sc_.wait_ge(s["zc"].h, 10 * (i + 1))
                sc_.dma_start(out=dst.ap()[0:rows, col:col + w],
                              in_=zs_out[0:rows, 0:w]).then_inc(s["zst"].h, 16)
                s["zst"].inc(16)

            for i, (qz, sc, dst, col, rows, w, qzoff, scoff) in enumerate(chunks):
                if i >= 1:
                    sc_.wait_ge(s["zc"].h, 10 * i)
                sc_.dma_start(out=qz_sb[0:rows, 0:w // 8],
                              in_=qz[:, qzoff:qzoff + w // 8]).then_inc(s["zplq"].h, 16)
                s["zplq"].inc(16)
                sc_.dma_start(out=szs_sb[0:rows, 0:w],
                              in_=sc[:, scoff:scoff + w]).then_inc(s["zpls"].h, 16)
                s["zpls"].inc(16)
                if i >= 1:
                    store(i - 1)
            store(len(chunks) - 1)

        @block.vector
        def _(v):
            for i, (qz, sc, dst, col, rows, w, qzoff, scoff) in enumerate(chunks):
                v.wait_ge(s["zplq"].h, 16 * (i + 1))
                v.wait_ge(s["zpls"].h, 16 * (i + 1))
                v.wait_ge(s["zc"].h, 10 * i)
                if i >= 1:
                    v.wait_ge(s["zst"].h, 16 * i)
                z4v = z4_sb[0:rows, 0:w].rearrange("g (m e) -> g m e", e=8)
                for j in range(8):
                    ins = v.tensor_scalar(z4v[:, :, j], qz_sb[0:rows, 0:w // 8],
                                          4 * j, 15,
                                          ALU.logical_shift_right, ALU.bitwise_and)
                    ins.then_inc(s["zc"].h, 1)
                    s["zc"].inc()
                v.wait_ge(s["zc"].h, s["zc"].n)
                v.tensor_scalar(tmp_sb[0:rows, 0:w], z4_sb[0:rows, 0:w],
                                -7.0, -1.0, ALU.add, ALU.mult).then_inc(s["zc"].h, 1)
                s["zc"].inc()
                v.wait_ge(s["zc"].h, s["zc"].n)
                v.tensor_tensor(zs_out[0:rows, 0:w], tmp_sb[0:rows, 0:w],
                                szs_sb[0:rows, 0:w], ALU.mult).then_inc(s["zc"].h, 1)
                s["zc"].inc()


def _p5_gateup(nc, io, C, s, yT, ygT_sb, gs_sb):
    for k in ["dqa", "dqb", "gstp", "ustp", "silu", "mul", "gstm", "gtr", "gse", "zsl", "gev"]:
        s[k] = Sem(nc, "s_" + k)
    s["qw"] = [Sem(nc, f"s_qw{i}") for i in range(2)]
    s["stl"] = [Sem(nc, f"s_stl{i}") for i in range(2)]
    s["hst"] = [Sem(nc, f"s_hst{i}") for i in range(2)]
    s["zsl2"] = [Sem(nc, f"s_zsl{i}") for i in range(2)]

    NPN = INTER // 256
    NPN_R = P5_LIMIT if P5_LIMIT else NPN
    NIT_R = 2 * NPN_R
    trunc = NPN_R != NPN

    with (
        nc.sbuf_tensor("qw_sl", [128, 2 * 8 * 256], I32) as qw_sl,
        nc.sbuf_tensor("st_sl", [128, 2 * 8 * 256], F32) as st_sl,
        nc.sbuf_tensor("st_bf", [128, 2 * 8 * 256], BF16) as st_bf,
        nc.sbuf_tensor("panelG", [128, 2 * 8192], BF16) as panelG,
        nc.sbuf_tensor("panelU", [128, 2 * 8192], BF16) as panelU,
        nc.sbuf_tensor("zs_sl", [32, 2 * 512], F32R) as zs_sl,
        nc.sbuf_tensor("silu_sb", [128, 2 * 1024], BF16) as silu_sb,
        nc.sbuf_tensor("hT_st", [128, 2 * 1024], BF16) as hT_st,
        nc.sbuf_tensor("gst_sb", [128, 8 * 86], F32) as gst_sb,
        nc.sbuf_tensor("dqt", [128, 8 * 256], I32) as dqt,
        nc.psum_tensor("ps_g", [128, 1024], F32) as ps_g,
        nc.psum_tensor("ps_u", [128, 1024], F32) as ps_u,
        nc.psum_tensor("ps_gst", [128, 8], F32) as ps_gst,
        nc.psum_tensor("ps_gtr", [86, 1024], F32) as ps_gtr,
        nc.Block("p5") as block,
    ):
        qws = [io["g_qw"], io["u_qw"]]
        scs = [io["g_sc"], io["u_sc"]]
        panels = [panelG, panelU]

        @block.sync
        def _(sy):
            sy.wait_ge(s["yst"].h, 16 * 8)
            sy.wait_ge(s["zc"].h, 200)
            sy.wait_ge(s["zst"].h, 16 * 20)
            for pn in range(NPN_R):
                sl = pn % 2
                if pn >= 2:
                    sy.wait_ge(s["dqb"].h, 8 * (pn - 1))
                for mat in range(2):
                    for c4 in range(4):
                        sy.dma_start(
                            out=qw_sl[:, (sl * 8 + mat * 4 + c4) * 256:
                                      (sl * 8 + mat * 4 + c4 + 1) * 256],
                            in_=qws[mat][128 * c4:128 * (c4 + 1),
                                         pn * 256:(pn + 1) * 256],
                        ).then_inc(s["qw"][sl].h, 16)
                        s["qw"][sl].inc(16)
                        src = scs[mat][8 * c4:8 * (c4 + 1), pn * 256:(pn + 1) * 256]
                        src = src.unsqueeze(1).broadcast_to([8, 16, 256])
                        sy.dma_start(
                            out=st_sl[:, (sl * 8 + mat * 4 + c4) * 256:
                                      (sl * 8 + mat * 4 + c4 + 1) * 256],
                            in_=src,
                        ).then_inc(s["stl"][sl].h, 16)
                        s["stl"][sl].inc(16)
                if pn >= 2:
                    sy.wait_ge(s["ustp"].h, 2 * pn - 2)
                sy.dma_start(
                    out=zs_sl[:, sl * 512:sl * 512 + 256],
                    in_=io["zs_gu_dram"].ap()[0:32, pn * 256:(pn + 1) * 256],
                ).then_inc(s["zsl2"][sl].h, 16)
                sy.dma_start(
                    out=zs_sl[:, sl * 512 + 256:(sl + 1) * 512],
                    in_=io["zs_gu_dram"].ap()[0:32, INTER + pn * 256:INTER + (pn + 1) * 256],
                ).then_inc(s["zsl2"][sl].h, 16)
                s["zsl2"][sl].inc(32)

        @block.gpsimd
        def _(g):
            g.wait_ge(s["yst"].h, 16 * 8)
            g.wait_ge(s["zc"].h, 200)

            def hstore(it):
                sl = it % 2
                g.wait_ge(s["mul"].h, it + 1)
                g.dma_start(out=io["hT_dram"].ap()[it * 128:(it + 1) * 128, :],
                            in_=hT_st[:, sl * 1024:(sl + 1) * 1024]).then_inc(s["hst"][sl].h, 16)
                s["hst"][sl].inc(16)

            for it in range(NIT_R):
                hstore(it)

        @block.vector
        def _(v):
            v.wait_ge(s["yst"].h, 16 * 8)
            v.wait_ge(s["zst"].h, 16 * 20)
            def emit_dequant(pn):
                sl = pn % 2
                v.wait_ge(s["qw"][sl].h, 128 * (pn // 2 + 1))
                v.wait_ge(s["stl"][sl].h, 128 * (pn // 2 + 1))
                if pn >= 2:
                    v.wait_ge(s["gstp"].h, 2 * pn - 2)
                v.tensor_copy(st_bf[:, sl * 2048:(sl + 1) * 2048],
                              st_sl[:, sl * 2048:(sl + 1) * 2048]).then_inc(s["dqa"].h, 1)
                s["dqa"].inc()
                for r, (mat, c4) in enumerate([(m, c) for m in range(2) for c in range(4)]):
                    gr = pn * 8 + r
                    base = (sl * 8 + mat * 4 + c4) * 256
                    if gr >= 1:
                        v.wait_ge(s["dqb"].h, gr)
                    if pn >= 2 and mat == 1 and c4 == 0:
                        v.wait_ge(s["ustp"].h, 2 * pn - 2)
                    for j in range(8):
                        ins = v.tensor_scalar(dqt[:, j * 256:(j + 1) * 256],
                                              qw_sl[:, base:base + 256], 4 * j, 15,
                                              ALU.logical_shift_right, ALU.bitwise_and)
                    ins.then_inc(s["dqa"].h, 1)
                    s["dqa"].inc()
                    v.wait_ge(s["dqa"].h, s["dqa"].n)
                    pan = panels[mat]
                    for j in range(8):
                        cj = c4 * 8 + j
                        ins = v.scalar_tensor_tensor(
                            pan[:, sl * 8192 + cj * 256:sl * 8192 + (cj + 1) * 256],
                            dqt[:, j * 256:(j + 1) * 256], -8.0,
                            st_bf[:, base:base + 256], ALU.add, ALU.mult)
                    ins.then_inc(s["dqb"].h, 1)
                    s["dqb"].inc()

            def emit_gev(it):
                if NO_AUX:
                    return
                v.wait_ge(s["gstm"].h, it + 1)
                v.tensor_copy(gst_sb[:, it * 8:(it + 1) * 8], ps_gst[:, :]).then_inc(s["gev"].h, 1)
                s["gev"].inc()

            def emit_mul(it):
                sl = it % 2
                v.wait_ge(s["ustp"].h, it + 1)
                v.wait_ge(s["silu"].h, it + 1)
                if it >= 2:
                    v.wait_ge(s["hst"][sl].h, 16 * (it // 2))
                    if not NO_AUX:
                        v.wait_ge(s["gstm"].h, it - 1)
                v.tensor_tensor(hT_st[:, sl * 1024:sl * 1024 + 512],
                                ps_u[:, 0:512], silu_sb[:, sl * 1024:sl * 1024 + 512],
                                ALU.mult)
                v.tensor_tensor(hT_st[:, sl * 1024 + 512:(sl + 1) * 1024],
                                ps_u[:, 512:1024], silu_sb[:, sl * 1024 + 512:(sl + 1) * 1024],
                                ALU.mult).then_inc(s["mul"].h, 1)
                s["mul"].inc()

            for pn in range(NPN_R):
                emit_dequant(pn)
                if pn >= 2:
                    emit_gev(2 * (pn - 2))
                    emit_gev(2 * (pn - 2) + 1)
                if pn >= 1:
                    emit_mul(2 * (pn - 1))
                    emit_mul(2 * (pn - 1) + 1)
            if NPN_R >= 2:
                emit_gev(2 * NPN_R - 4)
                emit_gev(2 * NPN_R - 3)
            emit_mul(2 * (NPN_R - 1))
            emit_mul(2 * (NPN_R - 1) + 1)
            emit_gev(2 * NPN_R - 2)
            emit_gev(2 * NPN_R - 1)
            if not NO_AUX:
                v.wait_ge(s["gev"].h, NIT_R)
                v.engine_nop().then_inc(s["gse"].h, 1)
                s["gse"].inc()
                if not NO_TR:
                    v.wait_ge(s["gtr"].h, 8)
                    v.tensor_copy(gs_sb[0:86, :],
                                  ps_gtr[0:86, :]).then_inc(s["gse"].h, 1)
                    s["gse"].inc()

        @block.scalar
        def _(sc):
            for it in range(NIT_R):
                sl = it % 2
                sc.wait_ge(s["gstp"].h, it + 1)
                if it >= 2:
                    sc.wait_ge(s["mul"].h, it - 1)
                sc.activation(silu_sb[:, sl * 1024:(sl + 1) * 1024],
                              ps_g[:, :], AF.Silu).then_inc(s["silu"].h, 1)
                s["silu"].inc()

        @block.tensor
        def _(t):
            t.wait_ge(s["yge"].h, 2)
            t.wait_ge(s["zst"].h, 16 * 20)
            for pn in range(NPN_R):
                sl = pn % 2
                t.wait_ge(s["zsl2"][pn % 2].h, 32 * (pn // 2 + 1))
                for itp in range(2):
                    it = 2 * pn + itp
                    if it > 0:
                        t.wait_ge(s["silu"].h, it)
                    for cj in range(NDMT):
                        if itp == 0 and cj % 8 == 0:
                            t.wait_ge(s["dqb"].h, 8 * pn + cj // 8 + 1)
                        for th in range(2):
                            ins = t.matmul(ps_g[:, th * 512:(th + 1) * 512],
                                           panelG[:, sl * 8192 + cj * 256 + itp * 128:
                                                  sl * 8192 + cj * 256 + (itp + 1) * 128],
                                           yT[:, cj * MY + th * 512:cj * MY + (th + 1) * 512],
                                           start=(cj == 0), stop=(NO_Z and cj == NDMT - 1))
                    if not NO_Z:
                        for th in range(2):
                            ins = t.matmul(ps_g[:, th * 512:(th + 1) * 512],
                                           zs_sl[0:32, sl * 512 + itp * 128:sl * 512 + (itp + 1) * 128],
                                           ygT_sb[0:32, th * 512:(th + 1) * 512],
                                           start=False, stop=True)
                    ins.then_inc(s["gstp"].h, 1)
                    s["gstp"].inc()
                    if it > 0:
                        t.wait_ge(s["mul"].h, it)
                    for cj in range(NDMT):
                        if itp == 0 and cj % 8 == 0:
                            t.wait_ge(s["dqb"].h, 8 * pn + 4 + cj // 8 + 1)
                        for th in range(2):
                            ins = t.matmul(ps_u[:, th * 512:(th + 1) * 512],
                                           panelU[:, sl * 8192 + cj * 256 + itp * 128:
                                                  sl * 8192 + cj * 256 + (itp + 1) * 128],
                                           yT[:, cj * MY + th * 512:cj * MY + (th + 1) * 512],
                                           start=(cj == 0), stop=(NO_Z and cj == NDMT - 1))
                    if not NO_Z:
                        for th in range(2):
                            ins = t.matmul(ps_u[:, th * 512:(th + 1) * 512],
                                           zs_sl[0:32, sl * 512 + 256 + itp * 128:
                                                 sl * 512 + 256 + (itp + 1) * 128],
                                           ygT_sb[0:32, th * 512:(th + 1) * 512],
                                           start=False, stop=True)
                    ins.then_inc(s["ustp"].h, 1)
                    s["ustp"].inc()
                    if it > 0 and not NO_AUX:
                        t.wait_ge(s["mul"].h, it)
                        if it > 1:
                            t.wait_ge(s["gev"].h, it - 1)
                        psl = (it - 1) % 2
                        if NO_GST:
                            ins = t.matmul(ps_gst[:, 0:8],
                                           hT_st[:, psl * 1024:psl * 1024 + 128],
                                           hT_st[:, psl * 1024:psl * 1024 + 8],
                                           start=True, stop=True)
                        else:
                            for tt in range(8):
                                ins = t.matmul(ps_gst[:, tt:tt + 1],
                                               hT_st[:, psl * 1024 + tt * 128:psl * 1024 + (tt + 1) * 128],
                                               C["ones_col"], start=True, stop=True)
                        ins.then_inc(s["gstm"].h, 1)
                        s["gstm"].inc()
            if NO_AUX:
                return
            t.wait_ge(s["mul"].h, NIT_R)
            t.wait_ge(s["gev"].h, NIT_R - 1)
            psl = (NIT_R - 1) % 2
            if NO_GST:
                ins = t.matmul(ps_gst[:, 0:8],
                               hT_st[:, psl * 1024:psl * 1024 + 128],
                               hT_st[:, psl * 1024:psl * 1024 + 8],
                               start=True, stop=True)
            else:
                for tt in range(8):
                    ins = t.matmul(ps_gst[:, tt:tt + 1],
                                   hT_st[:, psl * 1024 + tt * 128:psl * 1024 + (tt + 1) * 128],
                                   C["ones_col"], start=True, stop=True)
            ins.then_inc(s["gstm"].h, 1)
            s["gstm"].inc()
            t.wait_ge(s["gse"].h, 1)
            if not NO_TR:
                gstv = gst_sb[:, :].rearrange("p (i t) -> p i t", t=8)
                for tt in range(8):
                    t.transpose(ps_gtr[0:86, tt * 128:(tt + 1) * 128],
                                gstv[:, :, tt],
                                C["ident_f32"]).then_inc(s["gtr"].h, 1)
                    s["gtr"].inc()


def _p6_down(nc, io, C, s, gs_sb, o2_sb):
    for k in ["ddq", "dmm", "dtm", "dev"]:
        s[k] = Sem(nc, "s_" + k)
    s["dqw"] = [Sem(nc, f"s_dqw{i}") for i in range(2)]
    s["dsd"] = [Sem(nc, f"s_dsd{i}") for i in range(2)]
    s["dht"] = [Sem(nc, f"s_dht{i}") for i in range(4)]
    s["dzs"] = [Sem(nc, f"s_dzs{i}") for i in range(2)]

    NCI = 11
    rows_of = lambda ci: 96 if ci == 10 else 128
    htv = io["hT_dram"].ap().rearrange("(q e) t -> e q t", e=8)

    with (
        nc.sbuf_tensor("dqw_sl", [128, 2 * 512], I32) as dqw_sl,
        nc.sbuf_tensor("dsd_sl", [128, 2 * 512], F32) as dsd_sl,
        nc.sbuf_tensor("dsd_bf", [128, 2 * 512], BF16) as dsd_bf,
        nc.sbuf_tensor("dht_sl", [128, 4 * 1024], BF16) as dht_sl,
        nc.sbuf_tensor("dzs_sl", [86, 2 * 512], F32R) as dzs_sl,
        nc.sbuf_tensor("wd_sb", [128, 4 * 512], BF16) as wd_sb,
        nc.sbuf_tensor("dqt_d", [128, 512], I32) as dqt_d,
        nc.psum_tensor("ps_o2", [128, 8 * 512], F32) as ps_o2,
        nc.Block("p6") as block,
    ):
        def nidx(hc, ci, j):
            return hc * 88 + ci * 8 + j

        @block.sync
        def _(sy):
            sy.wait_ge(s["ustp"].h, N_IT)
            for sl in range(2):
                sy.wait_ge(s["hst"][sl].h, 16 * 43)
            for hc in range(8):
                sl2 = hc % 2
                if hc >= 2:
                    sy.wait_ge(s["dmm"].h, hc - 1)
                sy.dma_start(out=dzs_sl[0:86, sl2 * 512:(sl2 + 1) * 512],
                             in_=io["zs_d_dram"].ap()[0:86, hc * 512:(hc + 1) * 512]
                             ).then_inc(s["dzs"][sl2].h, 16)
                s["dzs"][sl2].inc(16)
                for ci in range(NCI):
                    rows = rows_of(ci)
                    nq = hc * NCI + ci
                    slq = nq % 2
                    if nq >= 2:
                        sy.wait_ge(s["ddq"].h, 17 * (nq - 1))
                    sy.dma_start(out=dqw_sl[0:rows, slq * 512:(slq + 1) * 512],
                                 in_=io["d_qw"][128 * ci:128 * ci + rows,
                                                hc * 512:(hc + 1) * 512]
                                 ).then_inc(s["dqw"][slq].h, 16)
                    s["dqw"][slq].inc(16)
                    grows = 6 if ci == 10 else 8
                    srcs_ = io["d_sc"][8 * ci:8 * ci + grows, hc * 512:(hc + 1) * 512]
                    srcs_ = srcs_.unsqueeze(1).broadcast_to([grows, 16, 512])
                    sy.dma_start(out=dsd_sl[0:rows, slq * 512:(slq + 1) * 512],
                                 in_=srcs_).then_inc(s["dsd"][slq].h, 16)
                    s["dsd"][slq].inc(16)
                    for j in range(8):
                        n = nidx(hc, ci, j)
                        sl = n % 4
                        if n >= 4:
                            sy.wait_ge(s["dtm"].h, n - 3)
                        sy.dma_start(out=dht_sl[0:rows, sl * 1024:(sl + 1) * 1024],
                                     in_=htv[j, 128 * ci:128 * ci + rows, :]
                                     ).then_inc(s["dht"][sl].h, 16)
                        s["dht"][sl].inc(16)

        @block.gpsimd
        def _(g):
            pass

        @block.vector
        def _(v):
            v.wait_ge(s["ustp"].h, N_IT)
            for sl in range(2):
                v.wait_ge(s["hst"][sl].h, 16 * 43)
            for hc in range(8):
                for ci in range(NCI):
                    rows = rows_of(ci)
                    nq = hc * NCI + ci
                    slq = nq % 2
                    v.wait_ge(s["dqw"][slq].h, 16 * (nq // 2 + 1))
                    v.wait_ge(s["dsd"][slq].h, 16 * (nq // 2 + 1))
                    v.tensor_copy(dsd_bf[0:rows, slq * 512:(slq + 1) * 512],
                                  dsd_sl[0:rows, slq * 512:(slq + 1) * 512]).then_inc(s["ddq"].h, 1)
                    s["ddq"].inc()
                    for j in range(8):
                        n = nidx(hc, ci, j)
                        sl = n % 4
                        if n >= 4:
                            v.wait_ge(s["dtm"].h, n - 3)
                        v.wait_ge(s["ddq"].h, s["ddq"].n)
                        v.tensor_scalar(dqt_d[0:rows, :], dqw_sl[0:rows, slq * 512:
                                        (slq + 1) * 512], 4 * j, 15,
                                        ALU.logical_shift_right,
                                        ALU.bitwise_and).then_inc(s["ddq"].h, 1)
                        s["ddq"].inc()
                        v.wait_ge(s["ddq"].h, s["ddq"].n)
                        v.scalar_tensor_tensor(wd_sb[0:rows, sl * 512:(sl + 1) * 512],
                                               dqt_d[0:rows, :], -8.0,
                                               dsd_bf[0:rows, slq * 512:(slq + 1) * 512],
                                               ALU.add, ALU.mult).then_inc(s["ddq"].h, 1)
                        s["ddq"].inc()
                v.wait_ge(s["dmm"].h, hc + 1)
                for tt in range(8):
                    ins = v.tensor_copy(o2_sb[:, tt * D + hc * 512:tt * D + (hc + 1) * 512],
                                        ps_o2[:, tt * 512:(tt + 1) * 512])
                ins.then_inc(s["dev"].h, 1)
                s["dev"].inc()

        @block.tensor
        def _(t):
            t.wait_ge(s["gse"].h, 2)
            for hc in range(8):
                sl2 = hc % 2
                t.wait_ge(s["dzs"][sl2].h, 16 * (hc // 2 + 1))
                if hc >= 1:
                    t.wait_ge(s["dev"].h, hc)
                for ci in range(NCI):
                    rows = rows_of(ci)
                    for j in range(8):
                        n = nidx(hc, ci, j)
                        sl = n % 4
                        t.wait_ge(s["ddq"].h, 17 * (hc * NCI + ci) + 2 * j + 3)
                        t.wait_ge(s["dht"][sl].h, 16 * (n // 4 + 1))
                        for tt in range(8):
                            ins = t.matmul(ps_o2[:, tt * 512:(tt + 1) * 512],
                                           dht_sl[0:rows, sl * 1024 + tt * 128:sl * 1024 + (tt + 1) * 128],
                                           wd_sb[0:rows, sl * 512:(sl + 1) * 512],
                                           start=(ci == 0 and j == 0), stop=False)
                        ins.then_inc(s["dtm"].h, 1)
                        s["dtm"].inc()
                for tt in range(8):
                    ins = t.matmul(ps_o2[:, tt * 512:(tt + 1) * 512],
                                   gs_sb[0:86, tt * 128:(tt + 1) * 128],
                                   dzs_sl[0:86, sl2 * 512:(sl2 + 1) * 512],
                                   start=False, stop=True)
                ins.then_inc(s["dmm"].h, 1)
                s["dmm"].inc()


def _p7_final(nc, io, C, s, o2_sb):
    for k in ["f_v", "f_a", "f_yl", "f_st"]:
        s[k] = Sem(nc, "s_" + k)
    NV = 8

    with (
        nc.sbuf_tensor("y2_tile", [128, D], BF16) as y2_tile,
        nc.sbuf_tensor("z2_sb", [128, D], F32) as z2_sb,
        nc.sbuf_tensor("sq2_sb", [128, D], F32) as sq2_sb,
        nc.sbuf_tensor("out_t", [128, D], F32) as out_t,
        nc.sbuf_tensor("ln2_sc", [128, 8], F32) as ln2_sc,
        nc.Block("p7") as block,
    ):
        S1, S2, M, M2, VE, STD, RSTD, NEGMR = range(8)

        @block.sync
        def _(sy):
            sy.wait_ge(s["dmm"].h, 8)
            for tt in range(8):
                if tt > 0:
                    sy.wait_ge(s["f_v"].h, (tt - 1) * NV + 1)
                sy.dma_start(out=y2_tile[:, :],
                             in_=io["y_dram"].ap()[tt * 128:(tt + 1) * 128, :]
                             ).then_inc(s["f_yl"].h, 16)
                s["f_yl"].inc(16)
                sy.wait_ge(s["f_v"].h, tt * NV + 8)
                sy.dma_start(out=io["out_my"][tt * 128:(tt + 1) * 128, :],
                             in_=out_t[:, :]).then_inc(s["f_st"].h, 16)
                s["f_st"].inc(16)
            sy.wait_ge(s["f_st"].h, 16 * 8)

        @block.vector
        def _(v):
            v.wait_ge(s["dev"].h, 8)
            for tt in range(8):
                base = tt * NV
                abase = tt * 2
                v.wait_ge(s["f_yl"].h, 16 * (tt + 1))
                v.tensor_tensor(z2_sb[:, :], y2_tile[:, :],
                                o2_sb[:, tt * D:(tt + 1) * D], ALU.add).then_inc(s["f_v"].h, 1)
                v.wait_ge(s["f_v"].h, base + 1)
                v.reduce_sum(ln2_sc[:, S1:S1 + 1], z2_sb[:, :], axis=AX.X).then_inc(s["f_v"].h, 1)
                v.wait_ge(s["f_v"].h, base + 2)
                v.tensor_scalar(ln2_sc[:, M:M + 1], ln2_sc[:, S1:S1 + 1],
                                1.0 / D, None, ALU.mult).then_inc(s["f_v"].h, 1)
                v.wait_ge(s["f_v"].h, base + 3)
                v.tensor_scalar(ln2_sc[:, M2:M2 + 1], ln2_sc[:, M:M + 1],
                                ln2_sc[:, M:M + 1], -1e-5,
                                ALU.mult, ALU.add).then_inc(s["f_v"].h, 1)
                v.wait_ge(s["f_v"].h, base + 4)
                v.wait_ge(s["f_a"].h, abase + 1)
                v.scalar_tensor_tensor(ln2_sc[:, VE:VE + 1], ln2_sc[:, S2:S2 + 1],
                                       1.0 / D, ln2_sc[:, M2:M2 + 1],
                                       ALU.mult, ALU.subtract).then_inc(s["f_v"].h, 1)
                v.wait_ge(s["f_a"].h, abase + 2)
                v.reciprocal(ln2_sc[:, RSTD:RSTD + 1],
                             ln2_sc[:, STD:STD + 1]).then_inc(s["f_v"].h, 1)
                v.wait_ge(s["f_v"].h, base + 6)
                v.scalar_tensor_tensor(ln2_sc[:, NEGMR:NEGMR + 1], ln2_sc[:, M:M + 1],
                                       -1.0, ln2_sc[:, RSTD:RSTD + 1],
                                       ALU.mult, ALU.mult).then_inc(s["f_v"].h, 1)
                v.wait_ge(s["f_v"].h, base + 7)
                if tt > 0:
                    v.wait_ge(s["f_st"].h, 16 * tt)
                v.tensor_scalar(out_t[:, :], z2_sb[:, :],
                                ln2_sc[:, RSTD:RSTD + 1], ln2_sc[:, NEGMR:NEGMR + 1],
                                ALU.mult, ALU.add).then_inc(s["f_v"].h, 1)

        @block.scalar
        def _(sc):
            for tt in range(8):
                base = tt * NV
                sc.wait_ge(s["f_v"].h, base + 1)
                sc.activation(sq2_sb[:, :], z2_sb[:, :], AF.Square,
                              accum_out=ln2_sc[:, S2:S2 + 1]).then_inc(s["f_a"].h, 1)
                sc.wait_ge(s["f_v"].h, base + 5)
                sc.activation(ln2_sc[:, STD:STD + 1], ln2_sc[:, VE:VE + 1],
                              AF.Sqrt).then_inc(s["f_a"].h, 1)
                s["f_a"].inc(2)


def _shard_inputs(inputs):
    x = np.ascontiguousarray(np.asarray(inputs["L0_attn_x"]).reshape(T, D))
    wq = np.asarray(inputs["L0_attn_wq"])
    wk = np.asarray(inputs["L0_attn_wk"])
    wv = np.asarray(inputs["L0_attn_wv"])
    maps = []
    for c in range(NC):
        m = {
            "x_my": np.ascontiguousarray(x[c * MY:(c + 1) * MY]),
            "wq_my": np.ascontiguousarray(wq[:, c * HPC * HD:(c + 1) * HPC * HD]),
            "wk_my": np.ascontiguousarray(wk[:, c * HD:(c + 1) * HD]),
            "wv_my": np.ascontiguousarray(wv[:, c * HD:(c + 1) * HD]),
            "wo": np.asarray(inputs["L0_attn_wo"]),
            "g_qw": np.asarray(inputs["L0_ffn_gate_qweight"]),
            "g_sc": np.asarray(inputs["L0_ffn_gate_scales"]),
            "g_qz": np.asarray(inputs["L0_ffn_gate_qzeros"]),
            "u_qw": np.asarray(inputs["L0_ffn_up_qweight"]),
            "u_sc": np.asarray(inputs["L0_ffn_up_scales"]),
            "u_qz": np.asarray(inputs["L0_ffn_up_qzeros"]),
            "d_qw": np.asarray(inputs["L0_ffn_down_qweight"]),
            "d_sc": np.asarray(inputs["L0_ffn_down_scales"]),
            "d_qz": np.asarray(inputs["L0_ffn_down_qzeros"]),
        }
        gidx = np.zeros((128, 32), dtype=np.int32)
        p = np.arange(128)
        for chunk in range(4):
            for cc in range(8):
                gidx[:, chunk * 8 + cc] = ((c // 4) * NC * HD + cc * HD + p) \
                    * (S // MY) + (c % 4)
        m["gidx"] = gidx
        maps.append(m)
    return maps


def kernel(**inputs):
    nc, _ = build_program(dbg=False)
    maps = _shard_inputs(inputs)
    res = run_bass_kernel_spmd(nc, maps, core_ids=list(range(NC)))
    out = np.concatenate([r["out_my"] for r in res.results], axis=0)
    return out.reshape(B, S, D)


# version 11
# speedup vs baseline: 1.1030x; 1.0065x over previous
import numpy as np
import ml_dtypes
from contextlib import ExitStack
import concourse.bass as bass
import concourse.mybir as mybir
from concourse.bass import IndirectOffsetOnAxis
from concourse.bass_utils import run_bass_kernel_spmd

F32 = mybir.dt.float32
F32R = mybir.dt.float32r
BF16 = mybir.dt.bfloat16
I32 = mybir.dt.int32
AF = mybir.ActivationFunctionType
ALU = mybir.AluOpType
AX = mybir.AxisListType

NC = 8
B, S, D = 2, 4096, 4096
T = B * S
MY = T // NC
HD = 128
HPC = 4
INTER = 11008
SCALE = HD ** -0.5
NDMT = D // 128
N_IT = INTER // 128
bf16 = ml_dtypes.bfloat16
NO_Z = False
NO_GST = False
P5_LIMIT = None
NO_AUX = False
NO_TR = False


class Sem:
    def __init__(self, nc, name):
        self.h = nc.alloc_semaphore(name)
        self.n = 0

    def inc(self, k=1):
        self.n += k
        return self.n


def build_program(dbg=False):
    nc = bass.Bass("TRN2", target_bir_lowering=False)

    io = {}
    def inp(name, shape, dt):
        io[name] = nc.dram_tensor(name, shape, dt, kind="ExternalInput").ap()
    inp("x_my", [MY, D], F32)
    inp("wq_my", [D, HPC * HD], F32)
    inp("wk_my", [D, HD], F32)
    inp("wv_my", [D, HD], F32)
    inp("wo", [D, D], F32)
    inp("g_qw", [D // 8, INTER], I32)
    inp("g_sc", [D // 128, INTER], F32)
    inp("g_qz", [D // 128, INTER // 8], I32)
    inp("u_qw", [D // 8, INTER], I32)
    inp("u_sc", [D // 128, INTER], F32)
    inp("u_qz", [D // 128, INTER // 8], I32)
    inp("d_qw", [INTER // 8, D], I32)
    inp("d_sc", [INTER // 128, D], F32)
    inp("d_qz", [INTER // 128, D // 8], I32)
    inp("gidx", [128, 32], I32)
    out_my = nc.dram_tensor("out_my", [MY, D], F32, kind="ExternalOutput").ap()
    io["out_my"] = out_my

    xt_contrib = [nc.dram_tensor(f"xt_contrib{i}", [D, MY // 4], BF16) for i in range(4)]
    xt_full = [nc.dram_tensor(f"xt_full{i}", [NC * D, MY // 4], BF16) for i in range(4)]
    at_contrib = [[nc.dram_tensor(f"at_contrib{h}_{b}", [HD, S], BF16) for b in range(2)]
                  for h in range(4)]
    at_full = [nc.dram_tensor(f"at_full{h}", [2 * NC * HD, S], BF16) for h in range(4)]
    o_dram = nc.dram_tensor("o_dram", [MY, D], BF16)
    zs_gu_dram = nc.dram_tensor("zs_gu_dram", [32, 2 * INTER], F32R)
    zs_d_dram = nc.dram_tensor("zs_d_dram", [INTER // 128, D], F32R)
    y_dram = nc.dram_tensor("y_dram", [MY, D], BF16)
    hT_dram = nc.dram_tensor("hT_dram", [INTER, MY], BF16)
    io.update(xt_contrib=xt_contrib, xt_full=xt_full, at_contrib=at_contrib,
              at_full=at_full, o_dram=o_dram, y_dram=y_dram, hT_dram=hT_dram,
              zs_gu_dram=zs_gu_dram, zs_d_dram=zs_d_dram)

    C = {}
    C["ident_bf"] = nc.inline_tensor(np.eye(128, dtype=bf16), "ident_bf").ap()
    C["ident_f32"] = nc.inline_tensor(np.eye(128, dtype=np.float32), "ident_f32").ap()
    C["ones_col"] = nc.inline_tensor(np.ones((128, 1), dtype=bf16), "ones_col").ap()
    C["ones_row_f32"] = nc.inline_tensor(np.ones((1, 128), dtype=np.float32), "ones_row_f32").ap()
    m = np.zeros((128, 4 * 512), dtype=bf16)
    for o in range(4):
        kk = np.arange(128)[:, None] + o * 128
        qq = np.arange(512)[None, :]
        m[:, o * 512:(o + 1) * 512] = (kk <= qq).astype(bf16)
    C["masks"] = nc.inline_tensor(m, "masks").ap()
    e = np.zeros((128, 4 * 32), dtype=bf16)
    for c4 in range(4):
        for p in range(128):
            e[p, c4 * 32 + 8 * c4 + p // 16] = 1
    C["e_cmats"] = nc.inline_tensor(e, "e_cmats").ap()

    dbg_out = {}
    if dbg == "qkv":
        for name, shape, dt in [
            ("dbg_qT", [128, HPC * T], BF16),
            ("dbg_kT", [128, T], BF16),
            ("dbg_v", [128, T], BF16),
        ]:
            dbg_out[name] = nc.dram_tensor(name, shape, dt, kind="ExternalOutput").ap()
    elif dbg == "p2":
        dbg_out["dbg_at0"] = nc.dram_tensor("dbg_at0", [2 * NC * HD, S], BF16,
                                            kind="ExternalOutput").ap()
        dbg_out["dbg_at1"] = nc.dram_tensor("dbg_at1", [2 * NC * HD, S], BF16,
                                            kind="ExternalOutput").ap()
    elif dbg == "p5a":
        dbg_out["dbg_zs"] = nc.dram_tensor("dbg_zs", [32, 2 * INTER], F32R, kind="ExternalOutput").ap()
        dbg_out["dbg_zsd"] = nc.dram_tensor("dbg_zsd", [INTER // 128, D], F32R, kind="ExternalOutput").ap()
    elif dbg == "p5":
        dbg_out["dbg_hT"] = nc.dram_tensor("dbg_hT", [INTER, MY], BF16, kind="ExternalOutput").ap()
        dbg_out["dbg_gs"] = nc.dram_tensor("dbg_gs", [86, MY], F32R, kind="ExternalOutput").ap()
        dbg_out["dbg_y2"] = nc.dram_tensor("dbg_y2", [MY, D], BF16, kind="ExternalOutput").ap()
        dbg_out["dbg_zs"] = nc.dram_tensor("dbg_zs", [32, 2 * INTER], F32R, kind="ExternalOutput").ap()
        dbg_out["dbg_zsd"] = nc.dram_tensor("dbg_zsd", [INTER // 128, D], F32R, kind="ExternalOutput").ap()
    elif dbg == "p4":
        dbg_out["dbg_at0"] = nc.dram_tensor("dbg_at0", [2 * NC * HD, S], BF16,
                                            kind="ExternalOutput").ap()
        dbg_out["dbg_at1"] = nc.dram_tensor("dbg_at1", [2 * NC * HD, S], BF16,
                                            kind="ExternalOutput").ap()
        dbg_out["dbg_o"] = nc.dram_tensor("dbg_o", [MY, D], BF16, kind="ExternalOutput").ap()
        dbg_out["dbg_y"] = nc.dram_tensor("dbg_y", [MY, D], BF16, kind="ExternalOutput").ap()
        dbg_out["dbg_yg"] = nc.dram_tensor("dbg_yg", [32, MY], F32R, kind="ExternalOutput").ap()

    s = {k: Sem(nc, "s_" + k) for k in
         ["tp", "tpe", "ag", "w", "mmq", "mmkv", "evq", "evkv", "const"]}
    s["x"] = [Sem(nc, f"s_x{i}") for i in range(2)]
    s["ct"] = [Sem(nc, f"s_ct{i}") for i in range(4)]
    s["xt"] = [Sem(nc, f"s_xt{i}") for i in range(3)]

    with (
        nc.sbuf_tensor("cst", [128, 3456], BF16) as cst,
        nc.sbuf_tensor("cst32", [128, 256], F32) as cst32,
    ):
        SC = {
            "ident_bf": cst[:, 0:128],
            "ones_col": cst[:, 128:129],
            "masks": cst[:, 256:256 + 2048],
            "e_cmats": cst[:, 2304:2304 + 128],
            "ident_f32": cst32[:, 0:128],
            "ones_row_f32": cst32[0:1, 128:256],
        }
        with nc.Block("pconst") as block:
            @block.sync
            def _(sy):
                sy.dma_start(out=SC["ident_bf"], in_=C["ident_bf"]).then_inc(s["const"].h, 16)
                sy.dma_start(out=SC["ones_col"], in_=C["ones_col"]).then_inc(s["const"].h, 16)
                sy.dma_start(out=SC["masks"], in_=C["masks"]).then_inc(s["const"].h, 16)
                sy.dma_start(out=SC["e_cmats"], in_=C["e_cmats"]).then_inc(s["const"].h, 16)
                sy.dma_start(out=SC["ident_f32"], in_=C["ident_f32"]).then_inc(s["const"].h, 16)
                sy.dma_start(out=SC["ones_row_f32"], in_=C["ones_row_f32"]).then_inc(s["const"].h, 16)
                s["const"].inc(96)
        C = dict(C)
        C.update(SC)
        _build_main(nc, io, C, s, dbg, dbg_out)
    return nc, io


def _build_main(nc, io, C, s, dbg, dbg_out):
    with (
        nc.sbuf_tensor("qT", [128, HPC * T], BF16) as qT,
        nc.sbuf_tensor("kT", [128, T], BF16) as kT,
        nc.sbuf_tensor("v_sb", [128, T], BF16) as v_sb,
    ):
        with (
            nc.sbuf_tensor("wq_sb", [128, NDMT * HPC * HD], BF16) as wq_sb,
            nc.sbuf_tensor("wkv_sb", [128, NDMT * 2 * HD], BF16) as wkv_sb,
        ):
            _p0_transpose_gather(nc, io, C, s, wq_sb, wkv_sb)
            _p1_qkv(nc, io, C, s, qT, kT, v_sb, wq_sb, wkv_sb)
        _p2_attention(nc, io, C, s, qT, kT, v_sb)
        if dbg == "qkv":
            _p_dbg_qkv(nc, s, dbg_out, qT, kT, v_sb)
            return
    if dbg == "p2":
        _p_dbg_p2(nc, io, s, dbg_out)
        return
    with (
        nc.sbuf_tensor("yT", [128, NDMT * MY], BF16) as yT,
        nc.sbuf_tensor("ygT_sb", [32, MY], F32R) as ygT_sb,
    ):
        with (
            nc.sbuf_tensor("qz_sb", [86, 172], I32) as qz_sb,
            nc.sbuf_tensor("z4_sb", [86, 1376], I32) as z4_sb,
            nc.sbuf_tensor("tmp_sb", [86, 1376], F32) as tmp_sb,
            nc.sbuf_tensor("szs_sb", [86, 1376], F32) as szs_sb,
            nc.sbuf_tensor("zs_out", [86, 1376], F32R) as zs_out,
        ):
            _p5pre_zeros(nc, io, C, s, qz_sb, z4_sb, tmp_sb, szs_sb, zs_out)
            with nc.sbuf_tensor("at_my", [128, NDMT * MY], BF16) as at_my:
                _p3_oproj(nc, io, C, s, at_my)
            _p4_postln1(nc, io, C, s, yT, ygT_sb)
        if dbg == "p4":
            _p_dbg_p4(nc, io, s, dbg_out, ygT_sb)
            return
        with nc.sbuf_tensor("gs_sb", [86, 1024], F32R) as gs_sb:
            if dbg == "p5a":
                _p_dbg_p5a(nc, io, s, dbg_out)
                return
            _p5_gateup(nc, io, C, s, yT, ygT_sb, gs_sb)
            if dbg == "p5":
                _p_dbg_p5(nc, io, s, dbg_out, gs_sb)
                return
            with nc.sbuf_tensor("o2_sb", [128, 8 * D], BF16) as o2_sb:
                _p6_down(nc, io, C, s, gs_sb, o2_sb)
                _p7_final(nc, io, C, s, o2_sb)


def _p_dbg_p5a(nc, io, s, dbg_out):
    sd = Sem(nc, "s_dbg")
    with nc.Block("pdbg5a") as block:
        @block.sync
        def _(sy):
            sy.wait_ge(s["zst"].h, 16 * 20)
            sy.wait_ge(s["yst"].h, 16 * 8)
            sy.dma_start(out=dbg_out["dbg_zs"], in_=io["zs_gu_dram"].ap()).then_inc(sd.h, 16)
            sy.dma_start(out=dbg_out["dbg_zsd"], in_=io["zs_d_dram"].ap()).then_inc(sd.h, 16)
            sy.wait_ge(sd.h, 32)


def _p_dbg_p5(nc, io, s, dbg_out, gs_sb):
    sd = Sem(nc, "s_dbg")
    with nc.Block("pdbg5") as block:
        @block.sync
        def _(sy):
            for sl in range(2):
                sy.wait_ge(s["hst"][sl].h, s["hst"][sl].n)
            sy.wait_ge(s["gse"].h, s["gse"].n)
            sy.wait_ge(s["yst"].h, 16 * 8)
            sy.dma_start(out=dbg_out["dbg_hT"], in_=io["hT_dram"].ap()).then_inc(sd.h, 16)
            sy.dma_start(out=dbg_out["dbg_gs"], in_=gs_sb[0:86, :]).then_inc(sd.h, 16)
            sy.dma_start(out=dbg_out["dbg_y2"], in_=io["y_dram"].ap()).then_inc(sd.h, 16)
            sy.dma_start(out=dbg_out["dbg_zs"], in_=io["zs_gu_dram"].ap()).then_inc(sd.h, 16)
            sy.dma_start(out=dbg_out["dbg_zsd"], in_=io["zs_d_dram"].ap()).then_inc(sd.h, 16)
            sy.wait_ge(sd.h, 80)


def _p_dbg_p4(nc, io, s, dbg_out, ygT_sb):
    sd = Sem(nc, "s_dbg")
    with nc.Block("pdbg4") as block:
        @block.vector
        def _(v):
            v.wait_ge(s["yge"].h, 2)
            v.engine_nop().then_inc(sd.h, 1)

        @block.sync
        def _(sy):
            sy.wait_ge(s["ag2"].h, 8)
            sy.wait_ge(s["yst"].h, 16 * 8)
            sy.wait_ge(sd.h, 1)
            sy.dma_start(out=dbg_out["dbg_at0"], in_=io["at_full"][0].ap()).then_inc(sd.h, 16)
            sy.dma_start(out=dbg_out["dbg_at1"], in_=io["at_full"][1].ap()).then_inc(sd.h, 16)
            sy.dma_start(out=dbg_out["dbg_o"], in_=io["o_dram"].ap()).then_inc(sd.h, 16)
            sy.dma_start(out=dbg_out["dbg_y"], in_=io["y_dram"].ap()).then_inc(sd.h, 16)
            sy.dma_start(out=dbg_out["dbg_yg"], in_=ygT_sb[:, :]).then_inc(sd.h, 16)
            sy.wait_ge(sd.h, 81)


def _p0_transpose_gather(nc, io, C, s, wq_sb, wkv_sb):
    with (
        nc.sbuf_tensor("x_my_bf", [128, 2 * D], BF16) as x_my_bf,
        nc.sbuf_tensor("tp_st", [128, 4 * 512], BF16) as tp_st,
        nc.psum_tensor("ps_tpA", [128, 1024], BF16) as ps_tpA,
        nc.psum_tensor("ps_tpB", [128, 1024], BF16) as ps_tpB,
        nc.Block("p0") as block,
    ):
        @block.gpsimd
        def _(g):
            for tt in range(8):
                if tt >= 2:
                    g.wait_ge(s["tp"].h, (tt - 1) * 8)
                sx = s["x"][tt % 2]
                g.dma_start(out=x_my_bf[:, (tt % 2) * D:(tt % 2 + 1) * D],
                            in_=io["x_my"][tt * 128:(tt + 1) * 128, :]).then_inc(sx.h, 16)
                sx.inc(16)
            g.dma_start(out=wq_sb[:, :].rearrange("p (d h) -> p d h", d=NDMT),
                        in_=io["wq_my"].rearrange("(d p) h -> p d h", p=128)).then_inc(s["w"].h, 16)
            g.dma_start(out=wkv_sb[:, 0:NDMT * HD].rearrange("p (d h) -> p d h", d=NDMT),
                        in_=io["wk_my"].rearrange("(d p) h -> p d h", p=128)).then_inc(s["w"].h, 16)
            g.dma_start(out=wkv_sb[:, NDMT * HD:2 * NDMT * HD].rearrange("p (d h) -> p d h", d=NDMT),
                        in_=io["wv_my"].rearrange("(d p) h -> p d h", p=128)).then_inc(s["w"].h, 16)
            s["w"].inc(48)
            for q in range(4):
                for m4 in range(4):
                    g.wait_ge(s["ct"][m4].h, 16 * 4 * (q + 1))
                g.collective_compute(
                    "AllGather", ALU.bypass,
                    replica_groups=[list(range(NC))],
                    ins=[io["xt_contrib"][q].ap().opt()],
                    outs=[io["xt_full"][q].ap().opt()],
                ).then_inc(s["ag"].h, 1)
                s["ag"].inc()

        @block.tensor
        def _(t):
            t.wait_ge(s["const"].h, 96)
            n = 0
            for tt in range(8):
                t.wait_ge(s["x"][tt % 2].h, 16 * (tt // 2 + 1))
                for d4 in range(8):
                    pp = n % 2
                    ps_tp = ps_tpA if pp == 0 else ps_tpB
                    if n >= 2:
                        t.wait_ge(s["tpe"].h, n - 1)
                    for k in range(4):
                        d = d4 * 4 + k
                        ins = t.transpose(ps_tp[:, k * 128:(k + 1) * 128],
                                          x_my_bf[:, (tt % 2) * D + d * 128: (tt % 2) * D + (d + 1) * 128],
                                          C["ident_bf"])
                        if k == 3:
                            ins.then_inc(s["tp"].h, 1)
                            s["tp"].inc()
                    n += 1

        @block.vector
        def _(v):
            for g8 in range(64):
                pp = g8 % 2
                m4 = g8 % 4
                ps_tp = ps_tpA if pp == 0 else ps_tpB
                v.wait_ge(s["tp"].h, g8 + 1)
                if g8 >= 4:
                    v.wait_ge(s["ct"][m4].h, 16 * (g8 // 4))
                v.tensor_copy(tp_st[:, m4 * 512:(m4 + 1) * 512],
                              ps_tp[:, 0:512]).then_inc(s["tpe"].h, 1)
                s["tpe"].inc()

        @block.sync
        def _(sy):
            n = 0
            for tt in range(8):
                for d4 in range(8):
                    m4 = n % 4
                    n += 1
                    sy.wait_ge(s["tpe"].h, n)
                    sct = s["ct"][m4]
                    sy.dma_start(
                        out=io["xt_contrib"][tt // 2].ap().rearrange("(d p) t -> p d t", p=128)
                            [:, d4 * 4:(d4 + 1) * 4, (tt % 2) * 128:(tt % 2 + 1) * 128],
                        in_=tp_st[:, m4 * 512:(m4 + 1) * 512]
                            .rearrange("p (d t) -> p d t", d=4),
                    ).then_inc(sct.h, 16)
                    sct.inc(16)


def _p1_qkv(nc, io, C, s, qT, kT, v_sb, wq_sb, wkv_sb):
    with (
        nc.sbuf_tensor("xt_buf", [128, 3 * 16 * 512], BF16) as xt_buf,
        nc.psum_tensor("ps_q", [128, 4 * 512], F32) as ps_q,
        nc.psum_tensor("ps_kv0", [128, 2 * 512], F32) as ps_kv0,
        nc.psum_tensor("ps_kv1", [128, 2 * 512], F32) as ps_kv1,
        nc.Block("p1") as block,
    ):
        xtv = [io["xt_full"][i].ap().rearrange("(r d p) t -> r d p t", d=NDMT, p=128)
               for i in range(4)]

        @block.sync
        def _(sy):
            nb = 0
            for ch in range(16):
                r, half = ch // 2, ch % 2
                for hh in range(2):
                    b = nb % 3
                    if nb >= 3:
                        sy.wait_ge(s["mmkv"].h, (nb - 3) // 2 + 1)
                    sxt = s["xt"][b]
                    dst = xt_buf[:, b * 8192:(b + 1) * 8192] \
                        .rearrange("p (d t) -> p d t", d=16)
                    for qq in range(2):
                        q = 2 * half + qq
                        sy.wait_ge(s["ag"].h, q + 1)
                        sy.dma_start(
                            out=dst[:, :, qq * 256:(qq + 1) * 256],
                            in_=xtv[q][r, hh * 16:(hh + 1) * 16, :, :]
                                .transpose([1, 0, 2]),
                        ).then_inc(sxt.h, 16)
                        sxt.inc(16)
                    nb += 1

        @block.tensor
        def _(t):
            t.wait_ge(s["w"].h, 48)
            nb = 0
            for ch in range(16):
                for hh in range(2):
                    b = nb % 3
                    nb += 1
                    t.wait_ge(s["xt"][b].h, 32 * ((nb - 1) // 3 + 1))
                    if hh == 0 and ch > 0:
                        t.wait_ge(s["evq"].h, ch)
                    for d16 in range(16):
                        d = hh * 16 + d16
                        xs = xt_buf[:, b * 8192 + d16 * 512:b * 8192 + (d16 + 1) * 512]
                        for h in range(HPC):
                            ins = t.matmul(ps_q[:, h * 512:(h + 1) * 512],
                                           wq_sb[:, (d * HPC + h) * HD:(d * HPC + h + 1) * HD],
                                           xs, start=(d == 0), stop=(d == NDMT - 1))
                            if d == NDMT - 1 and h == HPC - 1:
                                ins.then_inc(s["mmq"].h, 1)
                                s["mmq"].inc()
                ps_kv = ps_kv0 if ch % 2 == 0 else ps_kv1
                if ch >= 2:
                    t.wait_ge(s["evkv"].h, ch - 1)
                for hh in range(2):
                    b = (nb - 2 + hh) % 3
                    for d16 in range(16):
                        d = hh * 16 + d16
                        xs = xt_buf[:, b * 8192 + d16 * 512:b * 8192 + (d16 + 1) * 512]
                        t.matmul(ps_kv[:, 0:512],
                                 wkv_sb[:, d * HD:(d + 1) * HD],
                                 xs, start=(d == 0), stop=(d == NDMT - 1))
                        for q4 in range(4):
                            ins = t.matmul(ps_kv[:, 512 + q4 * 128:512 + (q4 + 1) * 128],
                                           xs[:, q4 * 128:(q4 + 1) * 128],
                                           wkv_sb[:, NDMT * HD + d * HD:NDMT * HD + (d + 1) * HD],
                                           start=(d == 0 and q4 == 0), stop=(d == NDMT - 1 and q4 == 3))
                            if d == NDMT - 1 and q4 == 3:
                                ins.then_inc(s["mmkv"].h, 1)
                                s["mmkv"].inc()

        @block.vector
        def _(v):
            for ch in range(16):
                v.wait_ge(s["mmq"].h, ch + 1)
                for h in range(HPC):
                    ins = v.tensor_scalar(qT[:, h * T + ch * 512:h * T + (ch + 1) * 512],
                                          ps_q[:, h * 512:(h + 1) * 512],
                                          SCALE, None, ALU.mult)
                    if h == HPC - 1:
                        ins.then_inc(s["evq"].h, 1)
                        s["evq"].inc()

        @block.scalar
        def _(sc):
            for ch in range(16):
                ps_kv = ps_kv0 if ch % 2 == 0 else ps_kv1
                sc.wait_ge(s["mmkv"].h, ch + 1)
                sc.copy(kT[:, ch * 512:(ch + 1) * 512], ps_kv[:, 0:512])
                sc.copy(v_sb[:, ch * 512:(ch + 1) * 512],
                        ps_kv[:, 512:1024]).then_inc(s["evkv"].h, 1)
                s["evkv"].inc()


def _p2_attention(nc, io, C, s, qT, kT, v_sb):
    for k in ["sc", "exp", "msk", "pav", "lav", "rec", "bca", "lrbe", "avv"]:
        s[k] = Sem(nc, "s_" + k)
    s["avst"] = [Sem(nc, f"s_avst{i}") for i in range(2)]
    s["ag2"] = Sem(nc, "s_ag2")

    groups = [(h, b, g) for h in range(HPC) for b in range(B) for g in range(8)]
    npairs = [2 * (g + 1) for (_, _, g) in groups]
    ppre = [0]
    for np_ in npairs:
        ppre.append(ppre[-1] + np_)

    with (
        nc.sbuf_tensor("probs", [128, 3 * 1024], BF16) as probs,
        nc.sbuf_tensor("lrec", [1, 512], F32) as lrec,
        nc.sbuf_tensor("lrb_sb", [128, 512], F32) as lrb_sb,
        nc.sbuf_tensor("av_sb", [128, 2 * 512], BF16) as av_sb,
        nc.psum_tensor("ps_sc0", [128, 1024], F32) as ps_sc0,
        nc.psum_tensor("ps_sc1", [128, 1024], F32) as ps_sc1,
        nc.psum_tensor("ps_sc2", [128, 1024], F32) as ps_sc2,
        nc.psum_tensor("ps_av", [128, 512], F32) as ps_av,
        nc.psum_tensor("ps_l", [1, 512], F32) as ps_l,
        nc.Block("p2") as block,
    ):
        @block.tensor
        def _(t):
            def emit_qk(gi, kp):
                h, b, g = groups[gi]
                P = ppre[gi] + kp
                qs = qT[:, h * T + b * S + g * 512: h * T + b * S + (g + 1) * 512]
                ps = [ps_sc0, ps_sc1, ps_sc2][P % 3]
                if kp == 0:
                    t.wait_ge(s["lrbe"].h, gi)
                if P >= 3:
                    t.wait_ge(s["exp"].h, P - 2)
                for j in range(2):
                    kt = kp * 2 + j
                    ins = t.matmul(ps[:, j * 512:(j + 1) * 512],
                                   kT[:, b * S + kt * 128:b * S + (kt + 1) * 128],
                                   qs, start=True, stop=True)
                ins.then_inc(s["sc"].h, 1)
                s["sc"].inc()

            def emit_lav(gi, kp):
                h, b, g = groups[gi]
                npair = npairs[gi]
                nkt = 4 * (g + 1)
                P = ppre[gi] + kp
                d = kp - (npair - 2)
                if kp == 0:
                    t.wait_ge(s["rec"].h, gi)
                    t.wait_ge(s["avv"].h, gi)
                t.wait_ge(s["exp"].h, P + 1)
                if d >= 0:
                    t.wait_ge(s["msk"].h, 4 * gi + 2 * (d + 1))
                pb = probs[:, (P % 3) * 1024:(P % 3 + 1) * 1024]
                for j in range(2):
                    kt = kp * 2 + j
                    pc = pb[:, j * 512:(j + 1) * 512]
                    t.matmul(ps_l[0:1, :], C["ones_col"], pc,
                             start=(kt == 0), stop=(kt == nkt - 1))
                    ins = t.matmul(ps_av[:, :],
                                   v_sb[:, (b * 32 + kt) * 128:(b * 32 + kt + 1) * 128],
                                   pc, start=(kt == 0), stop=(kt == nkt - 1))
                ins.then_inc(s["pav"].h, 1)
                s["pav"].inc()

            for gi, (h, b, g) in enumerate(groups):
                npair = npairs[gi]
                for kp in range(npair):
                    emit_qk(gi, kp)
                    if kp >= 2:
                        emit_lav(gi, kp - 2)
                emit_lav(gi, npair - 2)
                emit_lav(gi, npair - 1)
                t.wait_ge(s["rec"].h, gi + 1)
                t.matmul(ps_sc0[:, 0:512], C["ones_row_f32"], lrec[0:1, :],
                         start=True, stop=True).then_inc(s["bca"].h, 1)
                s["bca"].inc()

        @block.scalar
        def _(sc):
            for gi, (h, b, g) in enumerate(groups):
                npair = npairs[gi]
                for kp in range(npair):
                    P = ppre[gi] + kp
                    ps = [ps_sc0, ps_sc1, ps_sc2][P % 3]
                    sc.wait_ge(s["sc"].h, P + 1)
                    if P >= 3:
                        sc.wait_ge(s["pav"].h, P - 2)
                    sc.activation(probs[:, (P % 3) * 1024:(P % 3 + 1) * 1024],
                                  ps[:, :], AF.Exp).then_inc(s["exp"].h, 1)
                    s["exp"].inc()

        @block.vector
        def _(v):
            for gi, (h, b, g) in enumerate(groups):
                npair = npairs[gi]
                for d in range(2):
                    kp = npair - 2 + d
                    P = ppre[gi] + kp
                    v.wait_ge(s["exp"].h, P + 1)
                    pb = probs[:, (P % 3) * 1024:(P % 3 + 1) * 1024]
                    for j in range(2):
                        o = 2 * d + j
                        pc = pb[:, j * 512:(j + 1) * 512]
                        v.tensor_tensor(pc, pc, C["masks"][:, o * 512:(o + 1) * 512],
                                        ALU.mult).then_inc(s["msk"].h, 1)
                        s["msk"].inc()
                v.wait_ge(s["pav"].h, ppre[gi + 1])
                v.reciprocal(lrec[0:1, :], ps_l[0:1, :]).then_inc(s["rec"].h, 1)
                s["rec"].inc()
                v.wait_ge(s["bca"].h, gi + 1)
                v.tensor_copy(lrb_sb[:, :], ps_sc0[:, 0:512]).then_inc(s["lrbe"].h, 1)
                s["lrbe"].inc()
                slot = gi % 2
                if gi >= 2:
                    v.wait_ge(s["avst"][slot].h, 16 * (gi // 2))
                v.wait_ge(s["lrbe"].h, gi + 1)
                v.tensor_tensor(av_sb[:, slot * 512:(slot + 1) * 512],
                                ps_av[:, :], lrb_sb[:, :], ALU.mult).then_inc(s["avv"].h, 1)
                s["avv"].inc()

        @block.sync
        def _(sy):
            for gi, (h, b, g) in enumerate(groups):
                slot = gi % 2
                sy.wait_ge(s["avv"].h, gi + 1)
                sy.dma_start(
                    out=io["at_contrib"][h][b].ap()[:, g * 512:(g + 1) * 512],
                    in_=av_sb[:, slot * 512:(slot + 1) * 512],
                ).then_inc(s["avst"][slot].h, 16)
                s["avst"][slot].inc(16)

        @block.gpsimd
        def _(gp):
            for h in range(4):
                for b in range(2):
                    ng = 16 * h + 8 * b + 8
                    gp.wait_ge(s["avst"][0].h, 16 * ((ng + 1) // 2))
                    gp.wait_ge(s["avst"][1].h, 16 * (ng // 2))
                    gp.collective_compute(
                        "AllGather", ALU.bypass,
                        replica_groups=[list(range(NC))],
                        ins=[io["at_contrib"][h][b].ap().opt()],
                        outs=[io["at_full"][h].ap()[b * NC * HD:(b + 1) * NC * HD, :].opt()],
                    ).then_inc(s["ag2"].h, 1)
                    s["ag2"].inc()


def _p_dbg_p2(nc, io, s, dbg_out):
    sd = Sem(nc, "s_dbg")
    with nc.Block("pdbg2") as block:
        @block.sync
        def _(sy):
            sy.wait_ge(s["ag2"].h, 8)
            sy.dma_start(out=dbg_out["dbg_at0"], in_=io["at_full"][0].ap()).then_inc(sd.h, 16)
            sy.dma_start(out=dbg_out["dbg_at1"], in_=io["at_full"][1].ap()).then_inc(sd.h, 16)
            sy.wait_ge(sd.h, 32)


def _p_dbg_qkv(nc, s, dbg_out, qT, kT, v_sb):
    sd = Sem(nc, "s_dbg")
    with nc.Block("pdbg") as block:
        @block.sync
        def _(sy):
            sy.wait_ge(s["evq"].h, 16)
            sy.wait_ge(s["evkv"].h, 16)
            sy.dma_start(out=dbg_out["dbg_qT"], in_=qT[:, :]).then_inc(sd.h, 16)
            sy.dma_start(out=dbg_out["dbg_kT"], in_=kT[:, :]).then_inc(sd.h, 16)
            sy.dma_start(out=dbg_out["dbg_v"], in_=v_sb[:, :]).then_inc(sd.h, 16)
            sy.wait_ge(sd.h, 48)


def _p3_oproj(nc, io, C, s, at_my):
    for k in ["omm", "oev", "gix"]:
        s[k] = Sem(nc, "s_" + k)
    s["gth"] = [Sem(nc, f"s_gth{i}") for i in range(4)]
    s["wo"] = [Sem(nc, f"s_wo{i}") for i in range(2)]
    s["ost"] = [Sem(nc, f"s_ost{i}") for i in range(2)]

    with (
        nc.sbuf_tensor("gix_sb", [128, 32], I32) as gix_sb,
        nc.sbuf_tensor("wo_sl", [128, 2 * 32 * 256], BF16) as wo_sl,
        nc.sbuf_tensor("o_st", [128, 2 * 2048], BF16) as o_st,
        nc.psum_tensor("ps_oA", [128, 2048], F32) as ps_oA,
        nc.psum_tensor("ps_oB", [128, 2048], F32) as ps_oB,
        nc.Block("p3") as block,
    ):
        wov = io["wo"].rearrange("(d p) n -> p d n", p=128)

        @block.sync
        def _(sy):
            sy.dma_start(out=gix_sb[:, :], in_=io["gidx"]).then_inc(s["gix"].h, 16)
            s["gix"].inc(16)
            for hc in range(16):
                sl = hc % 2
                sy.wait_ge(s["oev"].h, hc + 1)
                sy.dma_start(
                    out=io["o_dram"].ap().rearrange("(t p) n -> p t n", p=128)
                        [:, :, hc * 256:(hc + 1) * 256],
                    in_=o_st[:, sl * 2048:(sl + 1) * 2048]
                        .rearrange("p (t n) -> p t n", t=8),
                ).then_inc(s["ost"][sl].h, 16)
                s["ost"][sl].inc(16)

        @block.gpsimd
        def _(g):
            g.wait_ge(s["gix"].h, 16)
            g.wait_ge(s["evq"].h, 16)
            g.wait_ge(s["evkv"].h, 16)
            for chunk in range(4):
                g.wait_ge(s["ag2"].h, 2 * chunk + 2)
                src = io["at_full"][chunk].ap().rearrange("r (tb tk) -> (r tb) tk", tk=MY)
                for cc in range(8):
                    head = 4 * cc + chunk
                    i = chunk * 8 + cc
                    g.indirect_dma_start(
                        at_my[:, head * MY:(head + 1) * MY], None,
                        src, IndirectOffsetOnAxis(ap=gix_sb[:, i:i + 1], axis=0),
                    ).then_inc(s["gth"][chunk].h, 16)
                    s["gth"][chunk].inc(16)
            for hc in range(16):
                sl = hc % 2
                if hc >= 2:
                    g.wait_ge(s["omm"].h, hc - 1)
                g.dma_start(
                    out=wo_sl[:, sl * 8192:(sl + 1) * 8192]
                        .rearrange("p (d n) -> p d n", d=NDMT),
                    in_=wov[:, :, hc * 256:(hc + 1) * 256],
                ).then_inc(s["wo"][sl].h, 16)
                s["wo"][sl].inc(16)

        @block.tensor
        def _(t):
            d_list = [4 * cc + ch for ch in range(4) for cc in range(8)]
            for hc in range(16):
                sl = hc % 2
                ps_o = ps_oA if sl == 0 else ps_oB
                t.wait_ge(s["wo"][sl].h, 16 * (hc // 2 + 1))
                if hc >= 2:
                    t.wait_ge(s["oev"].h, hc - 1)
                for di, d in enumerate(d_list):
                    if di % 8 == 0:
                        t.wait_ge(s["gth"][di // 8].h, 16 * 8)
                    for tt in range(8):
                        ins = t.matmul(
                            ps_o[:, tt * 256:(tt + 1) * 256],
                            at_my[:, d * MY + tt * 128:d * MY + (tt + 1) * 128],
                            wo_sl[:, sl * 8192 + d * 256:sl * 8192 + (d + 1) * 256],
                            start=(di == 0 and tt % 2 == 0),
                            stop=(di == NDMT - 1 and tt % 2 == 1))
                ins.then_inc(s["omm"].h, 1)
                s["omm"].inc()

        @block.vector
        def _(v):
            for hc in range(16):
                sl = hc % 2
                ps_o = ps_oA if sl == 0 else ps_oB
                v.wait_ge(s["omm"].h, hc + 1)
                if hc >= 2:
                    v.wait_ge(s["ost"][sl].h, 16 * (hc // 2))
                ins = v.tensor_copy(o_st[:, sl * 2048:(sl + 1) * 2048], ps_o[:, :])
                ins.then_inc(s["oev"].h, 1)
                s["oev"].inc()


def _p4_postln1(nc, io, C, s, yT, ygT_sb):
    for k in ["lnv", "lna", "yt", "yte", "ygm", "yge", "xl", "ol", "yst"]:
        s[k] = Sem(nc, "s_" + k)
    NV, NA = 8, 2

    with (
        nc.sbuf_tensor("x_tile", [128, 2 * D], F32) as x_tile,
        nc.sbuf_tensor("o_tile", [128, 2 * D], BF16) as o_tile,
        nc.sbuf_tensor("z_sb", [128, D], F32) as z_sb,
        nc.sbuf_tensor("sq_sb", [128, D], F32) as sq_sb,
        nc.sbuf_tensor("y_tile", [128, D], BF16) as y_tile,
        nc.sbuf_tensor("ln_sc", [128, 8], F32) as ln_sc,
        nc.psum_tensor("ps_ytA", [128, 1024], BF16) as ps_ytA,
        nc.psum_tensor("ps_ytB", [128, 1024], BF16) as ps_ytB,
        nc.psum_tensor("ps_yg", [32, 1024], F32) as ps_yg,
        nc.Block("p4") as block,
    ):
        S1, S2, M, M2, VE, STD, RSTD, NEGMR = range(8)

        @block.sync
        def _(sy):
            for sl in range(2):
                sy.wait_ge(s["ost"][sl].h, 16 * 8)
            for tt in range(8):
                sl = tt % 2
                if tt > 1:
                    sy.wait_ge(s["lnv"].h, (tt - 2) * NV + 1)
                sy.dma_start(out=x_tile[:, sl * D:(sl + 1) * D],
                             in_=io["x_my"][tt * 128:(tt + 1) * 128, :]).then_inc(s["xl"].h, 16)
                s["xl"].inc(16)
                sy.dma_start(out=o_tile[:, sl * D:(sl + 1) * D],
                             in_=io["o_dram"].ap()[tt * 128:(tt + 1) * 128, :]).then_inc(s["ol"].h, 16)
                s["ol"].inc(16)
                sy.wait_ge(s["lnv"].h, tt * NV + 8)
                sy.dma_start(out=io["y_dram"].ap()[tt * 128:(tt + 1) * 128, :],
                             in_=y_tile[:, :]).then_inc(s["yst"].h, 16)
                s["yst"].inc(16)

        @block.vector
        def _(v):
            for tt in range(8):
                base = tt * NV
                abase = tt * NA
                v.wait_ge(s["xl"].h, 16 * (tt + 1))
                v.wait_ge(s["ol"].h, 16 * (tt + 1))

                v.tensor_tensor(z_sb[:, :], x_tile[:, (tt % 2) * D:(tt % 2 + 1) * D],
                                o_tile[:, (tt % 2) * D:(tt % 2 + 1) * D],
                                ALU.add).then_inc(s["lnv"].h, 1)
                v.wait_ge(s["lnv"].h, base + 1)
                v.reduce_sum(ln_sc[:, S1:S1 + 1], z_sb[:, :], axis=AX.X).then_inc(s["lnv"].h, 1)
                v.wait_ge(s["lnv"].h, base + 2)
                v.tensor_scalar(ln_sc[:, M:M + 1], ln_sc[:, S1:S1 + 1],
                                1.0 / D, None, ALU.mult).then_inc(s["lnv"].h, 1)
                v.wait_ge(s["lnv"].h, base + 3)
                v.tensor_scalar(ln_sc[:, M2:M2 + 1], ln_sc[:, M:M + 1],
                                ln_sc[:, M:M + 1], -1e-5,
                                ALU.mult, ALU.add).then_inc(s["lnv"].h, 1)
                v.wait_ge(s["lnv"].h, base + 4)
                v.wait_ge(s["lna"].h, abase + 1)
                v.scalar_tensor_tensor(ln_sc[:, VE:VE + 1], ln_sc[:, S2:S2 + 1],
                                       1.0 / D, ln_sc[:, M2:M2 + 1],
                                       ALU.mult, ALU.subtract).then_inc(s["lnv"].h, 1)
                v.wait_ge(s["lna"].h, abase + 2)
                v.reciprocal(ln_sc[:, RSTD:RSTD + 1],
                             ln_sc[:, STD:STD + 1]).then_inc(s["lnv"].h, 1)
                v.wait_ge(s["lnv"].h, base + 6)
                v.scalar_tensor_tensor(ln_sc[:, NEGMR:NEGMR + 1], ln_sc[:, M:M + 1],
                                       -1.0, ln_sc[:, RSTD:RSTD + 1],
                                       ALU.mult, ALU.mult).then_inc(s["lnv"].h, 1)
                v.wait_ge(s["lnv"].h, base + 7)
                if tt > 0:
                    v.wait_ge(s["yst"].h, 16 * tt)
                    v.wait_ge(s["yt"].h, tt * 32)
                v.tensor_scalar(y_tile[:, :], z_sb[:, :],
                                ln_sc[:, RSTD:RSTD + 1], ln_sc[:, NEGMR:NEGMR + 1],
                                ALU.mult, ALU.add).then_inc(s["lnv"].h, 1)
                for cj in range(NDMT):
                    n = tt * NDMT + cj
                    pp = n % 2
                    ps_yt = ps_ytA if pp == 0 else ps_ytB
                    v.wait_ge(s["yt"].h, n + 1)
                    v.tensor_copy(yT[:, cj * MY + tt * 128:cj * MY + (tt + 1) * 128],
                                  ps_yt[:, 0:128]).then_inc(s["yte"].h, 1)
                    s["yte"].inc()
            for half in range(2):
                v.wait_ge(s["ygm"].h, half + 1)
                v.tensor_copy(ygT_sb[:, half * 512:(half + 1) * 512],
                              ps_yg[:, half * 512:(half + 1) * 512]).then_inc(s["yge"].h, 1)
                s["yge"].inc()

        @block.scalar
        def _(sc):
            for tt in range(8):
                base = tt * NV
                sc.wait_ge(s["lnv"].h, base + 1)
                sc.activation(sq_sb[:, :], z_sb[:, :], AF.Square,
                              accum_out=ln_sc[:, S2:S2 + 1]).then_inc(s["lna"].h, 1)
                sc.wait_ge(s["lnv"].h, base + 5)
                sc.activation(ln_sc[:, STD:STD + 1], ln_sc[:, VE:VE + 1],
                              AF.Sqrt).then_inc(s["lna"].h, 1)
                s["lna"].inc(2)

        @block.tensor
        def _(t):
            n = 0
            for tt in range(8):
                t.wait_ge(s["lnv"].h, tt * NV + 8)
                for cj in range(NDMT):
                    c4, j = cj // 8, cj % 8
                    pp = n % 2
                    ps_yt = ps_ytA if pp == 0 else ps_ytB
                    if n >= 2:
                        t.wait_ge(s["yte"].h, n - 1)
                    src = y_tile[:, :].rearrange("p (a q e) -> p a q e", a=4, e=8)[:, c4, :, j]
                    ins = t.transpose(ps_yt[:, 0:128], src, C["ident_bf"])
                    ins.then_inc(s["yt"].h, 1)
                    s["yt"].inc()
                    n += 1
            t.wait_ge(s["yte"].h, 256)
            for half in range(2):
                for cj in range(NDMT):
                    c4 = cj // 8
                    ins = t.matmul(
                        ps_yg[:, half * 512:(half + 1) * 512],
                        C["e_cmats"][:, c4 * 32:(c4 + 1) * 32],
                        yT[:, cj * MY + half * 512:cj * MY + (half + 1) * 512],
                        start=(cj == 0), stop=(cj == NDMT - 1))
                ins.then_inc(s["ygm"].h, 1)
                s["ygm"].inc()



def _p5pre_zeros(nc, io, C, s, qz_sb, z4_sb, tmp_sb, szs_sb, zs_out):
    for k in ["zplq", "zpls", "zc", "zst"]:
        s[k] = Sem(nc, "s_" + k)

    CH = INTER // 8
    with nc.Block("p5pre") as block:
        chunks = []
        for mat, (qz, sc) in enumerate([(io["g_qz"], io["g_sc"]), (io["u_qz"], io["u_sc"])]):
            for ci in range(8):
                chunks.append((qz, sc, io["zs_gu_dram"], mat * INTER + ci * CH,
                               32, CH, ci * (CH // 8), ci * CH))
        for ci in range(4):
            chunks.append((io["d_qz"], io["d_sc"], io["zs_d_dram"], ci * 1024,
                           86, 1024, ci * 128, ci * 1024))

        @block.scalar
        def _(sc_):
            sc_.wait_ge(s["avst"][0].h, 16 * 32)
            sc_.wait_ge(s["avst"][1].h, 16 * 32)

            def store(i):
                _, _, dst, col, rows, w, _, _ = chunks[i]
                sc_.wait_ge(s["zc"].h, 10 * (i + 1))
                sc_.dma_start(out=dst.ap()[0:rows, col:col + w],
                              in_=zs_out[0:rows, 0:w]).then_inc(s["zst"].h, 16)
                s["zst"].inc(16)

            for i, (qz, sc, dst, col, rows, w, qzoff, scoff) in enumerate(chunks):
                if i >= 1:
                    sc_.wait_ge(s["zc"].h, 10 * i)
                sc_.dma_start(out=qz_sb[0:rows, 0:w // 8],
                              in_=qz[:, qzoff:qzoff + w // 8]).then_inc(s["zplq"].h, 16)
                s["zplq"].inc(16)
                sc_.dma_start(out=szs_sb[0:rows, 0:w],
                              in_=sc[:, scoff:scoff + w]).then_inc(s["zpls"].h, 16)
                s["zpls"].inc(16)
                if i >= 1:
                    store(i - 1)
            store(len(chunks) - 1)

        @block.vector
        def _(v):
            for i, (qz, sc, dst, col, rows, w, qzoff, scoff) in enumerate(chunks):
                v.wait_ge(s["zplq"].h, 16 * (i + 1))
                v.wait_ge(s["zpls"].h, 16 * (i + 1))
                v.wait_ge(s["zc"].h, 10 * i)
                if i >= 1:
                    v.wait_ge(s["zst"].h, 16 * i)
                z4v = z4_sb[0:rows, 0:w].rearrange("g (m e) -> g m e", e=8)
                for j in range(8):
                    ins = v.tensor_scalar(z4v[:, :, j], qz_sb[0:rows, 0:w // 8],
                                          4 * j, 15,
                                          ALU.logical_shift_right, ALU.bitwise_and)
                    ins.then_inc(s["zc"].h, 1)
                    s["zc"].inc()
                v.wait_ge(s["zc"].h, s["zc"].n)
                v.tensor_scalar(tmp_sb[0:rows, 0:w], z4_sb[0:rows, 0:w],
                                -7.0, -1.0, ALU.add, ALU.mult).then_inc(s["zc"].h, 1)
                s["zc"].inc()
                v.wait_ge(s["zc"].h, s["zc"].n)
                v.tensor_tensor(zs_out[0:rows, 0:w], tmp_sb[0:rows, 0:w],
                                szs_sb[0:rows, 0:w], ALU.mult).then_inc(s["zc"].h, 1)
                s["zc"].inc()


def _p5_gateup(nc, io, C, s, yT, ygT_sb, gs_sb):
    for k in ["dqa", "dqb", "gstp", "ustp", "silu", "mul", "gstm", "gtr", "gse", "zsl", "gev"]:
        s[k] = Sem(nc, "s_" + k)
    s["qw"] = [Sem(nc, f"s_qw{i}") for i in range(2)]
    s["stl"] = [Sem(nc, f"s_stl{i}") for i in range(2)]
    s["hst"] = [Sem(nc, f"s_hst{i}") for i in range(2)]
    s["zsl2"] = [Sem(nc, f"s_zsl{i}") for i in range(2)]

    NPN = INTER // 256
    NPN_R = P5_LIMIT if P5_LIMIT else NPN
    NIT_R = 2 * NPN_R
    trunc = NPN_R != NPN

    with (
        nc.sbuf_tensor("qw_sl", [128, 2 * 8 * 256], I32) as qw_sl,
        nc.sbuf_tensor("st_sl", [128, 2 * 8 * 256], F32) as st_sl,
        nc.sbuf_tensor("st_bf", [128, 2 * 8 * 256], BF16) as st_bf,
        nc.sbuf_tensor("panelG", [128, 2 * 8192], BF16) as panelG,
        nc.sbuf_tensor("panelU", [128, 2 * 8192], BF16) as panelU,
        nc.sbuf_tensor("zs_sl", [32, 2 * 512], F32R) as zs_sl,
        nc.sbuf_tensor("silu_sb", [128, 2 * 1024], BF16) as silu_sb,
        nc.sbuf_tensor("hT_st", [128, 2 * 1024], BF16) as hT_st,
        nc.sbuf_tensor("gst_sb", [128, 8 * 86], F32) as gst_sb,
        nc.sbuf_tensor("dqt", [128, 8 * 256], I32) as dqt,
        nc.psum_tensor("ps_g", [128, 1024], F32) as ps_g,
        nc.psum_tensor("ps_u", [128, 1024], F32) as ps_u,
        nc.psum_tensor("ps_gst", [128, 8], F32) as ps_gst,
        nc.psum_tensor("ps_gtr", [86, 1024], F32) as ps_gtr,
        nc.Block("p5") as block,
    ):
        qws = [io["g_qw"], io["u_qw"]]
        scs = [io["g_sc"], io["u_sc"]]
        panels = [panelG, panelU]

        @block.sync
        def _(sy):
            sy.wait_ge(s["yst"].h, 16 * 8)
            sy.wait_ge(s["zc"].h, 200)
            sy.wait_ge(s["zst"].h, 16 * 20)
            for pn in range(NPN_R):
                sl = pn % 2
                if pn >= 2:
                    sy.wait_ge(s["dqb"].h, 8 * (pn - 1))
                for mat in range(2):
                    for c4 in range(4):
                        sy.dma_start(
                            out=qw_sl[:, (sl * 8 + mat * 4 + c4) * 256:
                                      (sl * 8 + mat * 4 + c4 + 1) * 256],
                            in_=qws[mat][128 * c4:128 * (c4 + 1),
                                         pn * 256:(pn + 1) * 256],
                        ).then_inc(s["qw"][sl].h, 16)
                        s["qw"][sl].inc(16)
                        src = scs[mat][8 * c4:8 * (c4 + 1), pn * 256:(pn + 1) * 256]
                        src = src.unsqueeze(1).broadcast_to([8, 16, 256])
                        sy.dma_start(
                            out=st_sl[:, (sl * 8 + mat * 4 + c4) * 256:
                                      (sl * 8 + mat * 4 + c4 + 1) * 256],
                            in_=src,
                        ).then_inc(s["stl"][sl].h, 16)
                        s["stl"][sl].inc(16)
                if pn >= 2:
                    sy.wait_ge(s["ustp"].h, 2 * pn - 2)
                sy.dma_start(
                    out=zs_sl[:, sl * 512:sl * 512 + 256],
                    in_=io["zs_gu_dram"].ap()[0:32, pn * 256:(pn + 1) * 256],
                ).then_inc(s["zsl2"][sl].h, 16)
                sy.dma_start(
                    out=zs_sl[:, sl * 512 + 256:(sl + 1) * 512],
                    in_=io["zs_gu_dram"].ap()[0:32, INTER + pn * 256:INTER + (pn + 1) * 256],
                ).then_inc(s["zsl2"][sl].h, 16)
                s["zsl2"][sl].inc(32)

        @block.gpsimd
        def _(g):
            g.wait_ge(s["yst"].h, 16 * 8)
            g.wait_ge(s["zc"].h, 200)

            def hstore(it):
                sl = it % 2
                g.wait_ge(s["mul"].h, it + 1)
                g.dma_start(out=io["hT_dram"].ap()[it * 128:(it + 1) * 128, :],
                            in_=hT_st[:, sl * 1024:(sl + 1) * 1024]).then_inc(s["hst"][sl].h, 16)
                s["hst"][sl].inc(16)

            for it in range(NIT_R):
                hstore(it)

        @block.vector
        def _(v):
            v.wait_ge(s["yst"].h, 16 * 8)
            v.wait_ge(s["zst"].h, 16 * 20)
            def emit_dequant(pn):
                sl = pn % 2
                v.wait_ge(s["qw"][sl].h, 128 * (pn // 2 + 1))
                v.wait_ge(s["stl"][sl].h, 128 * (pn // 2 + 1))
                if pn >= 2:
                    v.wait_ge(s["gstp"].h, 2 * pn - 2)
                v.tensor_copy(st_bf[:, sl * 2048:(sl + 1) * 2048],
                              st_sl[:, sl * 2048:(sl + 1) * 2048]).then_inc(s["dqa"].h, 1)
                s["dqa"].inc()
                for r, (mat, c4) in enumerate([(m, c) for m in range(2) for c in range(4)]):
                    gr = pn * 8 + r
                    base = (sl * 8 + mat * 4 + c4) * 256
                    if gr >= 1:
                        v.wait_ge(s["dqb"].h, gr)
                    if pn >= 2 and mat == 1 and c4 == 0:
                        v.wait_ge(s["ustp"].h, 2 * pn - 2)
                    for j in range(8):
                        ins = v.tensor_scalar(dqt[:, j * 256:(j + 1) * 256],
                                              qw_sl[:, base:base + 256], 4 * j, 15,
                                              ALU.logical_shift_right, ALU.bitwise_and)
                    ins.then_inc(s["dqa"].h, 1)
                    s["dqa"].inc()
                    v.wait_ge(s["dqa"].h, s["dqa"].n)
                    pan = panels[mat]
                    for j in range(8):
                        cj = c4 * 8 + j
                        ins = v.scalar_tensor_tensor(
                            pan[:, sl * 8192 + cj * 256:sl * 8192 + (cj + 1) * 256],
                            dqt[:, j * 256:(j + 1) * 256], -8.0,
                            st_bf[:, base:base + 256], ALU.add, ALU.mult)
                    ins.then_inc(s["dqb"].h, 1)
                    s["dqb"].inc()

            def emit_gev(it):
                if NO_AUX:
                    return
                v.wait_ge(s["gstm"].h, it + 1)
                v.tensor_copy(gst_sb[:, it * 8:(it + 1) * 8], ps_gst[:, :]).then_inc(s["gev"].h, 1)
                s["gev"].inc()

            def emit_mul(it):
                sl = it % 2
                v.wait_ge(s["ustp"].h, it + 1)
                v.wait_ge(s["silu"].h, it + 1)
                if it >= 2:
                    v.wait_ge(s["hst"][sl].h, 16 * (it // 2))
                    if not NO_AUX:
                        v.wait_ge(s["gstm"].h, it - 1)
                v.tensor_tensor(hT_st[:, sl * 1024:sl * 1024 + 512],
                                ps_u[:, 0:512], silu_sb[:, sl * 1024:sl * 1024 + 512],
                                ALU.mult)
                v.tensor_tensor(hT_st[:, sl * 1024 + 512:(sl + 1) * 1024],
                                ps_u[:, 512:1024], silu_sb[:, sl * 1024 + 512:(sl + 1) * 1024],
                                ALU.mult).then_inc(s["mul"].h, 1)
                s["mul"].inc()

            for pn in range(NPN_R):
                emit_dequant(pn)
                if pn >= 2:
                    emit_gev(2 * (pn - 2))
                    emit_gev(2 * (pn - 2) + 1)
                if pn >= 1:
                    emit_mul(2 * (pn - 1))
                    emit_mul(2 * (pn - 1) + 1)
            if NPN_R >= 2:
                emit_gev(2 * NPN_R - 4)
                emit_gev(2 * NPN_R - 3)
            emit_mul(2 * (NPN_R - 1))
            emit_mul(2 * (NPN_R - 1) + 1)
            emit_gev(2 * NPN_R - 2)
            emit_gev(2 * NPN_R - 1)
            if not NO_AUX:
                v.wait_ge(s["gev"].h, NIT_R)
                v.engine_nop().then_inc(s["gse"].h, 1)
                s["gse"].inc()
                if not NO_TR:
                    v.wait_ge(s["gtr"].h, 8)
                    v.tensor_copy(gs_sb[0:86, :],
                                  ps_gtr[0:86, :]).then_inc(s["gse"].h, 1)
                    s["gse"].inc()

        @block.scalar
        def _(sc):
            for it in range(NIT_R):
                sl = it % 2
                sc.wait_ge(s["gstp"].h, it + 1)
                if it >= 2:
                    sc.wait_ge(s["mul"].h, it - 1)
                sc.activation(silu_sb[:, sl * 1024:(sl + 1) * 1024],
                              ps_g[:, :], AF.Silu).then_inc(s["silu"].h, 1)
                s["silu"].inc()

        @block.tensor
        def _(t):
            t.wait_ge(s["yge"].h, 2)
            t.wait_ge(s["zst"].h, 16 * 20)
            for pn in range(NPN_R):
                sl = pn % 2
                t.wait_ge(s["zsl2"][pn % 2].h, 32 * (pn // 2 + 1))
                for itp in range(2):
                    it = 2 * pn + itp
                    if it > 0:
                        t.wait_ge(s["silu"].h, it)
                    for cj in range(NDMT):
                        if itp == 0 and cj % 8 == 0:
                            t.wait_ge(s["dqb"].h, 8 * pn + cj // 8 + 1)
                        for th in range(2):
                            ins = t.matmul(ps_g[:, th * 512:(th + 1) * 512],
                                           panelG[:, sl * 8192 + cj * 256 + itp * 128:
                                                  sl * 8192 + cj * 256 + (itp + 1) * 128],
                                           yT[:, cj * MY + th * 512:cj * MY + (th + 1) * 512],
                                           start=(cj == 0), stop=(NO_Z and cj == NDMT - 1))
                    if not NO_Z:
                        for th in range(2):
                            ins = t.matmul(ps_g[:, th * 512:(th + 1) * 512],
                                           zs_sl[0:32, sl * 512 + itp * 128:sl * 512 + (itp + 1) * 128],
                                           ygT_sb[0:32, th * 512:(th + 1) * 512],
                                           start=False, stop=True)
                    ins.then_inc(s["gstp"].h, 1)
                    s["gstp"].inc()
                    if it > 0:
                        t.wait_ge(s["mul"].h, it)
                    for cj in range(NDMT):
                        if itp == 0 and cj % 8 == 0:
                            t.wait_ge(s["dqb"].h, 8 * pn + 4 + cj // 8 + 1)
                        for th in range(2):
                            ins = t.matmul(ps_u[:, th * 512:(th + 1) * 512],
                                           panelU[:, sl * 8192 + cj * 256 + itp * 128:
                                                  sl * 8192 + cj * 256 + (itp + 1) * 128],
                                           yT[:, cj * MY + th * 512:cj * MY + (th + 1) * 512],
                                           start=(cj == 0), stop=(NO_Z and cj == NDMT - 1))
                    if not NO_Z:
                        for th in range(2):
                            ins = t.matmul(ps_u[:, th * 512:(th + 1) * 512],
                                           zs_sl[0:32, sl * 512 + 256 + itp * 128:
                                                 sl * 512 + 256 + (itp + 1) * 128],
                                           ygT_sb[0:32, th * 512:(th + 1) * 512],
                                           start=False, stop=True)
                    ins.then_inc(s["ustp"].h, 1)
                    s["ustp"].inc()
                    if it > 0 and not NO_AUX:
                        t.wait_ge(s["mul"].h, it)
                        if it > 1:
                            t.wait_ge(s["gev"].h, it - 1)
                        psl = (it - 1) % 2
                        if NO_GST:
                            ins = t.matmul(ps_gst[:, 0:8],
                                           hT_st[:, psl * 1024:psl * 1024 + 128],
                                           hT_st[:, psl * 1024:psl * 1024 + 8],
                                           start=True, stop=True)
                        else:
                            for tt in range(8):
                                ins = t.matmul(ps_gst[:, tt:tt + 1],
                                               hT_st[:, psl * 1024 + tt * 128:psl * 1024 + (tt + 1) * 128],
                                               C["ones_col"], start=True, stop=True)
                        ins.then_inc(s["gstm"].h, 1)
                        s["gstm"].inc()
            if NO_AUX:
                return
            t.wait_ge(s["mul"].h, NIT_R)
            t.wait_ge(s["gev"].h, NIT_R - 1)
            psl = (NIT_R - 1) % 2
            if NO_GST:
                ins = t.matmul(ps_gst[:, 0:8],
                               hT_st[:, psl * 1024:psl * 1024 + 128],
                               hT_st[:, psl * 1024:psl * 1024 + 8],
                               start=True, stop=True)
            else:
                for tt in range(8):
                    ins = t.matmul(ps_gst[:, tt:tt + 1],
                                   hT_st[:, psl * 1024 + tt * 128:psl * 1024 + (tt + 1) * 128],
                                   C["ones_col"], start=True, stop=True)
            ins.then_inc(s["gstm"].h, 1)
            s["gstm"].inc()
            t.wait_ge(s["gse"].h, 1)
            if not NO_TR:
                gstv = gst_sb[:, :].rearrange("p (i t) -> p i t", t=8)
                for tt in range(8):
                    t.transpose(ps_gtr[0:86, tt * 128:(tt + 1) * 128],
                                gstv[:, :, tt],
                                C["ident_f32"]).then_inc(s["gtr"].h, 1)
                    s["gtr"].inc()


def _p6_down(nc, io, C, s, gs_sb, o2_sb):
    for k in ["ddq", "dmm", "dtm", "dev"]:
        s[k] = Sem(nc, "s_" + k)
    s["dqw"] = [Sem(nc, f"s_dqw{i}") for i in range(2)]
    s["dsd"] = [Sem(nc, f"s_dsd{i}") for i in range(2)]
    s["dht"] = [Sem(nc, f"s_dht{i}") for i in range(4)]
    s["dzs"] = [Sem(nc, f"s_dzs{i}") for i in range(2)]

    NCI = 11
    rows_of = lambda ci: 96 if ci == 10 else 128
    htv = io["hT_dram"].ap().rearrange("(q e) t -> e q t", e=8)

    with (
        nc.sbuf_tensor("dqw_sl", [128, 2 * 512], I32) as dqw_sl,
        nc.sbuf_tensor("dsd_sl", [128, 2 * 512], F32) as dsd_sl,
        nc.sbuf_tensor("dsd_bf", [128, 2 * 512], BF16) as dsd_bf,
        nc.sbuf_tensor("dht_sl", [128, 4 * 1024], BF16) as dht_sl,
        nc.sbuf_tensor("dzs_sl", [86, 2 * 512], F32R) as dzs_sl,
        nc.sbuf_tensor("wd_sb", [128, 4 * 512], BF16) as wd_sb,
        nc.sbuf_tensor("dqt_d", [128, 512], I32) as dqt_d,
        nc.psum_tensor("ps_o2", [128, 8 * 512], F32) as ps_o2,
        nc.Block("p6") as block,
    ):
        def nidx(hc, ci, j):
            return hc * 88 + ci * 8 + j

        @block.sync
        def _(sy):
            sy.wait_ge(s["ustp"].h, N_IT)
            for sl in range(2):
                sy.wait_ge(s["hst"][sl].h, 16 * 43)
            for hc in range(8):
                sl2 = hc % 2
                if hc >= 2:
                    sy.wait_ge(s["dmm"].h, hc - 1)
                sy.dma_start(out=dzs_sl[0:86, sl2 * 512:(sl2 + 1) * 512],
                             in_=io["zs_d_dram"].ap()[0:86, hc * 512:(hc + 1) * 512]
                             ).then_inc(s["dzs"][sl2].h, 16)
                s["dzs"][sl2].inc(16)
                for ci in range(NCI):
                    rows = rows_of(ci)
                    nq = hc * NCI + ci
                    slq = nq % 2
                    if nq >= 2:
                        sy.wait_ge(s["ddq"].h, 17 * (nq - 1))
                    sy.dma_start(out=dqw_sl[0:rows, slq * 512:(slq + 1) * 512],
                                 in_=io["d_qw"][128 * ci:128 * ci + rows,
                                                hc * 512:(hc + 1) * 512]
                                 ).then_inc(s["dqw"][slq].h, 16)
                    s["dqw"][slq].inc(16)
                    grows = 6 if ci == 10 else 8
                    srcs_ = io["d_sc"][8 * ci:8 * ci + grows, hc * 512:(hc + 1) * 512]
                    srcs_ = srcs_.unsqueeze(1).broadcast_to([grows, 16, 512])
                    sy.dma_start(out=dsd_sl[0:rows, slq * 512:(slq + 1) * 512],
                                 in_=srcs_).then_inc(s["dsd"][slq].h, 16)
                    s["dsd"][slq].inc(16)
                    for j in range(8):
                        n = nidx(hc, ci, j)
                        sl = n % 4
                        if n >= 4:
                            sy.wait_ge(s["dtm"].h, n - 3)
                        sy.dma_start(out=dht_sl[0:rows, sl * 1024:(sl + 1) * 1024],
                                     in_=htv[j, 128 * ci:128 * ci + rows, :]
                                     ).then_inc(s["dht"][sl].h, 16)
                        s["dht"][sl].inc(16)

        @block.gpsimd
        def _(g):
            pass

        @block.vector
        def _(v):
            v.wait_ge(s["ustp"].h, N_IT)
            for sl in range(2):
                v.wait_ge(s["hst"][sl].h, 16 * 43)
            for hc in range(8):
                for ci in range(NCI):
                    rows = rows_of(ci)
                    nq = hc * NCI + ci
                    slq = nq % 2
                    v.wait_ge(s["dqw"][slq].h, 16 * (nq // 2 + 1))
                    v.wait_ge(s["dsd"][slq].h, 16 * (nq // 2 + 1))
                    v.tensor_copy(dsd_bf[0:rows, slq * 512:(slq + 1) * 512],
                                  dsd_sl[0:rows, slq * 512:(slq + 1) * 512]).then_inc(s["ddq"].h, 1)
                    s["ddq"].inc()
                    for j in range(8):
                        n = nidx(hc, ci, j)
                        sl = n % 4
                        if n >= 4:
                            v.wait_ge(s["dtm"].h, n - 3)
                        v.wait_ge(s["ddq"].h, s["ddq"].n)
                        v.tensor_scalar(dqt_d[0:rows, :], dqw_sl[0:rows, slq * 512:
                                        (slq + 1) * 512], 4 * j, 15,
                                        ALU.logical_shift_right,
                                        ALU.bitwise_and).then_inc(s["ddq"].h, 1)
                        s["ddq"].inc()
                        v.wait_ge(s["ddq"].h, s["ddq"].n)
                        v.scalar_tensor_tensor(wd_sb[0:rows, sl * 512:(sl + 1) * 512],
                                               dqt_d[0:rows, :], -8.0,
                                               dsd_bf[0:rows, slq * 512:(slq + 1) * 512],
                                               ALU.add, ALU.mult).then_inc(s["ddq"].h, 1)
                        s["ddq"].inc()
                v.wait_ge(s["dmm"].h, hc + 1)
                for tt in range(8):
                    ins = v.tensor_copy(o2_sb[:, tt * D + hc * 512:tt * D + (hc + 1) * 512],
                                        ps_o2[:, tt * 512:(tt + 1) * 512])
                ins.then_inc(s["dev"].h, 1)
                s["dev"].inc()

        @block.tensor
        def _(t):
            t.wait_ge(s["gse"].h, 2)
            for hc in range(8):
                sl2 = hc % 2
                t.wait_ge(s["dzs"][sl2].h, 16 * (hc // 2 + 1))
                if hc >= 1:
                    t.wait_ge(s["dev"].h, hc)
                for ci in range(NCI):
                    rows = rows_of(ci)
                    for j in range(8):
                        n = nidx(hc, ci, j)
                        sl = n % 4
                        t.wait_ge(s["ddq"].h, 17 * (hc * NCI + ci) + 2 * j + 3)
                        t.wait_ge(s["dht"][sl].h, 16 * (n // 4 + 1))
                        for tt in range(8):
                            ins = t.matmul(ps_o2[:, tt * 512:(tt + 1) * 512],
                                           dht_sl[0:rows, sl * 1024 + tt * 128:sl * 1024 + (tt + 1) * 128],
                                           wd_sb[0:rows, sl * 512:(sl + 1) * 512],
                                           start=(ci == 0 and j == 0), stop=False)
                        ins.then_inc(s["dtm"].h, 1)
                        s["dtm"].inc()
                for tt in range(8):
                    ins = t.matmul(ps_o2[:, tt * 512:(tt + 1) * 512],
                                   gs_sb[0:86, tt * 128:(tt + 1) * 128],
                                   dzs_sl[0:86, sl2 * 512:(sl2 + 1) * 512],
                                   start=False, stop=True)
                ins.then_inc(s["dmm"].h, 1)
                s["dmm"].inc()


def _p7_final(nc, io, C, s, o2_sb):
    for k in ["f_v", "f_a", "f_yl", "f_st"]:
        s[k] = Sem(nc, "s_" + k)
    NV = 8

    with (
        nc.sbuf_tensor("y2_tile", [128, 2 * D], BF16) as y2_tile,
        nc.sbuf_tensor("z2_sb", [128, D], F32) as z2_sb,
        nc.sbuf_tensor("sq2_sb", [128, D], F32) as sq2_sb,
        nc.sbuf_tensor("out_t", [128, D], F32) as out_t,
        nc.sbuf_tensor("ln2_sc", [128, 8], F32) as ln2_sc,
        nc.Block("p7") as block,
    ):
        S1, S2, M, M2, VE, STD, RSTD, NEGMR = range(8)

        @block.sync
        def _(sy):
            sy.wait_ge(s["dmm"].h, 8)
            for tt in range(8):
                sl = tt % 2
                if tt > 1:
                    sy.wait_ge(s["f_v"].h, (tt - 2) * NV + 1)
                sy.dma_start(out=y2_tile[:, sl * D:(sl + 1) * D],
                             in_=io["y_dram"].ap()[tt * 128:(tt + 1) * 128, :]
                             ).then_inc(s["f_yl"].h, 16)
                s["f_yl"].inc(16)
                sy.wait_ge(s["f_v"].h, tt * NV + 8)
                sy.dma_start(out=io["out_my"][tt * 128:(tt + 1) * 128, :],
                             in_=out_t[:, :]).then_inc(s["f_st"].h, 16)
                s["f_st"].inc(16)
            sy.wait_ge(s["f_st"].h, 16 * 8)

        @block.vector
        def _(v):
            v.wait_ge(s["dev"].h, 8)
            for tt in range(8):
                base = tt * NV
                abase = tt * 2
                v.wait_ge(s["f_yl"].h, 16 * (tt + 1))
                v.tensor_tensor(z2_sb[:, :], y2_tile[:, (tt % 2) * D:(tt % 2 + 1) * D],
                                o2_sb[:, tt * D:(tt + 1) * D], ALU.add).then_inc(s["f_v"].h, 1)
                v.wait_ge(s["f_v"].h, base + 1)
                v.reduce_sum(ln2_sc[:, S1:S1 + 1], z2_sb[:, :], axis=AX.X).then_inc(s["f_v"].h, 1)
                v.wait_ge(s["f_v"].h, base + 2)
                v.tensor_scalar(ln2_sc[:, M:M + 1], ln2_sc[:, S1:S1 + 1],
                                1.0 / D, None, ALU.mult).then_inc(s["f_v"].h, 1)
                v.wait_ge(s["f_v"].h, base + 3)
                v.tensor_scalar(ln2_sc[:, M2:M2 + 1], ln2_sc[:, M:M + 1],
                                ln2_sc[:, M:M + 1], -1e-5,
                                ALU.mult, ALU.add).then_inc(s["f_v"].h, 1)
                v.wait_ge(s["f_v"].h, base + 4)
                v.wait_ge(s["f_a"].h, abase + 1)
                v.scalar_tensor_tensor(ln2_sc[:, VE:VE + 1], ln2_sc[:, S2:S2 + 1],
                                       1.0 / D, ln2_sc[:, M2:M2 + 1],
                                       ALU.mult, ALU.subtract).then_inc(s["f_v"].h, 1)
                v.wait_ge(s["f_a"].h, abase + 2)
                v.reciprocal(ln2_sc[:, RSTD:RSTD + 1],
                             ln2_sc[:, STD:STD + 1]).then_inc(s["f_v"].h, 1)
                v.wait_ge(s["f_v"].h, base + 6)
                v.scalar_tensor_tensor(ln2_sc[:, NEGMR:NEGMR + 1], ln2_sc[:, M:M + 1],
                                       -1.0, ln2_sc[:, RSTD:RSTD + 1],
                                       ALU.mult, ALU.mult).then_inc(s["f_v"].h, 1)
                v.wait_ge(s["f_v"].h, base + 7)
                if tt > 0:
                    v.wait_ge(s["f_st"].h, 16 * tt)
                v.tensor_scalar(out_t[:, :], z2_sb[:, :],
                                ln2_sc[:, RSTD:RSTD + 1], ln2_sc[:, NEGMR:NEGMR + 1],
                                ALU.mult, ALU.add).then_inc(s["f_v"].h, 1)

        @block.scalar
        def _(sc):
            for tt in range(8):
                base = tt * NV
                sc.wait_ge(s["f_v"].h, base + 1)
                sc.activation(sq2_sb[:, :], z2_sb[:, :], AF.Square,
                              accum_out=ln2_sc[:, S2:S2 + 1]).then_inc(s["f_a"].h, 1)
                sc.wait_ge(s["f_v"].h, base + 5)
                sc.activation(ln2_sc[:, STD:STD + 1], ln2_sc[:, VE:VE + 1],
                              AF.Sqrt).then_inc(s["f_a"].h, 1)
                s["f_a"].inc(2)


def _shard_inputs(inputs):
    x = np.ascontiguousarray(np.asarray(inputs["L0_attn_x"]).reshape(T, D))
    wq = np.asarray(inputs["L0_attn_wq"])
    wk = np.asarray(inputs["L0_attn_wk"])
    wv = np.asarray(inputs["L0_attn_wv"])
    maps = []
    for c in range(NC):
        m = {
            "x_my": np.ascontiguousarray(x[c * MY:(c + 1) * MY]),
            "wq_my": np.ascontiguousarray(wq[:, c * HPC * HD:(c + 1) * HPC * HD]),
            "wk_my": np.ascontiguousarray(wk[:, c * HD:(c + 1) * HD]),
            "wv_my": np.ascontiguousarray(wv[:, c * HD:(c + 1) * HD]),
            "wo": np.asarray(inputs["L0_attn_wo"]),
            "g_qw": np.asarray(inputs["L0_ffn_gate_qweight"]),
            "g_sc": np.asarray(inputs["L0_ffn_gate_scales"]),
            "g_qz": np.asarray(inputs["L0_ffn_gate_qzeros"]),
            "u_qw": np.asarray(inputs["L0_ffn_up_qweight"]),
            "u_sc": np.asarray(inputs["L0_ffn_up_scales"]),
            "u_qz": np.asarray(inputs["L0_ffn_up_qzeros"]),
            "d_qw": np.asarray(inputs["L0_ffn_down_qweight"]),
            "d_sc": np.asarray(inputs["L0_ffn_down_scales"]),
            "d_qz": np.asarray(inputs["L0_ffn_down_qzeros"]),
        }
        gidx = np.zeros((128, 32), dtype=np.int32)
        p = np.arange(128)
        for chunk in range(4):
            for cc in range(8):
                gidx[:, chunk * 8 + cc] = ((c // 4) * NC * HD + cc * HD + p) \
                    * (S // MY) + (c % 4)
        m["gidx"] = gidx
        maps.append(m)
    return maps


def kernel(**inputs):
    nc, _ = build_program(dbg=False)
    maps = _shard_inputs(inputs)
    res = run_bass_kernel_spmd(nc, maps, core_ids=list(range(NC)))
    out = np.concatenate([r["out_my"] for r in res.results], axis=0)
    return out.reshape(B, S, D)
